# version 03
# baseline (speedup 1.0000x reference)
import numpy as np
import ml_dtypes
from contextlib import ExitStack

import concourse.bass as bass
import concourse.mybir as mybir
from concourse.bass_utils import run_bass_kernel_spmd

F32 = mybir.dt.float32
BF16 = mybir.dt.bfloat16
U32 = mybir.dt.uint32
AF = mybir.ActivationFunctionType
ALU = mybir.AluOpType
AXX = mybir.AxisListType.X

NCORES = 8
T = 2048
D = 1024
FC = 8
TC = 512
NT = T // TC
QK = 2048
NQ = QK // 128
HID = 2048
MH = 4096
E = 8
NTOK = NCORES * T
CAP0 = 2458
J0 = 20
NS0 = J0 * 128
CAP1 = 3072
J1 = 24
NS1 = J1 * 128
LN_EPS = 1e-5
BISECT_ITERS = 46

ENG_SEM = {"tensor": "sT", "vector": "sV", "scalar": "sS",
           "gpsimd": "sG", "sync": "sHq"}


def _bfh(x):
    return np.ascontiguousarray(np.asarray(x, dtype=np.float32).astype(ml_dtypes.bfloat16))


def _f32(x):
    return np.ascontiguousarray(np.asarray(x, dtype=np.float32))


def _featmaj(vec):
    v = np.asarray(vec, dtype=np.float32)
    n = v.shape[0]
    return np.ascontiguousarray(v.reshape(n // 128, 128).T)


class Prog:
    def __init__(self, dev_stage=None):
        self.dev = dev_stage
        nc = self.nc = bass.Bass("TRN2", target_bir_lowering=False, debug=False)
        self._ctx = []
        self.dbg_aps = {}
        self._declare_io()
        self.sems = {}
        self.cnt = {}
        for name in (["sT", "sV", "sS", "sG", "sHq", "sC"]
                     + [f"sW{i}" for i in range(8)] + [f"sH{i}" for i in range(8)]):
            t = nc.semaphore(name)
            self.sems[name] = t.__enter__()
            self._ctx.append(t)
            self.cnt[name] = 0
        self.dma_rr = {"gpsimd": 0, "sync": 0}
        self.last_w = {}
        self.readers = {}
        self.mem_token = {}
        self.queues = {k: [] for k in ENG_SEM}
        self.hwm = {k: {} for k in ENG_SEM}
        self.serialize = False
        self.ser_lo = 0
        self.ser_hi = 10**9
        self.n_em = 0
        self.fence_scalar = False
        self.fence_vector = False
        self.fence_tensor = False

    def _declare_io(self):
        nc = self.nc
        dt = nc.dram_tensor
        self.i_x = dt("x", [D, T], F32, kind="ExternalInput").ap()
        self.i_l0_ln = dt("l0_ln", [128, 2 * FC], F32, kind="ExternalInput").ap()
        self.i_l0_Wh = dt("l0_Wh", [D, 2 * HID], F32, kind="ExternalInput").ap()
        self.i_l0_bh = dt("l0_bh", [128, 2 * HID // 128], F32, kind="ExternalInput").ap()
        self.i_l0_Wqk = dt("l0_Wqk", [D, QK], F32, kind="ExternalInput").ap()
        self.i_l0_os = dt("l0_os", [128, 4 * NQ], F32, kind="ExternalInput").ap()
        self.i_l0_osr = dt("l0_osr", [2, QK], F32, kind="ExternalInput").ap()
        self.i_l0_bqkr = dt("l0_bqkr", [1, QK], BF16, kind="ExternalInput").ap()
        self.i_l0_Wo = dt("l0_Wo", [HID, D], F32, kind="ExternalInput").ap()
        self.i_l0_bo = dt("l0_bo", [128, D // 128], F32, kind="ExternalInput").ap()
        self.i_l0_Wr = dt("l0_Wr", [D, E], F32, kind="ExternalInput").ap()
        self.i_l0_br = dt("l0_br", [E, 1], F32, kind="ExternalInput").ap()
        self.i_l0_W1 = dt("l0_W1", [D, MH], F32, kind="ExternalInput").ap()
        self.i_l0_b1 = dt("l0_b1", [128, MH // 128], F32, kind="ExternalInput").ap()
        self.i_l0_W2 = dt("l0_W2", [MH, D], F32, kind="ExternalInput").ap()
        self.i_l0_b2 = dt("l0_b2", [128, D // 128], F32, kind="ExternalInput").ap()
        self.i_l1_convw = dt("l1_convw", [128, FC * 4], F32, kind="ExternalInput").ap()
        self.i_l1_convb = dt("l1_convb", [128, FC], F32, kind="ExternalInput").ap()
        self.i_l1_qkw = dt("l1_qkw", [QK, D], F32, kind="ExternalInput").ap()
        self.i_l1_vw = dt("l1_vw", [QK, D], F32, kind="ExternalInput").ap()
        self.i_l1_gw = dt("l1_gw", [QK, D], F32, kind="ExternalInput").ap()
        self.i_l1_gb = dt("l1_gb", [128, NQ], F32, kind="ExternalInput").ap()
        self.i_l1_ow = dt("l1_ow", [D, QK], F32, kind="ExternalInput").ap()
        self.i_l1_ln = dt("l1_ln", [128, 2 * FC], F32, kind="ExternalInput").ap()
        self.i_l1_lnpost = dt("l1_lnpost", [128, 2 * FC], F32, kind="ExternalInput").ap()
        self.i_l1_freqs = dt("l1_freqs", [128, 16], F32, kind="ExternalInput").ap()
        self.i_l1_gamma = dt("l1_gamma", [16, 1], F32, kind="ExternalInput").ap()
        self.i_trow = dt("trow", [1, T], F32, kind="ExternalInput").ap()
        self.i_honehb = dt("honehb", [16, 2048], BF16, kind="ExternalInput").ap()
        self.i_l1_Wr = dt("l1_Wr", [D, E], F32, kind="ExternalInput").ap()
        self.i_l1_W1 = dt("l1_W1", [D, MH], F32, kind="ExternalInput").ap()
        self.i_l1_b1 = dt("l1_b1", [128, MH // 128], F32, kind="ExternalInput").ap()
        self.i_l1_W2 = dt("l1_W2", [MH, D], F32, kind="ExternalInput").ap()
        self.i_l1_b2 = dt("l1_b2", [128, D // 128], F32, kind="ExternalInput").ap()
        self.i_identb = dt("identb", [128, 128], BF16, kind="ExternalInput").ap()
        self.i_identf = dt("identf", [128, 128], F32, kind="ExternalInput").ap()
        self.i_inv1024 = dt("inv1024", [128, 1], BF16, kind="ExternalInput").ap()
        self.i_ones1x128 = dt("ones1x128", [1, 128], F32, kind="ExternalInput").ap()
        self.i_ones128x1b = dt("ones128x1b", [128, 1], BF16, kind="ExternalInput").ap()
        self.i_ones1x128b = dt("ones1x128b", [1, 128], BF16, kind="ExternalInput").ap()
        self.i_selE = dt("selE", [128, E], F32, kind="ExternalInput").ap()
        self.i_selET = dt("selET", [E, 128], F32, kind="ExternalInput").ap()
        self.i_triE = dt("triE", [128, 128], F32, kind="ExternalInput").ap()
        self.i_tril = dt("tril", [128, 128], F32, kind="ExternalInput").ap()
        self.i_iota8 = dt("iota8", [128, E], F32, kind="ExternalInput").ap()
        self.i_iotaJ = dt("iotaJ", [128, 32], F32, kind="ExternalInput").ap()
        self.i_iotat = dt("iotat", [128, 16], F32, kind="ExternalInput").ap()
        self.i_iotaf = dt("iotaf", [128, 128], F32, kind="ExternalInput").ap()
        self.i_bbase = dt("bbase", [128, 1], F32, kind="ExternalInput").ap()
        self.i_cid = dt("cid", [1, 2], U32, kind="ExternalInput").ap()
        self.o_y = dt("y", [D, T], F32, kind="ExternalOutput").ap()
        self.d_normtok = dt("d_normtok", [T, D], BF16)
        self.d_agnorm = dt("d_agnorm", [NTOK, D], BF16)
        self.d_gates_in = dt("d_gates_in", [E, T], F32)
        self.d_gates = dt("d_gates", [NCORES * E, T], F32)
        self.d_T0 = dt("d_T0", [NTOK, 1], U32)
        self.d_ret_in = [dt("d_ret0_in", [NS0, D], BF16), dt("d_ret1_in", [NS1, D], BF16)]
        self.d_ret = [dt("d_ret0", [NCORES * NS0, D], BF16),
                      dt("d_ret1", [NCORES * NS1, D], BF16)]

    def sbuf(self, name, shape, dtype, side=None):
        t = self.nc.sbuf_tensor("sb_" + name, shape, dtype, side=side)
        h = t.__enter__()
        self._ctx.append(t)
        return h

    def psum(self, name, shape, dtype=F32):
        t = self.nc.psum_tensor(name, shape, dtype)
        h = t.__enter__()
        self._ctx.append(t)
        return h

    def close(self):
        for t in reversed(self._ctx):
            t.__exit__(None, None, None)
        self._ctx = []

    from contextlib import contextmanager as _cm

    @_cm
    def scope(self, name, shape, dtype):
        t = self.nc.sbuf_tensor("sc_" + name, shape, dtype)
        h = t.__enter__()
        try:
            yield h
        finally:
            t.__exit__(None, None, None)
            self.mem_token = dict(self.cnt)

    def dbg(self, name, shape, dtype=F32):
        ap = self.nc.dram_tensor("dbg_" + name, shape, dtype, kind="ExternalOutput").ap()
        self.dbg_aps[name] = ap
        return ap

    def em(self, eng, reads, writes, fn, dma=False, coll=False):
        reads = list(reads)
        writes = list(writes)
        if coll:
            csem, cinc = "sC", 1
        elif dma:
            k = self.dma_rr[eng]
            self.dma_rr[eng] = (k + 1) % 8
            csem = (f"sW{k}" if eng == "gpsimd" else f"sH{k}")
            cinc = 16
        else:
            csem, cinc = ENG_SEM[eng], 1
        need = dict(self.mem_token)
        self.n_em += 1
        if self.serialize and self.ser_lo <= self.n_em < self.ser_hi:
            for s2 in ["sT", "sV", "sS", "sG", "sW", "sH", "sC"]:
                if self.cnt[s2]:
                    need[s2] = self.cnt[s2]
        for r in reads:
            w = self.last_w.get(r)
            if w:
                need[w[0]] = max(need.get(w[0], 0), w[1])
        for w_ in writes:
            w = self.last_w.get(w_)
            if w:
                need[w[0]] = max(need.get(w[0], 0), w[1])
            for rd in self.readers.get(w_, ()):
                need[rd[0]] = max(need.get(rd[0], 0), rd[1])
        if dma and self.cnt[csem] > 0:
            need[csem] = max(need.get(csem, 0), self.cnt[csem])
        issue_sem = ENG_SEM[eng]
        hw = self.hwm[eng]
        waits = []
        for s, v in sorted(need.items()):
            if s == issue_sem:
                continue
            if hw.get(s, -1) >= v:
                continue
            waits.append((s, v))
            hw[s] = v
        self.cnt[csem] += cinc
        tgt = self.cnt[csem]
        self.queues[eng].append((waits, fn, (csem, cinc)))
        new_w = (csem, tgt)
        for w_ in writes:
            self.last_w[w_] = new_w
            self.readers[w_] = []
        for r in reads:
            self.readers.setdefault(r, []).append(new_w)
        return new_w

    def flush(self):
        blk = self.block

        class _AutoDrain:

            def __init__(self, h):
                self.__dict__["_h"] = h
                self.__dict__["_nodrain"] = False

            def nodrain_on(self):
                self.__dict__["_nodrain"] = True

            def nodrain_off(self):
                self.__dict__["_nodrain"] = False

            def __getattr__(self, name):
                f = getattr(self._h, name)
                if name in ("wait_ge", "drain", "nop", "engine_nop", "memset"):
                    return f
                if not callable(f):
                    return f

                def wrap(*a, **k):
                    ins = f(*a, **k)
                    try:
                        out = k.get("out", a[0] if a else None)
                        sz = 1
                        for d in list(out.shape)[1:]:
                            sz *= d
                    except Exception:
                        sz = 0
                    if sz and sz < 256 and not self._nodrain:
                        return self._h.drain()
                    return ins
                return wrap

        for eng in ["sync", "gpsimd", "tensor", "vector", "scalar"]:
            q = self.queues[eng]
            if not q:
                continue

            def body(handle, q=q, eng=eng):
                h = _AutoDrain(handle) if eng in ("vector", "scalar") else handle
                for waits, fn, (csem, cinc) in q:
                    for s, v in waits:
                        handle.wait_ge(self.sems[s], v)
                    ins = fn(h)
                    ins.then_inc(self.sems[csem], cinc)
            getattr(blk, eng)(body)

    def build(self):
        nc = self.nc
        with nc.Block() as block:
            self.block = block
            self.alloc_persistent()
            self.p0_init()
            self.p1_l0_attn()
            if self.dev not in ("l0attn",):
                self.p_moe(0)
            if self.dev not in ("l0attn", "l0ffn"):
                self.p3_l1_attn()
            if self.dev not in ("l0attn", "l0ffn", "l1attn"):
                self.p_moe(1)
            self.p9_store()
            self.flush()
        self.close()
        return nc

    def alloc_persistent(self):
        self.x_sb = None
        self.ps8 = self.psum("ps8", [128, 6, 512], F32)
        self.psB = self.psum("psB", [128, 2, 1024], BF16)
        self.fence_buf = self.sbuf("fence", [1, 2], F32)
        self.identb = self.sbuf("identb", [128, 128], BF16)
        self.identf = self.sbuf("identf", [128, 128], F32)
        self.inv1024 = self.sbuf("inv1024", [128, 1], BF16)
        self.ones1x128 = self.sbuf("ones1x128", [1, 128], F32)
        self.ones128x1b = self.sbuf("ones128x1b", [128, 1], BF16)
        self.ones1x128b = self.sbuf("ones1x128b", [1, 128], BF16)
        self.selE = self.sbuf("selE", [128, E], F32)
        self.selET = self.sbuf("selET", [E, 128], F32)
        self.triE = self.sbuf("triE", [128, 128], F32)
        self.tril = self.sbuf("tril", [128, 128], F32)
        self.iota8 = self.sbuf("iota8", [128, E], F32)
        self.iotaJ = self.sbuf("iotaJ", [128, 32], F32)
        self.iotat = self.sbuf("iotat", [128, 16], F32)
        self.iotaf = self.sbuf("iotaf", [128, 128], F32)
        self.bbase = self.sbuf("bbase", [128, 1], F32)
        self.cid_sb = self.sbuf("cid_sb", [1, 2], U32)
        self.l0_ln = self.sbuf("l0_ln", [128, 2 * FC], F32)
        self.l1_ln = self.sbuf("l1_ln", [128, 2 * FC], F32)

        self.F1 = self.sbuf("F1", [128, 2, 512], F32)
        self.F2 = self.sbuf("F2", [128, 2, 512], F32)

    def p0_init(self):
        p = self
        loads = [
            (p.identb[:, :], p.i_identb[:, :], "identb"),
            (p.identf[:, :], p.i_identf[:, :], "identf"),
            (p.inv1024[:, :], p.i_inv1024[:, :], "inv1024"),
            (p.ones1x128[:, :], p.i_ones1x128[:, :], "ones1x128"),
            (p.ones128x1b[:, :], p.i_ones128x1b[:, :], "ones128x1b"),
            (p.ones1x128b[:, :], p.i_ones1x128b[:, :], "ones1x128b"),
            (p.selE[:, :], p.i_selE[:, :], "selE"),
            (p.selET[:, :], p.i_selET[:, :], "selET"),
            (p.triE[:, :], p.i_triE[:, :], "triE"),
            (p.tril[:, :], p.i_tril[:, :], "tril"),
            (p.iota8[:, :], p.i_iota8[:, :], "iota8"),
            (p.iotaJ[:, :], p.i_iotaJ[:, :], "iotaJ"),
            (p.iotat[:, :], p.i_iotat[:, :], "iotat"),
            (p.iotaf[:, :], p.i_iotaf[:, :], "iotaf"),
            (p.bbase[:, :], p.i_bbase[:, :], "bbase"),
            (p.cid_sb[:, :], p.i_cid[:, :], "cid"),
            (p.l0_ln[:, :], p.i_l0_ln[:, :], "l0_ln"),
            (p.l1_ln[:, :], p.i_l1_ln[:, :], "l1_ln"),
        ]
        for dst, srcap, resname in loads:
            self.em("sync", [], [resname],
                    lambda sy, dst=dst, srcap=srcap: sy.dma_start(out=dst, in_=srcap),
                    dma=True)

    def emit_ln(self, src_res, src_sb, out_res, out_sb, g_b, double,
                rows_out=None, W=T, accum=False, rows1_out=None):
        p = self
        NTc = W // TC
        with (
            self.scope(f"rA{self.cnt['sV']}", [1, TC], F32) as rowAl,
            self.scope(f"rB{self.cnt['sV']}", [1, TC], F32) as rowBl,
            self.scope(f"rC{self.cnt['sV']}", [1, TC], F32) as rowCl,
            self.scope(f"lnS_{self.cnt['sV']}", [128, FC, TC], BF16) as lnS,
        ):
            rowA = rowAl[0:1, :]
            rowB = rowBl[0:1, :]
            rowC = rowCl[0:1, :]
            for tc in range(NTc):
                def vcopy(v, tc=tc):
                    for fc in range(FC):
                        ins = v.tensor_copy(lnS[:, fc, :], src_sb[:, fc, bass.ts(tc, TC)])
                    return ins
                self.em("vector", [src_res, "lnS"], ["lnS"], vcopy)

                def stats(t, tc=tc):
                    for fc in range(FC):
                        ins = t.matmul(p.ps8[0:1, 4, :], p.inv1024[:, :],
                                       lnS[:, fc, :],
                                       start=(fc == 0), stop=(fc == FC - 1))
                    return ins
                self.em("tensor", ["lnS", "inv1024"], [("ps", 4)], stats)

                def evac(s, tc=tc):
                    return s.copy(rowA[:, :], p.ps8[0:1, 4, :])
                self.em("scalar", [("ps", 4)], ["rowA"], evac)

                def squares(s, tc=tc):
                    for fc in range(FC):
                        ins = s.activation(out=lnS[:, fc, :],
                                           in_=src_sb[:, fc, bass.ts(tc, TC)],
                                           func=AF.Square)
                    return ins
                self.em("scalar", [src_res, "lnS"], ["lnS"], squares)

                def stats2(t, tc=tc):
                    for fc in range(FC):
                        ins = t.matmul(p.ps8[0:1, 5, :], p.inv1024[:, :],
                                       lnS[:, fc, :],
                                       start=(fc == 0), stop=(fc == FC - 1))
                    return ins
                self.em("tensor", ["lnS", "inv1024"], [("ps", 5)], stats2)

                def evac2(s, tc=tc):
                    return s.copy(rowB[:, :], p.ps8[0:1, 5, :])
                self.em("scalar", [("ps", 5)], ["rowB"], evac2)

                def rowmath(v, tc=tc):
                    v.tensor_tensor(rowC[:, :], rowA[:, :], rowA[:, :], ALU.mult)
                    v.tensor_tensor(rowC[:, :], rowB[:, :], rowC[:, :], ALU.subtract)
                    v.tensor_scalar(rowB[:, :], rowC[:, :], float(LN_EPS), None, ALU.add)
                    v.drain()
                    v.reciprocal(rowB[:, :], rowB[:, :])
                    return v.drain()
                self.em("vector", ["rowA", "rowB"], ["rowB", "rowC"], rowmath)

                def sq1(s, tc=tc):
                    return s.sqrt(rowB[:, :], rowB[:, :])
                self.em("scalar", ["rowB"], ["rowB"], sq1)

                if rows1_out is not None:
                    spo, b1o = rows1_out

                    def sav1(v, tc=tc):
                        v.tensor_scalar(spo[0:1, bass.ts(tc, TC)], rowB[:, :], 1.0,
                                        None, ALU.add)
                        return v.tensor_tensor(b1o[0:1, bass.ts(tc, TC)], rowA[:, :],
                                               rowB[:, :], ALU.mult)
                    self.em("vector", ["rowA", "rowB"], ["lnrows1"], sav1)

                if double:
                    def dmath(v, tc=tc):
                        v.tensor_tensor(rowC[:, :], rowC[:, :], rowB[:, :], ALU.mult)
                        v.tensor_tensor(rowC[:, :], rowC[:, :], rowB[:, :], ALU.mult)
                        v.tensor_scalar(rowC[:, :], rowC[:, :], float(LN_EPS), None,
                                        ALU.add)
                        v.drain()
                        v.reciprocal(rowC[:, :], rowC[:, :])
                        return v.drain()
                    self.em("vector", ["rowB", "rowC"], ["rowC"], dmath)

                    def sq2(s, tc=tc):
                        return s.sqrt(rowC[:, :], rowC[:, :])
                    self.em("scalar", ["rowC"], ["rowC"], sq2)

                    def comb(v, tc=tc):
                        v.tensor_tensor(rowB[:, :], rowB[:, :], rowC[:, :], ALU.mult)
                        return v.tensor_tensor(rowA[:, :], rowA[:, :], rowB[:, :],
                                               ALU.mult)
                    self.em("vector", ["rowC", "rowA", "rowB"], ["rowA", "rowB"], comb)
                else:
                    def comb(v, tc=tc):
                        return v.tensor_tensor(rowA[:, :], rowA[:, :], rowB[:, :],
                                               ALU.mult)
                    self.em("vector", ["rowA", "rowB"], ["rowA"], comb)

                if rows_out is not None:
                    rA_out, rB_out = rows_out

                    def savrows(v, tc=tc):
                        v.tensor_copy(rA_out[:, bass.ts(tc, TC)], rowB[:, :])
                        return v.tensor_copy(rB_out[:, bass.ts(tc, TC)], rowA[:, :])
                    self.em("vector", ["rowA", "rowB"], ["lnrows"], savrows)

                def bcast(t, tc=tc):
                    t.matmul(p.ps8[:, 4, :], p.ones1x128[:, :], rowB[:, :],
                             start=True, stop=True)
                    return t.matmul(p.ps8[:, 5, :], p.ones1x128[:, :], rowA[:, :],
                                    start=True, stop=True)
                self.em("tensor", ["rowA", "rowB", "ones1x128"],
                        [("ps", 4), ("ps", 5)], bcast)

                def apply(v, tc=tc):
                    for fc in range(FC):
                        sl = p.F1[:, fc % 2, :]
                        v.tensor_tensor(sl, src_sb[:, fc, bass.ts(tc, TC)],
                                        p.ps8[:, 4, :], ALU.mult)
                        if g_b is None:
                            ins = v.tensor_tensor(out_sb[:, fc, bass.ts(tc, TC)], sl,
                                                  p.ps8[:, 5, :], ALU.subtract)
                        else:
                            gvec, bvec = g_b
                            v.tensor_tensor(sl, sl, p.ps8[:, 5, :], ALU.subtract)
                            if accum:
                                v.tensor_scalar(sl, sl, gvec[:, fc:fc + 1],
                                                bvec[:, fc:fc + 1], ALU.mult, ALU.add)
                                ins = v.tensor_tensor(out_sb[:, fc, bass.ts(tc, TC)],
                                                      out_sb[:, fc, bass.ts(tc, TC)],
                                                      sl, ALU.add)
                            else:
                                ins = v.tensor_scalar(out_sb[:, fc, bass.ts(tc, TC)],
                                                      sl, gvec[:, fc:fc + 1],
                                                      bvec[:, fc:fc + 1],
                                                      ALU.mult, ALU.add)
                    return ins
                self.em("vector", [src_res, out_res, ("ps", 4), ("ps", 5), "F1"],
                        [out_res, "F1"], apply)

    def p1_l0_attn(self):
        p = self
        nc = self.nc
        G0, G1, B0, B1c = 0, 1, 2, 3

        with (
            self.scope("n2_sb", [128, FC, T], BF16) as n2_sb,
            self.scope("os_sb", [128, 4 * NQ], F32) as os_sb,
            self.scope("bh_sb", [128, 2 * NQ], F32) as bh_sb,
            self.scope("bo_sb", [128, FC], F32) as bo_sb,

            self.scope("spb1", [1, 2 * T], F32) as spb1,
            self.scope("r_vec", [128, NQ], F32) as r_vec,
            self.scope("c2_fm", [128, NQ], F32) as c2_fm,
            self.scope("g0N_fm", [128, NQ], F32) as g0N_fm,

        ):
            for dst, srcap, resname in [
                (os_sb, p.i_l0_os[:, :], "os"), (bh_sb, p.i_l0_bh[:, :], "bh"),
                (bo_sb, p.i_l0_bo[:, :], "bo"),
            ]:
                self.em("sync", [], [resname],
                        lambda sy, dst=dst, srcap=srcap: sy.dma_start(out=dst[:, :], in_=srcap),
                        dma=True)

            with self.scope("xload", [128, FC, T], F32) as xload:
                self.em("sync", [], ["xload"],
                        lambda sy: sy.dma_start(
                            out=xload[:, :, :],
                            in_=p.i_x.rearrange("(fc k) t -> k fc t", k=128)),
                        dma=True)
                self.emit_ln("xload", xload, "n2", n2_sb,
                             (p.l0_ln[:, 0:FC], p.l0_ln[:, FC:2 * FC]), double=True,
                             rows1_out=(spb1[0:1, 0:T], spb1[0:1, T:2 * T]))

            if self.dev == "l0attn":
                apn = self.dbg("n2", [128, FC * T], F32)
                self.em("gpsimd", ["n2"], ["dbgn2"],
                        lambda g: g.dma_start(out=apn,
                                              in_=n2_sb[:, :, :].rearrange("k a b -> k (a b)")),
                        dma=True)
            with (
                self.scope("qkt", [128, NQ, QK], BF16) as qkt,
                self.scope("bqk_row", [1, QK], BF16) as bqk_row,
                self.scope("g1row", [1, QK], F32) as g1row,
                self.scope("b1row", [1, QK], F32) as b1row,
                self.scope("qs_row", [1, QK], F32) as qs_row,
            ):
                for dst, srcap, resname in [
                    (bqk_row[:, :], p.i_l0_bqkr[:, :], "bqk"),
                    (g1row[:, :], p.i_l0_osr[0:1, :], "osr"),
                    (b1row[:, :], p.i_l0_osr[1:2, :], "osrb"),
                ]:
                    self.em("sync", [], [resname],
                            lambda sy, dst=dst, srcap=srcap: sy.dma_start(out=dst, in_=srcap),
                            dma=True)
                with self.scope("wqk_sb", [128, FC, QK // 2], BF16) as wqk_sb:
                    for half in range(2):
                        self.em("gpsimd", [], ["wqk"],
                                lambda g, half=half: g.dma_start(
                                    out=wqk_sb[:, :, :],
                                    in_=p.i_l0_Wqk.rearrange("(fc k) n -> k fc n", k=128)[:, :, half * 1024:(half + 1) * 1024]),
                                dma=True)
                        for nh in range(2):
                            ncc = half * 2 + nh
                            for tt in range(16):
                                bank = (tt * 2 + nh) % 2

                                def mm(t, tt=tt, ncc=ncc, nh=nh, bank=bank, half=half):
                                    for fc in range(FC):
                                        t.matmul(p.ps8[:, bank, :],
                                                 n2_sb[:, fc, bass.ts(tt, 128)],
                                                 wqk_sb[:, fc, bass.ts(nh, 512)],
                                                 start=(fc == 0), stop=False)
                                    return t.matmul(p.ps8[:, bank, :], p.ones1x128b[:, :],
                                                    bqk_row[:, bass.ts(ncc, 512)],
                                                    start=False, stop=True)
                                self.em("tensor", ["n2", "wqk", "bqk", "ones1x128b"],
                                        [("ps", bank)], mm)

                                def ev(s, tt=tt, ncc=ncc, bank=bank):
                                    return s.activation(out=qkt[:, tt, bass.ts(ncc, 512)],
                                                        in_=p.ps8[:, bank, :], func=AF.Silu)
                                self.em("scalar", [("ps", bank)], [("qkt", tt, ncc)], ev)

                qkt_all = [("qkt", tt, ncc) for tt in range(16) for ncc in range(4)]

                for ncc in range(4):
                    def mm(t, ncc=ncc):
                        for tt in range(16):
                            ins = t.matmul(p.ps8[0:1, 2, :], p.ones128x1b[:, :],
                                           qkt[:, tt, bass.ts(ncc, 512)],
                                           start=(tt == 0), stop=(tt == 15))
                        return ins
                    self.em("tensor", qkt_all + ["ones128x1b"], [("ps", 2)], mm)

                    def ev(s, ncc=ncc):
                        return s.copy(qs_row[:, bass.ts(ncc, 512)], p.ps8[0:1, 2, :])
                    self.em("scalar", [("ps", 2)], [("qs", ncc)], ev)

                for dt in range(16):
                    def tp(t, dt=dt):
                        return t.transpose(p.ps8[:, 3, (dt % 2):(dt % 2) + 1],
                                           qs_row[:, bass.ts(dt, 128)], p.identf[0:1, 0:1])
                    self.em("tensor", [("qs", dt // 4), "identf"], [("ps", 3)], tp)

                    def ev(v, dt=dt):
                        return v.tensor_copy(c2_fm[:, dt:dt + 1], p.ps8[:, 3, (dt % 2):(dt % 2) + 1])
                    self.em("vector", [("ps", 3)], [("c2", dt)], ev)

                def c2math(v):
                    v.tensor_tensor(c2_fm[:, :], c2_fm[:, :],
                                    os_sb[:, G0 * NQ:(G0 + 1) * NQ], ALU.mult)
                    v.tensor_scalar(c2_fm[:, :], c2_fm[:, :], 1.0 / T, None, ALU.mult)
                    return v.tensor_scalar(g0N_fm[:, :], os_sb[:, G0 * NQ:(G0 + 1) * NQ],
                                           1.0 / T, None, ALU.mult)
                self.em("vector", [("c2", dt2) for dt2 in range(16)] + ["os"],
                        ["c2r", "g0N"], c2math)

                if self.dev == "l0attn":
                    apq = self.dbg("qkt", [128, NQ * QK], F32)
                    self.em("gpsimd", qkt_all, ["dbgqkt"],
                            lambda g: g.dma_start(out=apq,
                                                  in_=qkt[:, :, :].rearrange("k a b -> k (a b)")),
                            dma=True)
                self.em("vector", [("qs", i) for i in range(4)], ["qsall"],
                        lambda v: v.engine_nop())

                with (
                    self.scope("Bg1", [128, QK], BF16) as Bg1,
                    self.scope("Bb1", [128, QK], BF16) as Bb1,
                    self.scope("Bup", [128, QK], BF16) as Bup,
                    self.scope("simB", [128, QK], F32) as simB,
                    self.scope("mx_sb", [128, 2], F32) as mx_sb,
                    self.scope("sum_sb", [128, 2], F32) as sum_sb,
                ):
                    for i, (Bt, row, rres, wres) in enumerate([
                            (Bg1, g1row[0:1, :], "osr", "Bg1"),
                            (Bb1, b1row[0:1, :], "osrb", "Bb1"),
                            (Bup, qs_row[0:1, :], "qsall", "Bup")]):
                        for ncc in range(4):
                            def mm(t, row=row, ncc=ncc, rres=rres):
                                return t.matmul(p.ps8[:, 2 + ncc % 2, :], p.ones1x128[:, :],
                                                row[:, bass.ts(ncc, 512)], start=True, stop=True)
                            self.em("tensor", [rres, "ones1x128"], [("ps", 2 + ncc % 2)], mm)

                            def ev(v, Bt=Bt, ncc=ncc):
                                return v.tensor_copy(Bt[:, bass.ts(ncc, 512)],
                                                     p.ps8[:, 2 + ncc % 2, :])
                            self.em("vector", [("ps", 2 + ncc % 2)], [wres], ev)

                    def upmath(v):
                        v.tensor_tensor(Bup[:, :], Bup[:, :], Bg1[:, :], ALU.mult)
                        v.tensor_scalar(Bup[:, :], Bup[:, :], 1.0 / T, None, ALU.mult)
                        return v.tensor_tensor(Bup[:, :], Bup[:, :], Bb1[:, :], ALU.add)
                    self.em("vector", ["Bg1", "Bb1", "Bup"], ["Bup"], upmath)

                    for dt in range(16):
                        grp = 0
                        banks = [("ps", i) for i in range(4)]

                        def mm(t, dt=dt, grp=grp):
                            for tt in range(16):
                                for ncc in range(4):
                                    ins = t.matmul(p.ps8[:, ncc, :],
                                                   qkt[:, tt, bass.ts(dt, 128)],
                                                   qkt[:, tt, bass.ts(ncc, 512)],
                                                   start=(tt == 0), stop=(tt == 15))
                            return ins
                        self.em("tensor", qkt_all, banks, mm)

                        def corr(v, dt=dt, grp=grp):
                            v.tensor_scalar(simB[:, :], Bup[:, :],
                                            os_sb[:, B0 * NQ + dt:B0 * NQ + dt + 1],
                                            None, ALU.mult)
                            v.scalar_tensor_tensor(simB[:, :], Bb1[:, :], c2_fm[:, dt:dt + 1],
                                                   simB[:, :], ALU.mult, ALU.add)
                            for ncc in range(4):
                                sl = p.F1[:, ncc % 2, :]
                                v.tensor_tensor(sl, p.ps8[:, ncc, :],
                                                Bg1[:, bass.ts(ncc, 512)], ALU.mult)
                                v.scalar_tensor_tensor(simB[:, bass.ts(ncc, 512)], sl,
                                                       g0N_fm[:, dt:dt + 1],
                                                       simB[:, bass.ts(ncc, 512)],
                                                       ALU.mult, ALU.add)
                            v.drain()
                            v.tensor_reduce(mx_sb[:, grp:grp + 1], simB[:, :], axis=AXX,
                                            op=ALU.max)
                            v.drain()
                            return v.tensor_scalar(mx_sb[:, grp:grp + 1], mx_sb[:, grp:grp + 1],
                                                   -1.0, None, ALU.mult)
                        self.em("vector",
                                banks + ["Bg1", "Bb1", "Bup", "c2r", "g0N", "os", "F1"],
                                ["simB", ("mx", grp), "F1"], corr)

                        def expsum(s, dt=dt, grp=grp):
                            s.activation(out=simB[:, :], in_=simB[:, :], func=AF.Exp,
                                         bias=mx_sb[:, grp:grp + 1],
                                         accum_out=sum_sb[:, grp:grp + 1])
                            return s.drain()
                        self.em("scalar", ["simB", ("mx", grp)],
                                ["simB", ("sum", grp)], expsum)

                        def rdone(v, dt=dt, grp=grp):
                            v.reciprocal(mx_sb[:, grp:grp + 1], sum_sb[:, grp:grp + 1])
                            v.drain()
                            return v.tensor_tensor(r_vec[:, dt:dt + 1], sum_sb[:, grp:grp + 1],
                                                   mx_sb[:, grp:grp + 1], ALU.mult)
                        self.em("vector", [("sum", grp)], [("mx", grp), ("rv", dt)], rdone)

            p.x_sb = self.sbuf("x_sb", [128, FC, T], F32, side="right")
            self.em("sync", [], ["x"],
                    lambda sy: sy.dma_start(
                        out=p.x_sb[:, :, :],
                        in_=p.i_x.rearrange("(fc k) t -> k fc t", k=128)),
                    dma=True)
            self._p1_ln1_residual(spb1[0:1, 0:T], spb1[0:1, T:2 * T])
            with (
                self.scope("wout_ring", [128, 2, NQ, 128], BF16) as wout_ring,
                self.scope("wh_ring", [128, 8, FC, 128], BF16) as wh_ring,
                self.scope("u_sb", [128, NQ, TC], BF16) as u_sb,
                self.scope("vtmp", [128, 2, TC], F32) as vtmp,
            ):
                whr = p.i_l0_Wh.rearrange("(fc k) n -> k fc n", k=128)
                for tc in range(NT):
                    for ht in range(NQ):
                        for half in range(2):
                            slot = (ht * 2 + half) % 8
                            col = ht * 128 if half == 0 else HID + ht * 128
                            self.em("gpsimd", [], [("whr", slot)],
                                    lambda g, slot=slot, col=col: g.dma_start(
                                        out=wh_ring[:, slot, :, :], in_=whr[:, :, col:col + 128]),
                                    dma=True)

                            bank = 2 + (ht * 2 + half) % 2

                            def mm(t, tc=tc, slot=slot, bank=bank):
                                for fc in range(FC):
                                    ins = t.matmul(p.ps8[:, bank, :], wh_ring[:, slot, fc, :],
                                                   n2_sb[:, fc, bass.ts(tc, TC)],
                                                   start=(fc == 0), stop=(fc == FC - 1))
                                return ins
                            self.em("tensor", [("whr", slot), "n2"], [("ps", bank)], mm)

                            if half == 0:
                                def ev(s, ht=ht, bank=bank):
                                    return s.activation(out=vtmp[:, ht % 2, :],
                                                        in_=p.ps8[:, bank, :], func=AF.Silu,
                                                        bias=bh_sb[:, ht:ht + 1])
                                self.em("scalar", [("ps", bank), "bh"], [("vt", ht % 2)], ev)
                            else:
                                def ev(s, ht=ht, bank=bank):
                                    return s.activation(out=p.F2[:, ht % 2, :],
                                                        in_=p.ps8[:, bank, :], func=AF.Silu,
                                                        bias=bh_sb[:, NQ + ht:NQ + ht + 1])
                                self.em("scalar", [("ps", bank), "bh"], [("gt", ht % 2)], ev)

                                def uop(v, tc=tc, ht=ht):
                                    return v.scalar_tensor_tensor(
                                        u_sb[:, ht, :], vtmp[:, ht % 2, :],
                                        r_vec[:, ht:ht + 1], p.F2[:, ht % 2, :],
                                        ALU.mult, ALU.mult)
                                self.em("vector",
                                        [("vt", ht % 2), ("gt", ht % 2), ("rv", ht)],
                                        [("u", ht)], uop)

                    for ot in range(FC):
                        bank = 4 + ot % 2
                        wsl = ot % 2
                        self.em("gpsimd", [], [("woutr", wsl)],
                                lambda g, ot=ot, wsl=wsl: g.dma_start(
                                    out=wout_ring[:, wsl, :, :],
                                    in_=p.i_l0_Wo.rearrange("(ht k) n -> k ht n", k=128)[:, :, ot * 128:(ot + 1) * 128]),
                                dma=True)

                        def mm(t, tc=tc, ot=ot, bank=bank, wsl=wsl):
                            for ht in range(NQ):
                                ins = t.matmul(p.ps8[:, bank, :],
                                               wout_ring[:, wsl, ht, :],
                                               u_sb[:, ht, :],
                                               start=(ht == 0), stop=(ht == NQ - 1))
                            return ins
                        self.em("tensor",
                                [("u", ht2) for ht2 in range(NQ)] + [("woutr", wsl)],
                                [("ps", bank)], mm)

                        def res(v, tc=tc, ot=ot, bank=bank):
                            return v.scalar_tensor_tensor(
                                p.x_sb[:, ot, bass.ts(tc, TC)], p.ps8[:, bank, :],
                                bo_sb[:, ot:ot + 1], p.x_sb[:, ot, bass.ts(tc, TC)],
                                ALU.add, ALU.add)
                        self.em("vector", [("ps", bank), "bo", "x"], ["x"], res)

        if self.dev == "l0attn":
            apr = self.dbg("rv", [128, NQ], F32)
            self.em("sync", [("rv", i) for i in range(16)], ["dbgrv"],
                    lambda sy: sy.dma_start(out=apr, in_=r_vec[:, :]),
                    dma=True)
            ap = self.dbg("x1", [128, FC * T], F32)
            self.em("sync", ["x"], ["dbg"],
                    lambda sy: sy.dma_start(out=ap,
                                            in_=p.x_sb[:, :, :].rearrange("k a b -> k (a b)")),
                    dma=True)

    def p9_store(self):
        p = self
        self.em("sync", ["x"], ["y"],
                lambda sy: sy.dma_start(out=p.o_y.rearrange("(fc k) t -> k fc t", k=128),
                                        in_=p.x_sb[:, :, :]),
                dma=True)

        def final_wait(sy):
            return sy.nop()
        self.em("sync", ["y", "dbg"], [], final_wait)

    def _p1_ln1_residual(self, sp_sv, b1_sv):
        p = self
        for tc in range(NT):
            def lbc(t, tc=tc):
                t.matmul(p.ps8[:, 4, :], p.ones1x128[:, :],
                         sp_sv[0:1, bass.ts(tc, TC)], start=True, stop=True)
                return t.matmul(p.ps8[:, 5, :], p.ones1x128[:, :],
                                b1_sv[0:1, bass.ts(tc, TC)], start=True, stop=True)
            self.em("tensor", ["lnrows1", "ones1x128"], [("ps", 4), ("ps", 5)], lbc)

            def lapp(v, tc=tc):
                for fc in range(FC):
                    v.tensor_tensor(p.x_sb[:, fc, bass.ts(tc, TC)],
                                    p.x_sb[:, fc, bass.ts(tc, TC)],
                                    p.ps8[:, 4, :], ALU.mult)
                    ins = v.tensor_tensor(p.x_sb[:, fc, bass.ts(tc, TC)],
                                          p.x_sb[:, fc, bass.ts(tc, TC)],
                                          p.ps8[:, 5, :], ALU.subtract)
                return ins
            self.em("vector", [("ps", 4), ("ps", 5), "x"], ["x"], lapp)

    def p_moe(self, layer):
        p = self
        nc = self.nc
        CAP = CAP0 if layer == 0 else CAP1
        J = J0 if layer == 0 else J1
        NS = NS0 if layer == 0 else NS1
        NST = NS // 512
        act = AF.Silu if layer == 0 else AF.Relu
        dmp = (self.dev == "l0ffn" and layer == 0) or (self.dev == "l1ffn" and layer == 1)
        if dmp:
            apxin = self.dbg("xin", [128, FC * T], F32)
            self.em("sync", ["x"], ["dbg"],
                    lambda sy: sy.dma_start(
                        out=apxin, in_=p.x_sb[:, :, :].rearrange("k a b -> k (a b)")),
                    dma=True)
        iWr = p.i_l0_Wr if layer == 0 else p.i_l1_Wr
        iW1 = p.i_l0_W1 if layer == 0 else p.i_l1_W1
        iW2 = p.i_l0_W2 if layer == 0 else p.i_l1_W2
        ib1 = p.i_l0_b1 if layer == 0 else p.i_l1_b1
        ib2 = p.i_l0_b2 if layer == 0 else p.i_l1_b2
        d_ret_in = p.d_ret_in[layer]
        d_ret = p.d_ret[layer]
        L = f"m{layer}_"

        with ExitStack() as _st0:
            probs = _st0.enter_context(self.scope(L + "probs", [128, 16, E], F32))
            e1_sb = _st0.enter_context(self.scope(L + "e1", [128, 16], F32))
            e2_sb = _st0.enter_context(self.scope(L + "e2", [128, 16], F32))
            g1_sb = _st0.enter_context(self.scope(L + "g1", [128, 16], F32))
            g2_sb = _st0.enter_context(self.scope(L + "g2", [128, 16], F32))
            sm = _st0.enter_context(self.scope(L + "sm", [128, 16], F32))
            msk = _st0.enter_context(self.scope(L + "msk", [128, 16], U32))
            scr = _st0.enter_context(self.scope(L + "scr", [128, 1024], F32))
            cst = _st0.enter_context(self.scope(L + "cst", [128, 144], F32))
            rA_cm = self.scope(L + "rAs", [1, 2048], F32)
            rA_sv = rA_cm.__enter__()
            rB_cm = self.scope(L + "rBs", [1, 2048], F32)
            rB_sv = rB_cm.__enter__()
            nF_cm = self.scope(L + "nF", [128, FC, T], BF16)
            nF = nF_cm.__enter__()
            self.emit_ln("x", p.x_sb, L + "nF", nF, None, False,
                         rows_out=(rA_sv, rB_sv))

            with self.scope(L + "ntok", [128, 16, 1024], BF16) as ntok:
                for tt in range(16):
                    for fc in range(FC):
                        bk = (tt * FC + fc) % 2

                        def tp(t, tt=tt, fc=fc, bk=bk):
                            return t.transpose(p.psB[:, bk, 0:128],
                                               nF[:, fc, bass.ts(tt, 128)], p.identb[:, :])
                        self.em("tensor", [L + "nF", "identb"], [("psB", bk)], tp)

                        def ev(v, tt=tt, fc=fc, bk=bk):
                            return v.tensor_copy(ntok[:, tt, bass.ts(fc, 128)],
                                                 p.psB[:, bk, 0:128])
                        self.em("vector", [("psB", bk)], [(L + "ntok", tt)], ev)
                self.em("sync", [(L + "ntok", tt) for tt in range(16)], [L + "ntokd"],
                        lambda sy: sy.dma_start(
                            out=p.d_normtok.rearrange("(tt k) d -> k tt d", k=128),
                            in_=ntok[:, :, :]),
                        dma=True)
            self.em("gpsimd", [L + "ntokd"], [L + "agn"],
                    lambda g: g.collective_compute(
                        "AllGather", ALU.bypass,
                        replica_groups=[list(range(NCORES))],
                        ins=[p.d_normtok.ap().opt()], outs=[p.d_agnorm.ap().opt()]),
                    coll=True)

            with (
                self.scope(L + "wr", [128, FC, E], F32) as wr_sb,
                self.scope(L + "lg", [8, T], F32) as lg_sb,
                self.scope(L + "br", [E, 1], F32) as br_sb,
                self.scope(L + "n32", [128, FC, TC], F32) as n32,
            ):
                self.em("sync", [], [L + "wr"],
                        lambda sy: sy.dma_start(out=wr_sb[:, :, :],
                                                in_=iWr.rearrange("(fc k) e -> k fc e", k=128)),
                        dma=True)
                if layer == 0:
                    self.em("sync", [], [L + "br"],
                            lambda sy: sy.dma_start(out=br_sb[:, :], in_=p.i_l0_br[:, :]),
                            dma=True)
                for tc in range(NT):
                    def rbc(t, tc=tc):
                        t.matmul(p.ps8[:, 4, :], p.ones1x128[:, :],
                                 rA_sv[:, bass.ts(tc, TC)], start=True, stop=True)
                        return t.matmul(p.ps8[:, 5, :], p.ones1x128[:, :],
                                        rB_sv[:, bass.ts(tc, TC)], start=True, stop=True)
                    self.em("tensor", ["lnrows", "ones1x128"], [("ps", 4), ("ps", 5)], rbc)

                    def mkn32(v, tc=tc):
                        for fc in range(FC):
                            v.tensor_tensor(n32[:, fc, :], p.x_sb[:, fc, bass.ts(tc, TC)],
                                            p.ps8[:, 4, :], ALU.mult)
                            ins = v.tensor_tensor(n32[:, fc, :], n32[:, fc, :],
                                                  p.ps8[:, 5, :], ALU.subtract)
                        return ins
                    self.em("vector", ["x", ("ps", 4), ("ps", 5)], [L + "n32"], mkn32)

                    def mm(t, tc=tc):
                        for fc in range(FC):
                            ins = t.matmul(p.ps8[0:E, 0, :], wr_sb[:, fc, :],
                                           n32[:, fc, :],
                                           start=(fc == 0), stop=(fc == FC - 1))
                        return ins
                    self.em("tensor", [L + "n32", L + "wr"], [("ps", 0)], mm)

                    def ev(s, tc=tc):
                        if layer == 0:
                            return s.activation(out=lg_sb[:, bass.ts(tc, TC)],
                                                in_=p.ps8[0:E, 0, :], func=AF.Identity,
                                                bias=br_sb[:, :])
                        return s.copy(lg_sb[:, bass.ts(tc, TC)], p.ps8[0:E, 0, :])
                    self.em("scalar", [("ps", 0), L + "br"], [(L + "lg", tc)], ev)

                if dmp:
                    aplg = self.dbg("lg", [8, T], F32)
                    self.em("sync", [(L + "lg", tc2) for tc2 in range(NT)], ["dbg"],
                            lambda sy: sy.dma_start(out=aplg, in_=lg_sb[:, :]),
                            dma=True)
                for tt in range(16):
                    def tp(t, tt=tt):
                        return t.transpose(p.ps8[:, 1, 0:E],
                                           lg_sb[:, bass.ts(tt, 128)], p.identf[0:E, 0:E])
                    self.em("tensor", [(L + "lg", tt // 4), "identf"], [("ps", 1)], tp)

                    def ev(v, tt=tt):
                        return v.tensor_copy(probs[:, tt, :], p.ps8[:, 1, 0:E])
                    self.em("vector", [("ps", 1)], [(L + "pr", tt)], ev)

            nF_cm.__exit__(None, None, None)
            rB_cm.__exit__(None, None, None)
            rA_cm.__exit__(None, None, None)

            self.em("vector", [], [L + "c999"], lambda v: v.memset(cst[:, 0:8], 999.0))
            self.em("vector", [], [L + "cneg"], lambda v: v.memset(cst[:, 8:16], -1.0))
            for tt in range(16):
                def smax(v, tt=tt):
                    pr = probs[:, tt, :]
                    v.drain()
                    v.tensor_reduce(sm[:, 0:1], pr, axis=AXX, op=ALU.max)
                    v.drain()
                    return v.tensor_scalar(sm[:, 0:1], sm[:, 0:1], -1.0, None, ALU.mult)
                self.em("vector", [(L + "pr", tt)], [L + "sm"], smax)

                def sexp(s, tt=tt):
                    s.activation(out=probs[:, tt, :], in_=probs[:, tt, :], func=AF.Exp,
                                 bias=sm[:, 0:1], accum_out=sm[:, 1:2])
                    return s.drain()
                self.em("scalar", [(L + "pr", tt), L + "sm"],
                        [(L + "pr", tt), L + "sm"], sexp)

                def top2(v, tt=tt):
                    pr = probs[:, tt, :]
                    v.reciprocal(sm[:, 2:3], sm[:, 1:2])
                    v.drain()
                    v.tensor_scalar(pr, pr, sm[:, 2:3], None, ALU.mult)
                    v.drain()
                    v.tensor_reduce(g1_sb[:, tt:tt + 1], pr, axis=AXX, op=ALU.max)
                    v.drain()
                    v.tensor_scalar(msk[:, 0:8], pr, g1_sb[:, tt:tt + 1], None, ALU.is_ge)
                    v.drain()
                    v.select(sm[:, 0:8], msk[:, 0:8], p.iota8[:, :], cst[:, 0:8])
                    v.drain()
                    v.tensor_reduce(e1_sb[:, tt:tt + 1], sm[:, 0:8], axis=AXX, op=ALU.min)
                    v.drain()
                    v.tensor_scalar(msk[:, 0:8], p.iota8[:, :], e1_sb[:, tt:tt + 1],
                                    None, ALU.is_equal)
                    v.drain()
                    v.select(scr[:, 0:8], msk[:, 0:8], cst[:, 8:16], pr)
                    v.drain()
                    v.tensor_reduce(g2_sb[:, tt:tt + 1], scr[:, 0:8], axis=AXX, op=ALU.max)
                    v.drain()
                    v.tensor_scalar(msk[:, 8:16], scr[:, 0:8], g2_sb[:, tt:tt + 1],
                                    None, ALU.is_ge)
                    v.drain()
                    v.select(sm[:, 8:16], msk[:, 8:16], p.iota8[:, :], cst[:, 0:8])
                    v.drain()
                    v.tensor_reduce(e2_sb[:, tt:tt + 1], sm[:, 8:16], axis=AXX, op=ALU.min)
                    v.drain()
                    v.tensor_scalar(sm[:, 0:8], p.iota8[:, :], e1_sb[:, tt:tt + 1],
                                    None, ALU.is_equal)
                    v.tensor_scalar(sm[:, 8:16], p.iota8[:, :], e2_sb[:, tt:tt + 1],
                                    None, ALU.is_equal)
                    v.tensor_tensor(sm[:, 0:8], sm[:, 0:8], sm[:, 8:16], ALU.add)
                    return v.tensor_tensor(probs[:, tt, :], pr, sm[:, 0:8], ALU.mult)
                self.em("vector",
                        [(L + "pr", tt), L + "sm", "iota8", L + "c999", L + "cneg",
                         L + "scr"],
                        [(L + "pr", tt), (L + "e", tt), L + "scr", L + "sm"], top2)

            if dmp:
                for nmx, tsr in [("e1", e1_sb), ("e2", e2_sb), ("g1", g1_sb), ("g2", g2_sb)]:
                    apx = self.dbg(nmx, [128, 16], F32)
                    self.em("sync", [(L + "e", tt2) for tt2 in range(16)], ["dbg"],
                            lambda sy, apx=apx, tsr=tsr: sy.dma_start(out=apx, in_=tsr[:, :]),
                            dma=True)
                appr = self.dbg("probs", [128, 16 * E], F32)
                self.em("sync", [(L + "pr", tt2) for tt2 in range(16)], ["dbg"],
                        lambda sy: sy.dma_start(
                            out=appr, in_=probs[:, :, :].rearrange("k a b -> k (a b)")),
                        dma=True)
            with self.scope(L + "gfm", [8, T], F32) as gfm:
                for tt in range(16):
                    def tp(t, tt=tt):
                        return t.transpose(p.ps8[0:E, 1, 0:128], probs[:, tt, :],
                                           p.identf[:, :])
                    self.em("tensor", [(L + "e", tt), (L + "pr", tt), "identf"],
                            [("ps", 1)], tp)

                    def ev(v, tt=tt):
                        return v.tensor_copy(gfm[:, bass.ts(tt, 128)], p.ps8[0:E, 1, 0:128])
                    self.em("vector", [("ps", 1)], [(L + "gfm", tt)], ev)
                self.em("sync", [(L + "gfm", tt) for tt in range(16)], [L + "gfmd"],
                        lambda sy: sy.dma_start(out=p.d_gates_in[:, :], in_=gfm[:, :]),
                        dma=True)
            self.em("gpsimd", [L + "gfmd"], [L + "agg"],
                    lambda g: g.collective_compute(
                        "AllGather", ALU.bypass,
                        replica_groups=[list(range(NCORES))],
                        ins=[p.d_gates_in.ap().opt()], outs=[p.d_gates.ap().opt()]),
                    coll=True)
            dkeep = nc.dram_tensor(L + "dkeep", [128, 1024], F32)
            drank = nc.dram_tensor(L + "drank", [128, 1024], F32)
            with ExitStack() as _str:
                Gfull = _str.enter_context(self.scope(L + "Gfull", [128, 1024], F32))
                keep_sb = _str.enter_context(self.scope(L + "keep", [128, 1024], F32))
                rank_sb = _str.enter_context(self.scope(L + "rank", [128, 1024], F32))
                scr2 = _str.enter_context(self.scope(L + "scr2", [128, 1024], F32))
                self.em("sync", [L + "agg"], [L + "Gf"],
                        lambda sy: sy.dma_start(
                            out=Gfull[:, :],
                            in_=p.d_gates.rearrange("re (h t) -> (re h) t", h=2)),
                        dma=True)
                if layer == 0:
                    self._moe_bisect(L, Gfull, keep_sb, rank_sb, scr, scr2, CAP)
                else:
                    self._moe_fcfs(L, Gfull, keep_sb, rank_sb, scr, CAP)
                self.em("sync", [L + "keepw"], [L + "dkeep"],
                        lambda sy: sy.dma_start(out=dkeep[:, :], in_=keep_sb[:, :]),
                        dma=True)
                self.em("sync", [L + "rankw"], [L + "drank"],
                        lambda sy: sy.dma_start(out=drank[:, :], in_=rank_sb[:, :]),
                        dma=True)

            Ftok_cm = self.scope(L + "Ftok", [128, 32], U32)
            Ftok = Ftok_cm.__enter__()
            with ExitStack() as _st:
                kb = _st.enter_context(self.scope(L + "kb", [128, 128], F32))
                rk = _st.enter_context(self.scope(L + "rk", [128, 128], F32))
                key = _st.enter_context(self.scope(L + "key", [128, 128], F32))
                keyB = _st.enter_context(self.scope(L + "keyB", [128, 128], F32))
                kmk = _st.enter_context(self.scope(L + "kmk", [128, 128], U32))
                Ttok = _st.enter_context(self.scope(L + "Ttok", [128, 128], U32))
                mwiv = _st.enter_context(self.scope(L + "mwiv", [128, 8], F32))
                mwii = _st.enter_context(self.scope(L + "mwii", [128, 8], U32))
                cB = _st.enter_context(self.scope(L + "cB", [128, 1], F32))
                ONx = _st.enter_context(self.scope(L + "ONx", [128, 1], F32))
                ONj = _st.enter_context(self.scope(L + "ONj", [128, 1], F32))
                wB = _st.enter_context(self.scope(L + "wB", [128, 1], F32))
                idx2 = _st.enter_context(self.scope(L + "idx2", [128, 32], U32))

                def own_slice(g, dsrc, dst):
                    cval = nc.values_load(p.cid_sb[0:1, 0:1],
                                          engines=[mybir.EngineType.Pool])
                    src = dsrc.rearrange("(r eh) (blk t2) -> r eh blk t2",
                                         eh=16, t2=128)
                    src2 = src[:, bass.DynSlice(cval * 2, 2), :, :]
                    return g.dma_start(out=dst[:, :], in_=src2)
                self.em("gpsimd", [L + "dkeep", "cid"], [L + "kb"],
                        lambda g: own_slice(g, dkeep, kb), dma=True)
                self.em("gpsimd", [L + "drank", "cid"], [L + "rk"],
                        lambda g: own_slice(g, drank, rk), dma=True)

                def cnts(v):
                    v.drain()
                    v.tensor_reduce(cB[:, :], kb[:, :], axis=AXX, op=ALU.add)
                    return v.drain()
                self.em("vector", [L + "kb"], [L + "cB"], cnts)

                def offmm(t):
                    return t.matmul(p.ps8[:, 0, 0:1], p.tril[:, :], cB[:, :],
                                    start=True, stop=True)
                self.em("tensor", [L + "cB", "tril"], [("ps", 0)], offmm)

                def offev(v):
                    v.tensor_tensor(ONx[:, :], p.ps8[:, 0, 0:1], cB[:, :], ALU.add)
                    return v.tensor_scalar(wB[:, :], cB[:, :], -1.0, 128.0,
                                           ALU.mult, ALU.add)
                self.em("vector", [("ps", 0), L + "cB"], [L + "ONx", L + "wB"], offev)

                def keyinit(v):
                    v.tensor_scalar(keyB[:, :], p.iotaf[:, :], -1.0, 1000.0,
                                    ALU.mult, ALU.add)
                    v.tensor_tensor(key[:, :], kb[:, :], keyB[:, :], ALU.mult)
                    v.tensor_scalar(key[:, :], key[:, :], -1.0, None, ALU.add)
                    return v.drain()
                self.em("vector", [L + "kb", "iotaf"], [L + "key", L + "keyB"], keyinit)

                self.em("vector", [], [L + "cneg2"],
                        lambda v: v.memset(cst[:, 16:144], -2.0))
                for r in range(16):
                    def rnd(v, r=r):
                        v.max(mwiv[:, :], key[:, :])
                        v.max_index(mwii[:, :], mwiv[:, :], key[:, :])
                        v.drain()
                        v.tensor_scalar(Ttok[:, r * 8:(r + 1) * 8], mwii[:, :],
                                        p.bbase[:, :], None, ALU.add)
                        v.tensor_scalar(kmk[:, :], key[:, :], mwiv[:, 7:8], None, ALU.is_ge)
                        v.drain()
                        v.select(key[:, :], kmk[:, :], cst[:, 16:144], key[:, :])
                        return v.drain()
                    self.em("vector",
                            [L + "key", "bbase", L + "cneg2"],
                            [L + "key", L + "keyB", (L + "Tt", r)], rnd)

                self.em("sync", [(L + "Tt", r) for r in range(16)], [L + "Td"],
                        lambda sy: sy.dma_start(
                            out=p.d_T0.rearrange("(b r) one -> b (r one)", b=128),
                            in_=Ttok[:, :]),
                        dma=True)

                for j in range(J):
                    def onj(v, j=j):
                        return v.tensor_scalar(ONj[:, :], ONx[:, :], float(-128 * j),
                                               None, ALU.add)
                    self.em("vector", [L + "ONx"], [L + "ONj"], onj)

                    def cmpj(v, j=j):
                        return v.tensor_scalar(scr[:, 0:128], p.iotaf[:, :], ONj[:, :],
                                               None, ALU.is_ge)
                    self.em("vector", [L + "ONj", "iotaf"], [L + "scr"], cmpj)

                    def gmm(t, j=j):
                        return t.matmul(p.ps8[:, 1, 0:1], scr[:, 0:128], wB[:, :],
                                        start=True, stop=True)
                    self.em("tensor", [L + "scr", L + "wB"], [("ps", 1)], gmm)

                    def gev(v, j=j):
                        return v.tensor_scalar(idx2[:, j:j + 1], p.ps8[:, 1, 0:1],
                                               p.iotaJ[:, j:j + 1], float(NTOK - 1),
                                               ALU.add, ALU.min)
                    self.em("vector", [("ps", 1), "iotaJ"], [(L + "i2", j)], gev)

                for j in range(J):
                    self.em("gpsimd", [(L + "i2", j), L + "Td"], [(L + "Ft", j)],
                            lambda g, j=j: g.indirect_dma_start(
                                out=Ftok[:, j:j + 1], out_offset=None,
                                in_=p.d_T0.ap(),
                                in_offset=bass.IndirectOffsetOnAxis(
                                    ap=idx2[:, j:j + 1], axis=0)),
                            dma=True)

            if dmp:
                apf = self.dbg("Ftok", [128, J], U32)
                self.em("sync", [(L + "Ft", j2) for j2 in range(J)], ["dbg"],
                        lambda sy, apf=apf, J=J: sy.dma_start(out=apf, in_=Ftok[:, 0:J]),
                        dma=True)
            GH = 12
            with self.scope(L + "Xe", [128, FC, NS], BF16) as Xe:
                with self.scope(L + "gath", [128, GH, 1024], BF16) as gath:
                    for j in range(J):
                        gs = j % GH
                        self.em("gpsimd", [(L + "Ft", j), L + "agn"], [(L + "g", gs)],
                                lambda g, j=j, gs=gs: g.indirect_dma_start(
                                    out=gath[:, gs, :], out_offset=None,
                                    in_=p.d_agnorm.ap(),
                                    in_offset=bass.IndirectOffsetOnAxis(
                                        ap=Ftok[:, j:j + 1], axis=0)),
                                dma=True)
                        for fc in range(FC):
                            bk = (j * FC + fc) % 2

                            def tp(t, j=j, gs=gs, fc=fc, bk=bk):
                                return t.transpose(p.psB[:, bk, 0:128],
                                                   gath[:, gs, bass.ts(fc, 128)],
                                                   p.identb[:, :])
                            self.em("tensor", [(L + "g", gs), "identb"],
                                    [("psB", bk)], tp)

                            def ev(v, j=j, fc=fc, bk=bk):
                                return v.tensor_copy(Xe[:, fc, bass.ts(j, 128)],
                                                     p.psB[:, bk, 0:128])
                            self.em("vector", [("psB", bk)], [(L + "Xe", j)], ev)

                if dmp:
                    apxe = self.dbg("Xe", [128, FC * NS], BF16)
                    self.em("gpsimd", [(L + "Xe", j2) for j2 in range(NS // 128)], ["dbg"],
                            lambda g, apxe=apxe: g.dma_start(
                                out=apxe, in_=Xe[:, :, :].rearrange("k a b -> k (a b)")),
                            dma=True)
                self._moe_ffn(L, layer, Xe, iW1, iW2, ib1, ib2, act,
                              NS, NST, d_ret_in)
            Ftok_cm.__exit__(None, None, None)

            self.em("gpsimd", [L + "reti"], [L + "agr"],
                    lambda g: g.collective_compute(
                        "AllGather", ALU.bypass,
                        replica_groups=[list(range(NCORES))],
                        ins=[d_ret_in.ap().opt()], outs=[d_ret.ap().opt()]),
                    coll=True)

            self._moe_owner(L, layer, d_ret, NS, CAP, dkeep, drank,
                            e1_sb, e2_sb, g1_sb, g2_sb, scr, sm, cst)

        if dmp:
            ap = self.dbg("x2", [128, FC * T], F32)
            self.em("sync", ["x"], ["dbg"],
                    lambda sy: sy.dma_start(out=ap,
                                            in_=p.x_sb[:, :, :].rearrange("k a b -> k (a b)")),
                    dma=True)
            for nm, dsrc, shp, rres in [
                ("gates", p.d_gates, [NCORES * E, T], L + "agg"),
                ("keep", dkeep, [128, 1024], L + "dkeep"),
                ("rank", drank, [128, 1024], L + "drank"),
                ("T0", p.d_T0, [NTOK, 1], L + "Td"),
                ("ret", p.d_ret[layer], [NCORES * NS, D], L + "agr"),
                ("agn", p.d_agnorm, [NTOK, D], L + "agn"),
            ]:
                dty = BF16 if nm in ("ret", "agn") else (U32 if nm == "T0" else F32)
                apd = self.dbg(nm, shp, dty)
                self.em("sync", [rres], ["dbg"],
                        lambda sy, apd=apd, dsrc=dsrc: sy.dma_start(
                            out=apd, in_=dsrc.ap()),
                        dma=True)

    def _moe_bisect(self, L, Gfull, keep_sb, rank_sb, scr, scr2, CAP):
        p = self
        nc = self.nc
        with (
            self.scope(L + "bst", [E, 4], F32) as bst,
            self.scope(L + "bmk", [E, 1], U32) as bmk,
            self.scope(L + "cnt", [128, 1], F32) as cnt,
            self.scope(L + "thv", [128, 1], F32) as thv,
            self.scope(L + "rv8", [128, 1], F32) as rv8,
            self.scope(L + "rc1", [128, 1], F32) as rc1,
        ):
            self.em("vector", [], [L + "blo"], lambda v: v.memset(bst[:, 0:1], 0.0))
            self.em("vector", [], [L + "bhi"], lambda v: v.memset(bst[:, 1:2], 1.0))
            for it in range(BISECT_ITERS):
                def mid(v, it=it):
                    v.tensor_tensor(bst[:, 2:3], bst[:, 0:1], bst[:, 1:2], ALU.add)
                    return v.tensor_scalar(bst[:, 2:3], bst[:, 2:3], 0.5, None, ALU.mult)
                self.em("vector", [L + "blo", L + "bhi"], [L + "bmid"], mid)

                def bc(t, it=it):
                    return t.matmul(p.ps8[:, 2, 0:1], p.selET[:, :], bst[:, 2:3],
                                    start=True, stop=True)
                self.em("tensor", [L + "bmid", "selET"], [("ps", 2)], bc)

                def cmp(v, it=it):
                    v.tensor_scalar(scr[:, :], Gfull[:, :], p.ps8[:, 2, 0:1],
                                    None, ALU.is_gt)
                    v.drain()
                    v.tensor_reduce(cnt[:, :], scr[:, :], axis=AXX, op=ALU.add)
                    return v.drain()
                self.em("vector", [L + "Gf", ("ps", 2), L + "scr"],
                        [L + "scr", L + "cntw"], cmp)

                def cmm(t, it=it):
                    return t.matmul(p.ps8[0:E, 3, 0:1], p.selE[:, :], cnt[:, :],
                                    start=True, stop=True)
                self.em("tensor", [L + "cntw", "selE"], [("ps", 3)], cmm)

                def upd(v, it=it):
                    v.tensor_scalar(bmk[:, :], p.ps8[0:E, 3, 0:1], float(CAP),
                                    None, ALU.is_ge)
                    v.drain()
                    v.select(bst[:, 0:1], bmk[:, :], bst[:, 2:3], bst[:, 0:1])
                    v.select(bst[:, 1:2], bmk[:, :], bst[:, 1:2], bst[:, 2:3])
                    return v.drain()
                self.em("vector", [("ps", 3), L + "bmid"], [L + "blo", L + "bhi"], upd)

            def fbc(t):
                return t.matmul(p.ps8[:, 2, 0:1], p.selET[:, :], bst[:, 1:2],
                                start=True, stop=True)
            self.em("tensor", [L + "bhi", "selET"], [("ps", 2)], fbc)
            self.em("vector", [("ps", 2)], [L + "thv"],
                    lambda v: v.tensor_copy(thv[:, :], p.ps8[:, 2, 0:1]))

            def cmp2(v):
                v.tensor_scalar(scr[:, :], Gfull[:, :], thv[:, :], None, ALU.is_gt)
                v.drain()
                v.tensor_reduce(cnt[:, :], scr[:, :], axis=AXX, op=ALU.add)
                return v.drain()
            self.em("vector", [L + "Gf", L + "thv", L + "scr"],
                    [L + "scr", L + "cntw"], cmp2)

            def cmm2(t):
                return t.matmul(p.ps8[0:E, 3, 0:1], p.selE[:, :], cnt[:, :],
                                start=True, stop=True)
            self.em("tensor", [L + "cntw", "selE"], [("ps", 3)], cmm2)

            def re8(v):
                return v.tensor_scalar(bst[:, 3:4], p.ps8[0:E, 3, 0:1], -1.0,
                                       float(CAP), ALU.mult, ALU.add)
            self.em("vector", [("ps", 3)], [L + "re8"], re8)

            def rbc(t):
                return t.matmul(p.ps8[:, 2, 0:1], p.selET[:, :], bst[:, 3:4],
                                start=True, stop=True)
            self.em("tensor", [L + "re8", "selET"], [("ps", 2)], rbc)
            self.em("vector", [("ps", 2)], [L + "rv8"],
                    lambda v: v.tensor_copy(rv8[:, :], p.ps8[:, 2, 0:1]))

            def eq(v):
                return v.tensor_scalar(scr2[:, :], Gfull[:, :], thv[:, :],
                                       None, ALU.is_equal)
            self.em("vector", [L + "Gf", L + "thv"], [L + "scr2"], eq)

            def eqscan(v):
                v.drain()
                v.tensor_tensor_scan(keep_sb[:, :], scr2[:, :], scr2[:, :], 0.0,
                                     ALU.add, ALU.bypass)
                v.drain()
                return v.tensor_copy(rc1[:, :], keep_sb[:, 1023:1024])
            self.em("vector", [L + "scr2"], [L + "keepw", L + "rc1"], eqscan)

            def eqoff(t):
                return t.matmul(p.ps8[:, 2, 0:1], p.triE[:, :], rc1[:, :],
                                start=True, stop=True)
            self.em("tensor", [L + "rc1", "triE"], [("ps", 2)], eqoff)

            def keepfin(v):
                v.tensor_tensor(keep_sb[:, :], keep_sb[:, :], scr2[:, :], ALU.subtract)
                v.tensor_scalar(keep_sb[:, :], keep_sb[:, :], p.ps8[:, 2, 0:1],
                                None, ALU.add)
                v.tensor_scalar(keep_sb[:, :], keep_sb[:, :], rv8[:, :], None, ALU.is_lt)
                v.tensor_tensor(keep_sb[:, :], keep_sb[:, :], scr2[:, :], ALU.mult)
                return v.tensor_tensor(keep_sb[:, :], keep_sb[:, :], scr[:, :], ALU.add)
            self.em("vector", [L + "keepw", L + "scr2", L + "scr", L + "rv8", ("ps", 2)],
                    [L + "keepw"], keepfin)

            self._rank_from_keep(L, keep_sb, rank_sb, scr2, rc1)

    def _moe_fcfs(self, L, Gfull, keep_sb, rank_sb, scr, CAP):
        p = self
        with (
            self.scope(L + "rc1", [128, 1], F32) as rc1,
        ):
            def assigned(v):
                return v.tensor_scalar(scr[:, :], Gfull[:, :], 0.0, None, ALU.is_gt)
            self.em("vector", [L + "Gf"], [L + "scr"], assigned)

            def pscan(v):
                v.drain()
                v.tensor_tensor_scan(keep_sb[:, :], scr[:, :], scr[:, :], 0.0,
                                     ALU.add, ALU.bypass)
                v.drain()
                return v.tensor_copy(rc1[:, :], keep_sb[:, 1023:1024])
            self.em("vector", [L + "scr"], [L + "keepw", L + "rc1"], pscan)

            def poff(t):
                return t.matmul(p.ps8[:, 2, 0:1], p.triE[:, :], rc1[:, :],
                                start=True, stop=True)
            self.em("tensor", [L + "rc1", "triE"], [("ps", 2)], poff)

            def keepfin(v):
                v.tensor_tensor(keep_sb[:, :], keep_sb[:, :], scr[:, :], ALU.subtract)
                v.tensor_scalar(keep_sb[:, :], keep_sb[:, :], p.ps8[:, 2, 0:1],
                                None, ALU.add)
                v.tensor_scalar(keep_sb[:, :], keep_sb[:, :], float(CAP), None, ALU.is_lt)
                return v.tensor_tensor(keep_sb[:, :], keep_sb[:, :], scr[:, :], ALU.mult)
            self.em("vector", [L + "keepw", L + "scr", ("ps", 2)], [L + "keepw"], keepfin)

            self._rank_from_keep(L, keep_sb, rank_sb, scr, rc1)

    def _rank_from_keep(self, L, keep_sb, rank_sb, tmp, rc1):
        p = self

        def rscan(v):
            v.drain()
            v.tensor_tensor_scan(rank_sb[:, :], keep_sb[:, :], keep_sb[:, :], 0.0,
                                 ALU.add, ALU.bypass)
            v.drain()
            return v.tensor_copy(rc1[:, :], rank_sb[:, 1023:1024])
        self.em("vector", [L + "keepw"], [L + "rankw", L + "rc1k"], rscan)

        def roff(t):
            return t.matmul(p.ps8[:, 2, 0:1], p.triE[:, :], rc1[:, :],
                            start=True, stop=True)
        self.em("tensor", [L + "rc1k", "triE"], [("ps", 2)], roff)

        def rfin(v):
            v.tensor_tensor(rank_sb[:, :], rank_sb[:, :], keep_sb[:, :], ALU.subtract)
            return v.tensor_scalar(rank_sb[:, :], rank_sb[:, :], p.ps8[:, 2, 0:1],
                                   None, ALU.add)
        self.em("vector", [L + "rankw", L + "keepw", ("ps", 2)], [L + "rankw"], rfin)

    def _moe_ffn(self, L, layer, Xe, iW1, iW2, ib1, ib2, act, NS, NST, d_ret_in):
        p = self
        nc = self.nc
        with (
            self.scope(L + "b1", [128, 32], F32) as b1_sb,
            self.scope(L + "b2", [128, 8], F32) as b2_sb,
            self.scope(L + "w1r", [128, 2, FC, 128], BF16) as w1r,
            self.scope(L + "w2r", [128, 2, 32, 128], BF16) as w2r,
            self.scope(L + "h", [128, 32, 512], BF16) as h_sb,
            self.scope(L + "of", [128, 1, 512], BF16) as of_fm,
            self.scope(L + "ot", [128, 4, 1024], BF16) as ot,
        ):
            self.em("sync", [], [L + "b1"],
                    lambda sy: sy.dma_start(out=b1_sb[:, :], in_=ib1[:, :]), dma=True)
            self.em("sync", [], [L + "b2"],
                    lambda sy: sy.dma_start(out=b2_sb[:, :], in_=ib2[:, :]), dma=True)
            w1v = iW1.rearrange("(fc k) hh -> k fc hh", k=128)
            w2v = iW2.rearrange("(hh k) d2 -> k hh d2", k=128)
            for st in range(NST):
                for ht in range(32):
                    slot = ht % 2
                    self.em("gpsimd", [], [(L + "w1r", slot)],
                            lambda g, ht=ht, slot=slot: g.dma_start(
                                out=w1r[:, slot, :, :],
                                in_=w1v[:, :, ht * 128:(ht + 1) * 128]),
                            dma=True)
                    bank = ht % 2

                    def mm(t, st=st, ht=ht, slot=slot, bank=bank):
                        for fc in range(FC):
                            ins = t.matmul(p.ps8[:, bank, :], w1r[:, slot, fc, :],
                                           Xe[:, fc, bass.ts(st, 512)],
                                           start=(fc == 0), stop=(fc == FC - 1))
                        return ins
                    self.em("tensor",
                            [(L + "w1r", slot)] + [(L + "Xe", j) for j in range(NS // 128)],
                            [("ps", bank)], mm)

                    def ev(s, ht=ht, bank=bank):
                        return s.activation(out=h_sb[:, ht, :], in_=p.ps8[:, bank, :],
                                            func=act, bias=b1_sb[:, ht:ht + 1])
                    self.em("scalar", [("ps", bank), L + "b1"], [(L + "h", ht)], ev)

                for dt in range(FC):
                    slot = dt % 2
                    self.em("gpsimd", [], [(L + "w2r", slot)],
                            lambda g, dt=dt, slot=slot: g.dma_start(
                                out=w2r[:, slot, :, :],
                                in_=w2v[:, :, dt * 128:(dt + 1) * 128]),
                            dma=True)
                    bank = 2 + dt % 2

                    def mm2(t, st=st, dt=dt, slot=slot, bank=bank):
                        for ht in range(32):
                            ins = t.matmul(p.ps8[:, bank, :], w2r[:, slot, ht, :],
                                           h_sb[:, ht, :],
                                           start=(ht == 0), stop=(ht == 31))
                        return ins
                    self.em("tensor",
                            [(L + "w2r", slot)] + [(L + "h", ht2) for ht2 in range(32)],
                            [("ps", bank)], mm2)

                    def ev2(s, dt=dt, bank=bank):
                        return s.activation(out=of_fm[:, 0, :], in_=p.ps8[:, bank, :],
                                            func=AF.Identity, bias=b2_sb[:, dt:dt + 1])
                    self.em("scalar", [("ps", bank), L + "b2"], [L + "of"], ev2)

                    for sub in range(4):
                        bk = (dt * 4 + sub) % 2

                        def tp(t, dt=dt, sub=sub, bk=bk):
                            return t.transpose(p.psB[:, bk, 0:128],
                                               of_fm[:, 0, bass.ts(sub, 128)],
                                               p.identb[:, :])
                        self.em("tensor", [L + "of", "identb"], [("psB", bk)], tp)

                        def ev3(v, dt=dt, sub=sub, bk=bk):
                            return v.tensor_copy(ot[:, sub, bass.ts(dt, 128)],
                                                 p.psB[:, bk, 0:128])
                        self.em("vector", [("psB", bk)], [(L + "ot", sub)], ev3)

                for sub in range(4):
                    self.em("sync", [(L + "ot", sub)], [L + "reti"],
                            lambda sy, st=st, sub=sub: sy.dma_start(
                                out=d_ret_in[st * 512 + sub * 128:
                                             st * 512 + (sub + 1) * 128, :],
                                in_=ot[:, sub, :]),
                            dma=True)

    def _moe_owner(self, L, layer, d_ret, NS, CAP, dkeep, drank,
                   e1_sb, e2_sb, g1_sb, g2_sb, scr, sm, cst):
        p = self
        nc = self.nc

        with (
            self.scope(L + "koA", [8, 1024], F32) as koA,
            self.scope(L + "koB", [8, 1024], F32) as koB,
            self.scope(L + "roA", [8, 1024], F32) as roA,
            self.scope(L + "roB", [8, 1024], F32) as roB,
            self.scope(L + "ktok", [128, 16, E], F32) as ktok,
            self.scope(L + "rtok", [128, 16, E], F32) as rtok,
            self.scope(L + "oidx", [128, 32], U32) as oidx,
            self.scope(L + "gkv", [128, 32], F32) as gkv,
        ):
            def own16(g, dsrc, dst, h2):
                cval = nc.values_load(p.cid_sb[0:1, 0:1], engines=[mybir.EngineType.Pool])
                src = dsrc.rearrange("(r e h) t -> r h e t", e=8, h=2)
                return g.dma_start(
                    out=dst[:, :],
                    in_=src[bass.DynSlice(cval, 1), h2, :, :])
            for h2, (kd, rd) in enumerate([(koA, roA), (koB, roB)]):
                self.em("gpsimd", [L + "dkeep", "cid"], [(L + "ko", h2)],
                        lambda g, h2=h2, kd=kd: own16(g, dkeep, kd, h2), dma=True)
                self.em("gpsimd", [L + "drank", "cid"], [(L + "ro", h2)],
                        lambda g, h2=h2, rd=rd: own16(g, drank, rd, h2), dma=True)

            for srcs, dst_sb, rkeyb, wkey in [((koA, koB), ktok, L + "ko", L + "ktok"),
                                              ((roA, roB), rtok, L + "ro", L + "rtok")]:
                for h2 in range(2):
                    for tsub in range(8):
                        tt = h2 * 8 + tsub
                        src_sb = srcs[h2]

                        def tp(t, src_sb=src_sb, tsub=tsub):
                            return t.transpose(p.ps8[:, 1, 0:E],
                                               src_sb[:, bass.ts(tsub, 128)],
                                               p.identf[0:E, 0:E])
                        self.em("tensor", [(rkeyb, h2), "identf"], [("ps", 1)], tp)

                        def ev(v, dst_sb=dst_sb, tt=tt):
                            return v.tensor_copy(dst_sb[:, tt, :], p.ps8[:, 1, 0:E])
                        self.em("vector", [("ps", 1)], [(wkey, tt)], ev)

            for k in range(2):
                ek = e1_sb if k == 0 else e2_sb
                gk = g1_sb if k == 0 else g2_sb
                for tt in range(16):
                    col = k * 16 + tt

                    def mkidx(v, k=k, tt=tt, col=col, ek=ek, gk=gk):
                        v.tensor_scalar(sm[:, 0:8], p.iota8[:, :], ek[:, tt:tt + 1],
                                        None, ALU.is_equal)
                        v.tensor_tensor(sm[:, 0:8], sm[:, 0:8], rtok[:, tt, :],
                                        ALU.mult)
                        v.drain()
                        v.tensor_reduce(gkv[:, col:col + 1], sm[:, 0:8], axis=AXX,
                                        op=ALU.add)
                        v.drain()
                        v.tensor_scalar(oidx[:, col:col + 1], ek[:, tt:tt + 1],
                                        float(NS), None, ALU.mult)
                        v.tensor_tensor(oidx[:, col:col + 1], oidx[:, col:col + 1],
                                        gkv[:, col:col + 1], ALU.add)
                        v.tensor_scalar(oidx[:, col:col + 1], oidx[:, col:col + 1],
                                        float(NCORES * NS - 1), None, ALU.min)
                        v.tensor_scalar(sm[:, 0:8], p.iota8[:, :], ek[:, tt:tt + 1],
                                        None, ALU.is_equal)
                        v.tensor_tensor(sm[:, 0:8], sm[:, 0:8], ktok[:, tt, :], ALU.mult)
                        v.drain()
                        v.tensor_reduce(gkv[:, col:col + 1], sm[:, 0:8], axis=AXX,
                                        op=ALU.add)
                        v.drain()
                        if layer == 0:
                            ins = v.tensor_tensor(gkv[:, col:col + 1], gkv[:, col:col + 1],
                                                  gk[:, tt:tt + 1], ALU.mult)
                        else:
                            v.tensor_tensor(sm[:, 8:9], g1_sb[:, tt:tt + 1],
                                            g2_sb[:, tt:tt + 1], ALU.add)
                            v.drain()
                            v.reciprocal(sm[:, 9:10], sm[:, 8:9])
                            v.drain()
                            v.tensor_tensor(gkv[:, col:col + 1], gkv[:, col:col + 1],
                                            gk[:, tt:tt + 1], ALU.mult)
                            ins = v.tensor_tensor(gkv[:, col:col + 1], gkv[:, col:col + 1],
                                                  sm[:, 9:10], ALU.mult)
                        return ins
                    self.em("vector",
                            [(L + "rtok", tt), (L + "ktok", tt), (L + "e", tt),
                             "iota8", L + "sm"],
                            [(L + "oi", col), L + "sm"], mkidx)

            if (self.dev == "l1ffn" and layer == 1) or (self.dev == "l0ffn" and layer == 0):
                apg = self.dbg("gkv", [128, 32], F32)
                apo = self.dbg("oidx", [128, 32], U32)
                self.em("sync", [(L + "oi", c2) for c2 in range(32)], ["dbg"],
                        lambda sy: sy.dma_start(out=apg, in_=gkv[:, :]), dma=True)
                self.em("sync", [(L + "oi", c2) for c2 in range(32)], ["dbg"],
                        lambda sy: sy.dma_start(out=apo, in_=oidx[:, :]), dma=True)
            with self.scope(L + "ct", [128, 16, 1024], BF16) as ct:
                for k in range(2):
                    for tt in range(16):
                        col = k * 16 + tt
                        self.em("gpsimd", [(L + "oi", col), L + "agr"], [(L + "ct", tt)],
                                lambda g, col=col, tt=tt: g.indirect_dma_start(
                                    out=ct[:, tt, :], out_offset=None,
                                    in_=d_ret.ap(),
                                    in_offset=bass.IndirectOffsetOnAxis(
                                        ap=oidx[:, col:col + 1], axis=0)),
                                dma=True)

                        def scale(v, col=col, tt=tt):
                            return v.tensor_scalar(ct[:, tt, :], ct[:, tt, :],
                                                   gkv[:, col:col + 1], None, ALU.mult)
                        self.em("vector", [(L + "ct", tt), (L + "oi", col)],
                                [(L + "ct", tt)], scale)

                        for fc in range(FC):
                            bk = fc % 2

                            def tp(t, tt=tt, fc=fc, bk=bk):
                                return t.transpose(p.psB[:, bk, 0:128],
                                                   ct[:, tt, bass.ts(fc, 128)],
                                                   p.identb[:, :])
                            self.em("tensor", [(L + "ct", tt), "identb"],
                                    [("psB", bk)], tp)

                            def add(v, tt=tt, fc=fc, bk=bk):
                                return v.tensor_tensor(
                                    p.x_sb[:, fc, bass.ts(tt, 128)],
                                    p.x_sb[:, fc, bass.ts(tt, 128)],
                                    p.psB[:, bk, 0:128], ALU.add)
                            self.em("vector", [("psB", bk), "x"], ["x"], add)

    def p3_l1_attn(self):
        p = self
        nc = self.nc
        T2 = 1024
        PI = 3.141592653589793
        TWO_PI = 6.283185307179586
        INV2PI = 1.0 / TWO_PI
        MAGIC = 12582912.0
        C1 = 6.28125
        C2 = 0.001935073079168797
        C3 = 3.3173541006524535e-09
        L = "a1_"

        with ExitStack() as st0:
            xc = st0.enter_context(self.scope(L + "xc", [128, FC, T], BF16))
            convw = st0.enter_context(self.scope(L + "cw", [128, FC * 4], F32))
            convb = st0.enter_context(self.scope(L + "cb", [128, FC], F32))
            for dst, srcap, rn in [
                (convw[:, :], p.i_l1_convw[:, :], L + "cw"),
                (convb[:, :], p.i_l1_convb[:, :], L + "cb"),
            ]:
                self.em("sync", [], [rn],
                        lambda sy, dst=dst, srcap=srcap: sy.dma_start(out=dst, in_=srcap),
                        dma=True)

            with self.scope(L + "n3p", [128, FC, T + 3], BF16) as n3p, \
                    self.scope(L + "Fw", [128, T], F32) as Fw:
                self.em("vector", [], [L + "zpad"],
                        lambda v: v.memset(n3p[:, :, 0:3], 0.0))
                self.emit_ln("x", p.x_sb, L + "n3", n3p[:, :, 3:3 + T], None, False)
                for fc in range(FC):
                    def conv(v, fc=fc):
                        v.tensor_scalar(Fw[:, :], n3p[:, fc, 0:T],
                                        convw[:, 0 * FC + fc:0 * FC + fc + 1], None,
                                        ALU.mult)
                        for k in range(1, 4):
                            v.scalar_tensor_tensor(Fw[:, :], n3p[:, fc, k:k + T],
                                                   convw[:, k * FC + fc:k * FC + fc + 1],
                                                   Fw[:, :], ALU.mult, ALU.add)
                        return v.tensor_scalar(xc[:, fc, :], Fw[:, :],
                                               convb[:, fc:fc + 1], None, ALU.add)
                    self.em("vector",
                            [L + "n3", L + "zpad", L + "cw", L + "cb", L + "Fw"],
                            [(L + "xc", fc), L + "Fw"], conv)
            decay = st0.enter_context(self.scope(L + "dec", [16, T], BF16))
            decr = st0.enter_context(self.scope(L + "der", [16, T], BF16))
            lng = st0.enter_context(self.scope(L + "lng", [16, 1], F32))
            gamv = st0.enter_context(self.scope(L + "gam", [16, 1], F32))
            f1 = st0.enter_context(self.scope(L + "f1", [128, 16], F32))
            s128 = st0.enter_context(self.scope(L + "s128", [128, 16], F32))
            cosA = st0.enter_context(self.scope(L + "cosA", [128, 256], F32))
            sinA = st0.enter_context(self.scope(L + "sinA", [128, 256], F32))
            aacc = st0.enter_context(self.scope(L + "aacc", [128, 16], F32))
            kcw = st0.enter_context(self.scope(L + "kcw", [128, 16], F32))
            gbv = st0.enter_context(self.scope(L + "gb", [128, 16], F32))
            lnp = st0.enter_context(self.scope(L + "lnp", [128, 2 * FC], F32))
            freq = st0.enter_context(self.scope(L + "fq", [128, 16], F32))
            carry = st0.enter_context(self.scope(L + "cry", [128, 16], F32))
            kbnd = st0.enter_context(self.scope(L + "kbd", [128, 16], F32))
            wtmp = st0.enter_context(self.scope(L + "wtmp", [128, FC, 128], BF16))
            Fk2 = st0.enter_context(self.scope(L + "Fk2", [128, 16], F32))
            honehb = st0.enter_context(self.scope(L + "hohb", [16, 2048], BF16))
            for dst, srcap, rn in [
                (honehb[:, :], p.i_honehb[:, :], L + "hohb"),
                (gbv[:, :], p.i_l1_gb[:, :], L + "gb"),
                (lnp[:, :], p.i_l1_lnpost[:, :], L + "lnp"),
                (freq[:, :], p.i_l1_freqs[:, :], L + "fq"),
                (gamv[:, :], p.i_l1_gamma[:, :], L + "gam"),
            ]:
                self.em("sync", [], [rn],
                        lambda sy, dst=dst, srcap=srcap: sy.dma_start(out=dst, in_=srcap),
                        dma=True)

            d_decb = nc.dram_tensor(L + "d_decb", [16 * 128, T], BF16)
            d_derb = nc.dram_tensor(L + "d_derb", [16 * 128, T], BF16)
            with self.scope(L + "trow", [1, T], F32) as trow, \
                    self.scope(L + "dsc", [16, T // 4], F32) as dscr:
                self.em("sync", [], [L + "trow"],
                        lambda sy: sy.dma_start(out=trow[:, :], in_=p.i_trow[:, :]),
                        dma=True)

                def lngf(s):
                    return s.activation(out=lng[:, :], in_=gamv[:, :], func=AF.Ln)
                self.em("scalar", [L + "gam"], [L + "lng"], lngf)
                for tc in range(NT):
                    def tb(t, tc=tc):
                        return t.matmul(p.ps8[0:16, 2, :], p.ones1x128[0:1, 0:16],
                                        trow[:, bass.ts(tc, TC)], start=True, stop=True)
                    self.em("tensor", [L + "trow", "ones1x128"], [("ps", 2)], tb)

                    def dc(s, tc=tc):
                        return s.activation(out=decay[:, bass.ts(tc, TC)],
                                            in_=p.ps8[0:16, 2, :], func=AF.Exp,
                                            scale=lng[:, :])
                    self.em("scalar", [("ps", 2), L + "lng"], [(L + "dct", tc)], dc)

                    def dr(v, tc=tc):
                        v.tensor_scalar(dscr[:, :], decay[:, bass.ts(tc, TC)],
                                        1e-6, None, ALU.add)
                        v.drain()
                        v.reciprocal(dscr[:, :], dscr[:, :])
                        v.drain()
                        return v.tensor_copy(decr[:, bass.ts(tc, TC)], dscr[:, :])
                    self.em("vector", [(L + "dct", tc), L + "dsc"],
                            [(L + "der", tc), L + "dsc"], dr)

                with self.scope(L + "dtmp", [128, T], BF16) as dtmp:
                    for h in range(16):
                        for src_row, dnm, dd in ((decay, "dct", d_decb),
                                                 (decr, "der", d_derb)):
                            for tc in range(NT):
                                bk = tc % 2

                                def bmm(t, h=h, src_row=src_row, tc=tc, bk=bk):
                                    return t.matmul(
                                        p.ps8[:, 2 + bk, :],
                                        honehb[0:16, h * 128:(h + 1) * 128],
                                        src_row[0:16, bass.ts(tc, TC)],
                                        start=True, stop=True)
                                self.em("tensor",
                                        [(L + dnm, tc2) for tc2 in range(NT)]
                                        + [L + "hohb"],
                                        [("ps", 2 + bk)], bmm)

                                def bev(v, tc=tc, bk=bk):
                                    return v.tensor_copy(dtmp[:, bass.ts(tc, TC)],
                                                         p.ps8[:, 2 + bk, :])
                                self.em("vector", [("ps", 2 + bk)],
                                        [(L + "dtmp", tc)], bev)
                            self.em("sync", [(L + "dtmp", tc2) for tc2 in range(NT)],
                                    [L + "dtmp", (L + "ddrm", h, dnm)],
                                    lambda sy, h=h, dd=dd: sy.dma_start(
                                        out=dd.ap()[h * 128:(h + 1) * 128, :],
                                        in_=dtmp[:, :]),
                                    dma=True)
            qk = st0.enter_context(self.scope(L + "qk", [128, 16, T2], BF16))

            def _cody(v, out, in_, n):
                v.scalar_tensor_tensor(out, n, -C1, in_, ALU.mult, ALU.add)
                v.scalar_tensor_tensor(out, n, -C2, out, ALU.mult, ALU.add)
                return v.scalar_tensor_tensor(out, n, -C3, out, ALU.mult, ALU.add)

            def _wrap(v, out, in_, addv, tmp):
                v.tensor_scalar(out, in_, addv, None, ALU.add)
                v.tensor_scalar(tmp, out, INV2PI, MAGIC, ALU.mult, ALU.add)
                v.tensor_scalar(tmp, tmp, MAGIC, None, ALU.subtract)
                return v.scalar_tensor_tensor(out, tmp, -TWO_PI, out, ALU.mult, ALU.add)

            def fred(v):
                v.tensor_scalar(kcw[:, :], freq[:, :], INV2PI, MAGIC, ALU.mult, ALU.add)
                v.tensor_scalar(kcw[:, :], kcw[:, :], MAGIC, None, ALU.subtract)
                _cody(v, f1[:, :], freq[:, :], kcw[:, :])
                v.tensor_scalar(s128[:, :], f1[:, :], 128.0, None, ALU.mult)
                v.tensor_scalar(kcw[:, :], s128[:, :], INV2PI, MAGIC, ALU.mult, ALU.add)
                v.tensor_scalar(kcw[:, :], kcw[:, :], MAGIC, None, ALU.subtract)
                _cody(v, s128[:, :], s128[:, :], kcw[:, :])
                return v.memset(aacc[:, :], 0.0)
            self.em("vector", [L + "fq"], [L + "f1", L + "s128", L + "aacc"], fred)

            for a in range(16):
                def awrap(v, a=a):
                    if a > 0:
                        v.tensor_tensor(aacc[:, :], aacc[:, :], s128[:, :], ALU.add)
                        _wrap(v, aacc[:, :], aacc[:, :], 0.0, kcw[:, :])
                    _wrap(v, kbnd[:, :], aacc[:, :], PI / 2, kcw[:, :])
                    return v.drain()
                self.em("vector", [L + "s128", L + "aacc", L + "kcww"],
                        [L + "aacc", L + "kbd", L + "kcww"], awrap)

                def asin(s, a=a):
                    s.activation(out=sinA[:, a * 16:(a + 1) * 16], in_=aacc[:, :],
                                 func=AF.Sin, scale=-1.0)
                    return s.activation(out=cosA[:, a * 16:(a + 1) * 16], in_=kbnd[:, :],
                                        func=AF.Sin)
                self.em("scalar", [L + "aacc", L + "kbd"],
                        [(L + "cA", a), (L + "sA", a)], asin)

            def wtrans(isrc, h, wi, wdst, wkey):
                self.em("gpsimd", [], [L + "wtmp"],
                        lambda g: g.dma_start(
                            out=wtmp[:, :, :],
                            in_=isrc.rearrange("(hh k) c -> k hh c", k=128)[:, h, :]
                                .rearrange("k (fc c2) -> k fc c2", c2=128)),
                        dma=True)
                for fc in range(FC):
                    bk = fc % 2

                    def tpw(t, fc=fc, bk=bk):
                        return t.transpose(p.psB[:, bk, 0:128], wtmp[:, fc, :],
                                           p.identb[:, :])
                    self.em("tensor", [L + "wtmp", "identb"], [("psB", bk)], tpw)

                    def evw(v, fc=fc, bk=bk, wi=wi, wdst=wdst):
                        return v.tensor_copy(wdst[:, wi, fc, :],
                                             p.psB[:, bk, 0:128])
                    self.em("vector", [("psB", bk)], [(wkey, fc)], evw)

            for half in range(2):
                HB = half * T2
                sth = ExitStack()
                sth.__enter__()
                hs = f"{half}_"
                wT = sth.enter_context(self.scope(L + hs + "wT", [128, 3, FC, 128], BF16))
                kcw2 = sth.enter_context(self.scope(L + hs + "kcw2", [128, 128], F32))
                Fv = sth.enter_context(self.scope(L + hs + "Fv", [128, T2], F32))
                Fg = sth.enter_context(self.scope(L + hs + "Fg", [128, T2], F32))
                Fc = sth.enter_context(self.scope(L + hs + "Fc", [128, T2], F32))
                Fm = sth.enter_context(self.scope(L + hs + "Fm", [128, T2], F32))
                Fk = sth.enter_context(self.scope(L + hs + "Fk", [128, T2], F32))
                FmB = sth.enter_context(self.scope(L + hs + "FmB", [128, T2], BF16))
                Fb = sth.enter_context(self.scope(L + hs + "Fb", [128, 2, 128], F32))
                Fb2 = sth.enter_context(self.scope(L + hs + "Fb2", [128, 2, 128], F32))
                rr = sth.enter_context(self.scope(L + hs + "rr", [1, T2], F32))
                decR = sth.enter_context(self.scope(L + hs + "decR", [128, 2, T2], BF16))
                derR = sth.enter_context(self.scope(L + hs + "derR", [128, 2, T2], BF16))

                for h in range(16):
                    wtrans(p.i_l1_qkw, h, 0, wT, (L + "wTq"))
                    for tc in range(2):
                        bank = tc % 2

                        def mmq(t, h=h, tc=tc, bank=bank, HB=HB):
                            for fc in range(FC):
                                ins = t.matmul(p.ps8[:, bank, :], wT[:, 0, fc, :],
                                               xc[:, fc, HB + tc * 512:HB + (tc + 1) * 512],
                                               start=(fc == 0), stop=(fc == FC - 1))
                            return ins
                        self.em("tensor",
                                [((L + "wTq"), fc2) for fc2 in range(FC)]
                                + [(L + "xc", fc2) for fc2 in range(FC)],
                                [("ps", bank)], mmq)

                        def evq(s, h=h, tc=tc, bank=bank):
                            return s.activation(out=qk[:, h, bass.ts(tc, 512)],
                                                in_=p.ps8[:, bank, :], func=AF.Identity)
                        self.em("scalar", [("ps", bank)],
                                [(L + "qk", h), (L + "op", h)], evq)

                    def sqh(s, h=h):
                        return s.activation(out=FmB[:, :], in_=qk[:, h, :], func=AF.Square)
                    self.em("scalar", [(L + "qk", h), L + "FmB"], [L + "FmB"], sqh)

                    def ssm(t, h=h):
                        t.matmul(p.ps8[0:1, 2, :], p.ones128x1b[:, :], FmB[:, 0:512],
                                 start=True, stop=True)
                        return t.matmul(p.ps8[0:1, 3, :], p.ones128x1b[:, :],
                                        FmB[:, 512:1024], start=True, stop=True)
                    self.em("tensor", [L + "FmB", "ones128x1b"], [("ps", 2), ("ps", 3)], ssm)

                    def rrev(s, h=h):
                        s.copy(rr[0:1, 0:512], p.ps8[0:1, 2, :])
                        s.copy(rr[0:1, 512:1024], p.ps8[0:1, 3, :])
                        return s.sqrt(rr[0:1, :], rr[0:1, :])
                    self.em("scalar", [("ps", 2), ("ps", 3), L + "rr"], [L + "rr"], rrev)

                    def rmath(v, h=h):
                        v.tensor_scalar(rr[0:1, :], rr[0:1, :], 1e-12, None, ALU.max)
                        v.drain()
                        v.reciprocal(rr[0:1, :], rr[0:1, :])
                        return v.drain()
                    self.em("vector", [L + "rr"], [L + "rr"], rmath)

                    def bcq(t, h=h):
                        t.matmul(p.ps8[:, 2, :], p.ones1x128[:, :], rr[0:1, 0:512],
                                 start=True, stop=True)
                        return t.matmul(p.ps8[:, 3, :], p.ones1x128[:, :],
                                        rr[0:1, 512:1024], start=True, stop=True)
                    self.em("tensor", [L + "rr", "ones1x128"], [("ps", 2), ("ps", 3)], bcq)

                    def qmul(v, h=h):
                        v.tensor_tensor(qk[:, h, 0:512], qk[:, h, 0:512], p.ps8[:, 2, :],
                                        ALU.mult)
                        return v.tensor_tensor(qk[:, h, 512:1024], qk[:, h, 512:1024],
                                               p.ps8[:, 3, :], ALU.mult)
                    self.em("vector", [("ps", 2), ("ps", 3), (L + "qk", h)],
                            [(L + "qk", h)], qmul)

                for h in range(16):
                    wtrans(p.i_l1_vw, h, 1, wT, (L + "wTv"))
                    wtrans(p.i_l1_gw, h, 2, wT, (L + "wTg"))
                    for tc in range(2):
                        def mmv(t, h=h, tc=tc, HB=HB):
                            for fc in range(FC):
                                ins = t.matmul(p.ps8[:, tc, :], wT[:, 1, fc, :],
                                               xc[:, fc, HB + tc * 512:HB + (tc + 1) * 512],
                                               start=(fc == 0), stop=(fc == FC - 1))
                            return ins
                        self.em("tensor",
                                [((L + "wTv"), fc2) for fc2 in range(FC)]
                                + [(L + "xc", fc2) for fc2 in range(FC)],
                                [("ps", tc)], mmv)

                        def mmg(t, h=h, tc=tc, HB=HB):
                            for fc in range(FC):
                                ins = t.matmul(p.ps8[:, 2 + tc, :], wT[:, 2, fc, :],
                                               xc[:, fc, HB + tc * 512:HB + (tc + 1) * 512],
                                               start=(fc == 0), stop=(fc == FC - 1))
                            return ins
                        self.em("tensor",
                                [((L + "wTg"), fc2) for fc2 in range(FC)]
                                + [(L + "xc", fc2) for fc2 in range(FC)],
                                [("ps", 2 + tc)], mmg)

                    def evv(v, h=h):
                        v.tensor_copy(Fv[:, 0:512], p.ps8[:, 0, :])
                        return v.tensor_copy(Fv[:, 512:1024], p.ps8[:, 1, :])
                    self.em("vector", [("ps", 0), ("ps", 1)], [L + "Fv"], evv)

                    def evg(s, h=h):
                        s.activation(out=Fg[:, 0:512], in_=p.ps8[:, 2, :],
                                     func=AF.Sigmoid, bias=gbv[:, h:h + 1])
                        return s.activation(out=Fg[:, 512:1024], in_=p.ps8[:, 3, :],
                                            func=AF.Sigmoid, bias=gbv[:, h:h + 1])
                    self.em("scalar", [("ps", 2), ("ps", 3), L + "gb"], [L + "Fg"], evg)

                    def cosb(v, h=h):
                        v.tensor_scalar(Fb[:, 0, :], p.iotaf[:, :], f1[:, h:h + 1],
                                        None, ALU.mult)
                        v.tensor_scalar(kcw2[:, :], Fb[:, 0, :], INV2PI, MAGIC,
                                        ALU.mult, ALU.add)
                        v.tensor_scalar(kcw2[:, :], kcw2[:, :], MAGIC, None, ALU.subtract)
                        v.scalar_tensor_tensor(Fb[:, 0, :], kcw2[:, :], -C1,
                                               Fb[:, 0, :], ALU.mult, ALU.add)
                        v.scalar_tensor_tensor(Fb[:, 0, :], kcw2[:, :], -C2,
                                               Fb[:, 0, :], ALU.mult, ALU.add)
                        v.scalar_tensor_tensor(Fb[:, 0, :], kcw2[:, :], -C3,
                                               Fb[:, 0, :], ALU.mult, ALU.add)
                        v.tensor_scalar(Fb[:, 1, :], Fb[:, 0, :], PI / 2, None, ALU.add)
                        v.tensor_scalar(kcw2[:, :], Fb[:, 1, :], INV2PI, MAGIC,
                                        ALU.mult, ALU.add)
                        v.tensor_scalar(kcw2[:, :], kcw2[:, :], MAGIC, None, ALU.subtract)
                        v.scalar_tensor_tensor(Fb[:, 1, :], kcw2[:, :], -TWO_PI,
                                               Fb[:, 1, :], ALU.mult, ALU.add)
                        return v.drain()
                    self.em("vector", ["iotaf", L + "f1", L + "kcw2", L + "Fb"],
                            [L + "Fb", L + "kcw2"], cosb)

                    def cosb2(s, h=h):
                        s.activation(out=Fb2[:, 0, :], in_=Fb[:, 0, :], func=AF.Sin)
                        return s.activation(out=Fb2[:, 1, :], in_=Fb[:, 1, :], func=AF.Sin)
                    self.em("scalar", [L + "Fb"], [L + "Fb2"], cosb2)

                    def costile(v, h=h, half=half):
                        v.nodrain_on()
                        for a in range(8):
                            aa = half * 8 + a
                            v.tensor_scalar(Fc[:, bass.ts(a, 128)], Fb2[:, 1, :],
                                            cosA[:, aa * 16 + h:aa * 16 + h + 1],
                                            None, ALU.mult)
                        v.drain()
                        for a in range(8):
                            aa = half * 8 + a
                            sl = Fc[:, bass.ts(a, 128)]
                            v.scalar_tensor_tensor(sl, Fb2[:, 0, :],
                                                   sinA[:, aa * 16 + h:aa * 16 + h + 1],
                                                   sl, ALU.mult, ALU.add)
                        v.nodrain_off()
                        return v.drain()
                    self.em("vector",
                            [L + "Fb2"]
                            + [(L + "cA", half * 8 + a2) for a2 in range(8)]
                            + [(L + "sA", half * 8 + a2) for a2 in range(8)],
                            [L + "Fc"], costile)

                    def ksh(v, h=h, half=half):
                        if half == 0:
                            v.memset(Fk[:, 0:1], 0.0)
                        else:
                            v.tensor_copy(Fk[:, 0:1], carry[:, h:h + 1])
                        v.tensor_copy(Fk[:, 1:T2], qk[:, h, 0:T2 - 1])
                        ins = v.tensor_copy(carry[:, h:h + 1], qk[:, h, T2 - 1:T2])
                        return ins
                    self.em("vector", [(L + "qk", h), (L + "cryr", h, half)],
                            [L + "Fk", (L + "cryr", h, half + 1)], ksh)

                    slot = h % 2
                    self.em("sync", [(L + "ddrm", h, "dct"), (L + "decR", slot)],
                            [(L + "decL", h)],
                            lambda sy, h=h, slot=slot, HB=HB: sy.dma_start(
                                out=decR[:, slot, :],
                                in_=d_decb.ap()[h * 128:(h + 1) * 128, HB:HB + T2]),
                            dma=True)
                    self.em("sync", [(L + "ddrm", h, "der"), (L + "derR", slot)],
                            [(L + "derL", h)],
                            lambda sy, h=h, slot=slot, HB=HB: sy.dma_start(
                                out=derR[:, slot, :],
                                in_=d_derb.ap()[h * 128:(h + 1) * 128, HB:HB + T2]),
                            dma=True)

                    def state(v, h=h, half=half, slot=slot):
                        v.tensor_tensor(Fc[:, :], Fc[:, :], Fk[:, :], ALU.add)
                        v.tensor_tensor(Fc[:, :], Fc[:, :], Fv[:, :], ALU.mult)
                        v.tensor_tensor(Fc[:, :], Fc[:, :], Fg[:, :], ALU.mult)
                        v.tensor_tensor(Fc[:, :], Fc[:, :], decR[:, slot, :], ALU.mult)
                        v.drain()
                        if half == 0:
                            v.memset(Fk2[:, h:h + 1], 0.0)
                            v.drain()
                        v.tensor_tensor_scan(Fm[:, :], Fc[:, :], Fc[:, :],
                                             Fk2[:, h:h + 1], ALU.add, ALU.bypass)
                        v.drain()
                        v.tensor_copy(Fk2[:, h:h + 1], Fm[:, T2 - 1:T2])
                        v.tensor_tensor(Fm[:, :], Fm[:, :], derR[:, slot, :], ALU.mult)
                        return v.tensor_tensor(qk[:, h, :], Fm[:, :], qk[:, h, :],
                                               ALU.mult)
                    self.em("vector",
                            [L + "Fc", L + "Fk", L + "Fv", L + "Fg", L + "Fm",
                             (L + "decL", h), (L + "derL", h),
                             (L + "qk", h), (L + "sc2", h, half)],
                            [L + "Fc", L + "Fm", L + "Fk2", (L + "qk", h),
                             (L + "op", h), (L + "sc2", h, half + 1),
                             (L + "decR", slot), (L + "derR", slot)],
                            state)

                sth.__exit__(None, None, None)
                self._l1_ow(L, half, HB, qk, wtmp, lnp)

        if self.dev == "l1attn":
            ap = self.dbg("x3", [128, FC * T], F32)
            self.em("sync", ["x"], ["dbg"],
                    lambda sy: sy.dma_start(out=ap,
                                            in_=p.x_sb[:, :, :].rearrange("k a b -> k (a b)")),
                    dma=True)

    def _l1_ow(self, L, half, HB, op_sb, wtmp, lnp):
        p = self
        nc = self.nc
        with self.scope(L + f"o32_{half}", [128, FC, 1024], F32) as o32:
            stow = ExitStack()
            stow.__enter__()
            owTg = stow.enter_context(self.scope(L + f"owTg{half}", [128, 4, FC, 128], BF16))
            for hg in range(4):
                for hh in range(4):
                    h = hg * 4 + hh
                    self.em("gpsimd", [], [L + "wtmp"],
                            lambda g, h=h: g.dma_start(
                                out=wtmp[:, :, :],
                                in_=p.i_l1_ow.rearrange("(fc k) e2 -> k fc e2", k=128)
                                    [:, :, h * 128:(h + 1) * 128]),
                            dma=True)
                    for fc in range(FC):
                        bk = fc % 2

                        def tpo(t, fc=fc, bk=bk):
                            return t.transpose(p.psB[:, bk, 0:128], wtmp[:, fc, :],
                                               p.identb[:, :])
                        self.em("tensor", [L + "wtmp", "identb"], [("psB", bk)], tpo)

                        def evo(v, hh=hh, fc=fc, bk=bk):
                            return v.tensor_copy(owTg[:, hh, fc, :], p.psB[:, bk, 0:128])
                        self.em("vector", [("psB", bk)], [(L + "owTg", hh, fc)], evo)

                for fc in range(FC):
                    for tc in range(2):
                        bank = (fc * 2 + tc) % 2

                        def mmo(t, hg=hg, fc=fc, tc=tc, bank=bank):
                            for hh in range(4):
                                h = hg * 4 + hh
                                ins = t.matmul(p.ps8[:, bank, :], owTg[:, hh, fc, :],
                                               op_sb[:, h, bass.ts(tc, 512)],
                                               start=(hh == 0), stop=(hh == 3))
                            return ins
                        self.em("tensor",
                                [(L + "owTg", hh2, fc) for hh2 in range(4)]
                                + [(L + "op", hg * 4 + hh2) for hh2 in range(4)],
                                [("ps", bank)], mmo)

                        def evo2(v, hg=hg, fc=fc, tc=tc, bank=bank):
                            if hg == 0:
                                return v.tensor_copy(o32[:, fc, bass.ts(tc, 512)],
                                                     p.ps8[:, bank, :])
                            return v.tensor_tensor(o32[:, fc, bass.ts(tc, 512)],
                                                   o32[:, fc, bass.ts(tc, 512)],
                                                   p.ps8[:, bank, :], ALU.add)
                        self.em("vector", [("ps", bank), (L + "o32f", fc, tc)],
                                [(L + "o32f", fc, tc)], evo2)

            stow.__exit__(None, None, None)
            self.em("vector", [(L + "o32f", fc2, tc2) for fc2 in range(FC)
                               for tc2 in range(2)], [L + "o32r"],
                    lambda v: v.engine_nop())
            self.emit_ln(L + "o32r", o32, "x", p.x_sb[:, :, HB:HB + 1024],
                         (lnp[:, 0:FC], lnp[:, FC:2 * FC]), False, W=1024,
                         accum=True)

def _host_inputs(inputs, core):
    ii = {}
    ii["x"] = np.ascontiguousarray(inputs["L0_attn_x"][core].T.astype(np.float32))
    ii["l0_ln"] = np.concatenate([_featmaj(inputs["L0_attn_ln_g"]),
                                  _featmaj(inputs["L0_attn_ln_b"])], axis=1)
    ii["l0_Wh"] = _f32(inputs["L0_attn_W_hidden"])
    ii["l0_bh"] = _featmaj(inputs["L0_attn_b_hidden"])
    ii["l0_Wqk"] = _f32(inputs["L0_attn_W_qk"])
    osg = inputs["L0_attn_os_gamma"]
    osb = inputs["L0_attn_os_beta"]
    ii["l0_os"] = np.concatenate([_featmaj(osg[0]), _featmaj(osg[1]),
                                  _featmaj(osb[0]), _featmaj(osb[1])], axis=1)
    ii["l0_osr"] = np.stack([_f32(osg[1]), _f32(osb[1])], axis=0)
    ii["l0_bqkr"] = _bfh(inputs["L0_attn_b_qk"]).reshape(1, -1)
    ii["l0_Wo"] = _f32(inputs["L0_attn_W_out"])
    ii["l0_bo"] = _featmaj(inputs["L0_attn_b_out"])
    ii["l0_Wr"] = _f32(inputs["L0_ffn_Wr"])
    ii["l0_br"] = _f32(inputs["L0_ffn_br"]).reshape(E, 1)
    ii["l0_W1"] = _f32(inputs["L0_ffn_W1"][core])
    ii["l0_b1"] = _featmaj(inputs["L0_ffn_b1"][core])
    ii["l0_W2"] = _f32(inputs["L0_ffn_W2"][core])
    ii["l0_b2"] = _featmaj(inputs["L0_ffn_b2"][core])
    ii["l1_convw"] = np.concatenate(
        [_featmaj(inputs["L1_attn_conv_w"][:, 0, k]) for k in range(4)], axis=1)
    ii["l1_convb"] = _featmaj(inputs["L1_attn_conv_b"])
    ii["l1_qkw"] = _f32(inputs["L1_attn_qk_w"])
    ii["l1_vw"] = _f32(inputs["L1_attn_v_w"])
    ii["l1_gw"] = _f32(inputs["L1_attn_g_w"])
    ii["l1_gb"] = _featmaj(inputs["L1_attn_g_b"])
    ii["l1_ow"] = _f32(inputs["L1_attn_o_w"])
    ii["l1_ln"] = np.concatenate([_featmaj(np.ones(D, np.float32)),
                                  _featmaj(np.zeros(D, np.float32))], axis=1)
    ii["l1_lnpost"] = np.concatenate([_featmaj(inputs["L1_attn_ln_g"]),
                                      _featmaj(inputs["L1_attn_ln_b"])], axis=1)
    ii["l1_freqs"] = np.ascontiguousarray(_f32(inputs["L1_attn_freqs"]).T)
    ii["l1_gamma"] = _f32(inputs["L1_attn_gamma"]).reshape(16, 1)
    ii["trow"] = _f32(np.arange(T).reshape(1, T))
    ii["honehb"] = _bfh(np.repeat(np.eye(16), 128, axis=1))
    ii["l1_Wr"] = _f32(inputs["L1_ffn_w_router"])
    ii["l1_W1"] = _f32(inputs["L1_ffn_w1"][core])
    ii["l1_b1"] = _featmaj(inputs["L1_ffn_b1"][core])
    ii["l1_W2"] = _f32(inputs["L1_ffn_w2"][core])
    ii["l1_b2"] = _featmaj(inputs["L1_ffn_b2"][core])
    ii["identb"] = _bfh(np.eye(128))
    ii["identf"] = _f32(np.eye(128))
    ii["inv1024"] = _bfh(np.full((128, 1), 1.0 / 1024))
    ii["ones1x128"] = _f32(np.ones((1, 128)))
    ii["ones128x1b"] = _bfh(np.ones((128, 1)))
    ii["ones1x128b"] = _bfh(np.ones((1, 128)))
    pe = (np.arange(128) // 2) % 8
    ii["selE"] = _f32(pe[:, None] == np.arange(E)[None, :])
    ii["selET"] = _f32(pe[None, :] == np.arange(E)[:, None])
    rowrank = (np.arange(128) // 16) * 2 + (np.arange(128) % 2)
    ii["triE"] = _f32((pe[:, None] == pe[None, :]) &
                      (rowrank[:, None] < rowrank[None, :]))
    ii["tril"] = _f32(np.arange(128)[:, None] < np.arange(128)[None, :])
    ii["iota8"] = _f32(np.tile(np.arange(E)[None, :], (128, 1)))
    ii["iotaJ"] = _f32(np.arange(128)[:, None] + 128 * np.arange(32)[None, :])
    ii["iotat"] = _f32(np.arange(128)[:, None] + 128 * np.arange(16)[None, :])
    ii["iotaf"] = _f32(np.tile(np.arange(128)[None, :], (128, 1)))
    ii["bbase"] = _f32(128 * np.arange(128)[:, None])
    ii["cid"] = np.array([[core, core]], dtype=np.uint32)
    return ii


_LN_DEFAULT_LN = None


def kernel(**inputs):
    dev_stage = inputs.pop("_dev_stage", None)
    prog = Prog(dev_stage)
    nc = prog.build()
    in_maps = [_host_inputs(inputs, c) for c in range(NCORES)]
    res = run_bass_kernel_spmd(nc, in_maps, core_ids=list(range(NCORES)))
    if dev_stage:
        return res
    out = np.stack([r["y"].T for r in res.results], axis=0)
    return np.ascontiguousarray(out.astype(np.float32))


# version 05 (final)
# speedup vs baseline: 1.0457x; 1.0457x over previous
import numpy as np
import ml_dtypes
from contextlib import ExitStack

import concourse.bass as bass
import concourse.mybir as mybir
from concourse.bass_utils import run_bass_kernel_spmd

F32 = mybir.dt.float32
BF16 = mybir.dt.bfloat16
U32 = mybir.dt.uint32
AF = mybir.ActivationFunctionType
ALU = mybir.AluOpType
AXX = mybir.AxisListType.X

NCORES = 8
T = 2048
D = 1024
FC = 8
TC = 512
NT = T // TC
QK = 2048
NQ = QK // 128
HID = 2048
MH = 4096
E = 8
NTOK = NCORES * T
CAP0 = 2458
J0 = 20
NS0 = J0 * 128
CAP1 = 3072
J1 = 24
NS1 = J1 * 128
LN_EPS = 1e-5
BISECT_ITERS = 46

ENG_SEM = {"tensor": "sT", "vector": "sV", "scalar": "sS",
           "gpsimd": "sG", "sync": "sHq"}


def _bfh(x):
    return np.ascontiguousarray(np.asarray(x, dtype=np.float32).astype(ml_dtypes.bfloat16))


def _f32(x):
    return np.ascontiguousarray(np.asarray(x, dtype=np.float32))


def _featmaj(vec):
    v = np.asarray(vec, dtype=np.float32)
    n = v.shape[0]
    return np.ascontiguousarray(v.reshape(n // 128, 128).T)


class Prog:
    def __init__(self, dev_stage=None):
        self.dev = dev_stage
        nc = self.nc = bass.Bass("TRN2", target_bir_lowering=False, debug=False)
        self._ctx = []
        self.dbg_aps = {}
        self._declare_io()
        self.sems = {}
        self.cnt = {}
        for name in (["sT", "sV", "sS", "sG", "sHq", "sC"]
                     + [f"sW{i}" for i in range(8)] + [f"sH{i}" for i in range(8)]):
            t = nc.semaphore(name)
            self.sems[name] = t.__enter__()
            self._ctx.append(t)
            self.cnt[name] = 0
        self.dma_rr = {"gpsimd": 0, "sync": 0}
        self.last_w = {}
        self.readers = {}
        self.mem_token = {}
        self.queues = {k: [] for k in ENG_SEM}
        self.hwm = {k: {} for k in ENG_SEM}
        self.serialize = False
        self.ser_lo = 0
        self.ser_hi = 10**9
        self.n_em = 0
        self.fence_scalar = False
        self.fence_vector = False
        self.fence_tensor = False

    def _declare_io(self):
        nc = self.nc
        dt = nc.dram_tensor
        self.i_x = dt("x", [D, T], F32, kind="ExternalInput").ap()
        self.i_l0_ln = dt("l0_ln", [128, 2 * FC], F32, kind="ExternalInput").ap()
        self.i_l0_Wh = dt("l0_Wh", [D, 2 * HID], F32, kind="ExternalInput").ap()
        self.i_l0_bh = dt("l0_bh", [128, 2 * HID // 128], F32, kind="ExternalInput").ap()
        self.i_l0_Wqk = dt("l0_Wqk", [D, QK], F32, kind="ExternalInput").ap()
        self.i_l0_os = dt("l0_os", [128, 4 * NQ], F32, kind="ExternalInput").ap()
        self.i_l0_osr = dt("l0_osr", [2, QK], F32, kind="ExternalInput").ap()
        self.i_l0_bqkr = dt("l0_bqkr", [1, QK], BF16, kind="ExternalInput").ap()
        self.i_l0_Wo = dt("l0_Wo", [HID, D], F32, kind="ExternalInput").ap()
        self.i_l0_bo = dt("l0_bo", [128, D // 128], F32, kind="ExternalInput").ap()
        self.i_l0_Wr = dt("l0_Wr", [D, E], F32, kind="ExternalInput").ap()
        self.i_l0_br = dt("l0_br", [E, 1], F32, kind="ExternalInput").ap()
        self.i_l0_W1 = dt("l0_W1", [D, MH], F32, kind="ExternalInput").ap()
        self.i_l0_b1 = dt("l0_b1", [128, MH // 128], F32, kind="ExternalInput").ap()
        self.i_l0_W2 = dt("l0_W2", [MH, D], F32, kind="ExternalInput").ap()
        self.i_l0_b2 = dt("l0_b2", [128, D // 128], F32, kind="ExternalInput").ap()
        self.i_l1_convw = dt("l1_convw", [128, FC * 4], F32, kind="ExternalInput").ap()
        self.i_l1_convb = dt("l1_convb", [128, FC], F32, kind="ExternalInput").ap()
        self.i_l1_qkw = dt("l1_qkw", [QK, D], F32, kind="ExternalInput").ap()
        self.i_l1_vw = dt("l1_vw", [QK, D], F32, kind="ExternalInput").ap()
        self.i_l1_gw = dt("l1_gw", [QK, D], F32, kind="ExternalInput").ap()
        self.i_l1_gb = dt("l1_gb", [128, NQ], F32, kind="ExternalInput").ap()
        self.i_l1_ow = dt("l1_ow", [D, QK], F32, kind="ExternalInput").ap()
        self.i_l1_ln = dt("l1_ln", [128, 2 * FC], F32, kind="ExternalInput").ap()
        self.i_l1_lnpost = dt("l1_lnpost", [128, 2 * FC], F32, kind="ExternalInput").ap()
        self.i_l1_freqs = dt("l1_freqs", [128, 16], F32, kind="ExternalInput").ap()
        self.i_l1_gamma = dt("l1_gamma", [16, 1], F32, kind="ExternalInput").ap()
        self.i_trow = dt("trow", [1, T], F32, kind="ExternalInput").ap()
        self.i_honehb = dt("honehb", [16, 2048], BF16, kind="ExternalInput").ap()
        self.i_l1_Wr = dt("l1_Wr", [D, E], F32, kind="ExternalInput").ap()
        self.i_l1_W1 = dt("l1_W1", [D, MH], F32, kind="ExternalInput").ap()
        self.i_l1_b1 = dt("l1_b1", [128, MH // 128], F32, kind="ExternalInput").ap()
        self.i_l1_W2 = dt("l1_W2", [MH, D], F32, kind="ExternalInput").ap()
        self.i_l1_b2 = dt("l1_b2", [128, D // 128], F32, kind="ExternalInput").ap()
        self.i_identb = dt("identb", [128, 128], BF16, kind="ExternalInput").ap()
        self.i_identf = dt("identf", [128, 128], F32, kind="ExternalInput").ap()
        self.i_inv1024 = dt("inv1024", [128, 1], BF16, kind="ExternalInput").ap()
        self.i_ones1x128 = dt("ones1x128", [1, 128], F32, kind="ExternalInput").ap()
        self.i_ones128x1b = dt("ones128x1b", [128, 1], BF16, kind="ExternalInput").ap()
        self.i_ones1x128b = dt("ones1x128b", [1, 128], BF16, kind="ExternalInput").ap()
        self.i_selE = dt("selE", [128, E], F32, kind="ExternalInput").ap()
        self.i_selET = dt("selET", [E, 128], F32, kind="ExternalInput").ap()
        self.i_triE = dt("triE", [128, 128], F32, kind="ExternalInput").ap()
        self.i_tril = dt("tril", [128, 128], F32, kind="ExternalInput").ap()
        self.i_iota8 = dt("iota8", [128, E], F32, kind="ExternalInput").ap()
        self.i_iotaJ = dt("iotaJ", [128, 32], F32, kind="ExternalInput").ap()
        self.i_iotat = dt("iotat", [128, 16], F32, kind="ExternalInput").ap()
        self.i_iotaf = dt("iotaf", [128, 128], F32, kind="ExternalInput").ap()
        self.i_bbase = dt("bbase", [128, 1], F32, kind="ExternalInput").ap()
        self.i_cid = dt("cid", [1, 2], U32, kind="ExternalInput").ap()
        self.o_y = dt("y", [D, T], F32, kind="ExternalOutput").ap()
        self.d_normtok = dt("d_normtok", [T, D], BF16)
        self.d_agnorm = dt("d_agnorm", [NTOK, D], BF16)
        self.d_gates_in = dt("d_gates_in", [E, T], F32)
        self.d_gates = dt("d_gates", [NCORES * E, T], F32)
        self.d_T0 = dt("d_T0", [NTOK, 1], U32)
        self.d_ret_in = [dt("d_ret0_in", [NS0, D], BF16), dt("d_ret1_in", [NS1, D], BF16)]
        self.d_ret = [dt("d_ret0", [NCORES * NS0, D], BF16),
                      dt("d_ret1", [NCORES * NS1, D], BF16)]

    def sbuf(self, name, shape, dtype, side=None):
        t = self.nc.sbuf_tensor("sb_" + name, shape, dtype, side=side)
        h = t.__enter__()
        self._ctx.append(t)
        return h

    def psum(self, name, shape, dtype=F32):
        t = self.nc.psum_tensor(name, shape, dtype)
        h = t.__enter__()
        self._ctx.append(t)
        return h

    def close(self):
        for t in reversed(self._ctx):
            t.__exit__(None, None, None)
        self._ctx = []

    from contextlib import contextmanager as _cm

    @_cm
    def scope(self, name, shape, dtype):
        t = self.nc.sbuf_tensor("sc_" + name, shape, dtype)
        h = t.__enter__()
        try:
            yield h
        finally:
            t.__exit__(None, None, None)
            self.mem_token = dict(self.cnt)

    def dbg(self, name, shape, dtype=F32):
        ap = self.nc.dram_tensor("dbg_" + name, shape, dtype, kind="ExternalOutput").ap()
        self.dbg_aps[name] = ap
        return ap

    def em(self, eng, reads, writes, fn, dma=False, coll=False):
        reads = list(reads)
        writes = list(writes)
        if coll:
            csem, cinc = "sC", 1
        elif dma:
            k = self.dma_rr[eng]
            self.dma_rr[eng] = (k + 1) % 8
            csem = (f"sW{k}" if eng == "gpsimd" else f"sH{k}")
            cinc = 16
        else:
            csem, cinc = ENG_SEM[eng], 1
        need = dict(self.mem_token)
        self.n_em += 1
        if self.serialize and self.ser_lo <= self.n_em < self.ser_hi:
            for s2 in ["sT", "sV", "sS", "sG", "sW", "sH", "sC"]:
                if self.cnt[s2]:
                    need[s2] = self.cnt[s2]
        for r in reads:
            w = self.last_w.get(r)
            if w:
                need[w[0]] = max(need.get(w[0], 0), w[1])
        for w_ in writes:
            w = self.last_w.get(w_)
            if w:
                need[w[0]] = max(need.get(w[0], 0), w[1])
            for rd in self.readers.get(w_, ()):
                need[rd[0]] = max(need.get(rd[0], 0), rd[1])
        if dma and self.cnt[csem] > 0:
            need[csem] = max(need.get(csem, 0), self.cnt[csem])
        issue_sem = ENG_SEM[eng]
        hw = self.hwm[eng]
        waits = []
        for s, v in sorted(need.items()):
            if s == issue_sem:
                continue
            if hw.get(s, -1) >= v:
                continue
            waits.append((s, v))
            hw[s] = v
        self.cnt[csem] += cinc
        tgt = self.cnt[csem]
        self.queues[eng].append((waits, fn, (csem, cinc)))
        new_w = (csem, tgt)
        for w_ in writes:
            self.last_w[w_] = new_w
            self.readers[w_] = []
        for r in reads:
            self.readers.setdefault(r, []).append(new_w)
        return new_w

    def flush(self):
        blk = self.block

        class _AutoDrain:

            def __init__(self, h):
                self.__dict__["_h"] = h
                self.__dict__["_nodrain"] = False

            def nodrain_on(self):
                self.__dict__["_nodrain"] = True

            def nodrain_off(self):
                self.__dict__["_nodrain"] = False

            def __getattr__(self, name):
                f = getattr(self._h, name)
                if name in ("wait_ge", "drain", "nop", "engine_nop", "memset"):
                    return f
                if not callable(f):
                    return f

                def wrap(*a, **k):
                    ins = f(*a, **k)
                    try:
                        out = k.get("out", a[0] if a else None)
                        sz = 1
                        for d in list(out.shape)[1:]:
                            sz *= d
                    except Exception:
                        sz = 0
                    if sz and sz < 256 and not self._nodrain:
                        return self._h.drain()
                    return ins
                return wrap

        for eng in ["sync", "gpsimd", "tensor", "vector", "scalar"]:
            q = self.queues[eng]
            if not q:
                continue

            def body(handle, q=q, eng=eng):
                h = _AutoDrain(handle) if eng in ("vector", "scalar") else handle
                for waits, fn, (csem, cinc) in q:
                    for s, v in waits:
                        handle.wait_ge(self.sems[s], v)
                    ins = fn(h)
                    ins.then_inc(self.sems[csem], cinc)
            getattr(blk, eng)(body)

    def build(self):
        nc = self.nc
        with nc.Block() as block:
            self.block = block
            self.alloc_persistent()
            self.p0_init()
            self.p1_l0_attn()
            if self.dev not in ("l0attn",):
                self.p_moe(0)
            if self.dev not in ("l0attn", "l0ffn"):
                self.p3_l1_attn()
            if self.dev not in ("l0attn", "l0ffn", "l1attn"):
                self.p_moe(1)
            self.p9_store()
            self.flush()
        self.close()
        return nc

    def alloc_persistent(self):
        self.x_sb = None
        self.ps8 = self.psum("ps8", [128, 6, 512], F32)
        self.psB = self.psum("psB", [128, 2, 1024], BF16)
        self.fence_buf = self.sbuf("fence", [1, 2], F32)
        self.identb = self.sbuf("identb", [128, 128], BF16)
        self.identf = self.sbuf("identf", [128, 128], F32)
        self.inv1024 = self.sbuf("inv1024", [128, 1], BF16)
        self.ones1x128 = self.sbuf("ones1x128", [1, 128], F32)
        self.ones128x1b = self.sbuf("ones128x1b", [128, 1], BF16)
        self.ones1x128b = self.sbuf("ones1x128b", [1, 128], BF16)
        self.selE = self.sbuf("selE", [128, E], F32)
        self.selET = self.sbuf("selET", [E, 128], F32)
        self.triE = self.sbuf("triE", [128, 128], F32)
        self.tril = self.sbuf("tril", [128, 128], F32)
        self.iota8 = self.sbuf("iota8", [128, E], F32)
        self.iotaJ = self.sbuf("iotaJ", [128, 32], F32)
        self.iotat = self.sbuf("iotat", [128, 16], F32)
        self.iotaf = self.sbuf("iotaf", [128, 128], F32)
        self.bbase = self.sbuf("bbase", [128, 1], F32)
        self.cid_sb = self.sbuf("cid_sb", [1, 2], U32)
        self.l0_ln = self.sbuf("l0_ln", [128, 2 * FC], F32)
        self.l1_ln = self.sbuf("l1_ln", [128, 2 * FC], F32)

        self.F1 = self.sbuf("F1", [128, 2, 512], F32)
        self.F2 = self.sbuf("F2", [128, 2, 512], F32)

    def p0_init(self):
        p = self
        loads = [
            (p.identb[:, :], p.i_identb[:, :], "identb"),
            (p.identf[:, :], p.i_identf[:, :], "identf"),
            (p.inv1024[:, :], p.i_inv1024[:, :], "inv1024"),
            (p.ones1x128[:, :], p.i_ones1x128[:, :], "ones1x128"),
            (p.ones128x1b[:, :], p.i_ones128x1b[:, :], "ones128x1b"),
            (p.ones1x128b[:, :], p.i_ones1x128b[:, :], "ones1x128b"),
            (p.selE[:, :], p.i_selE[:, :], "selE"),
            (p.selET[:, :], p.i_selET[:, :], "selET"),
            (p.triE[:, :], p.i_triE[:, :], "triE"),
            (p.tril[:, :], p.i_tril[:, :], "tril"),
            (p.iota8[:, :], p.i_iota8[:, :], "iota8"),
            (p.iotaJ[:, :], p.i_iotaJ[:, :], "iotaJ"),
            (p.iotat[:, :], p.i_iotat[:, :], "iotat"),
            (p.iotaf[:, :], p.i_iotaf[:, :], "iotaf"),
            (p.bbase[:, :], p.i_bbase[:, :], "bbase"),
            (p.cid_sb[:, :], p.i_cid[:, :], "cid"),
            (p.l0_ln[:, :], p.i_l0_ln[:, :], "l0_ln"),
            (p.l1_ln[:, :], p.i_l1_ln[:, :], "l1_ln"),
        ]
        for dst, srcap, resname in loads:
            self.em("sync", [], [resname],
                    lambda sy, dst=dst, srcap=srcap: sy.dma_start(out=dst, in_=srcap),
                    dma=True)

    def emit_ln(self, src_res, src_sb, out_res, out_sb, g_b, double,
                rows_out=None, W=T, accum=False, rows1_out=None):
        p = self
        NTc = W // TC
        with (
            self.scope(f"rA{self.cnt['sV']}", [1, TC], F32) as rowAl,
            self.scope(f"rB{self.cnt['sV']}", [1, TC], F32) as rowBl,
            self.scope(f"rC{self.cnt['sV']}", [1, TC], F32) as rowCl,
            self.scope(f"lnS_{self.cnt['sV']}", [128, FC, TC], BF16) as lnS,
        ):
            rowA = rowAl[0:1, :]
            rowB = rowBl[0:1, :]
            rowC = rowCl[0:1, :]
            for tc in range(NTc):
                def vcopy(v, tc=tc):
                    for fc in range(FC):
                        ins = v.tensor_copy(lnS[:, fc, :], src_sb[:, fc, bass.ts(tc, TC)])
                    return ins
                self.em("vector", [src_res, "lnS"], ["lnS"], vcopy)

                def stats(t, tc=tc):
                    for fc in range(FC):
                        ins = t.matmul(p.ps8[0:1, 4, :], p.inv1024[:, :],
                                       lnS[:, fc, :],
                                       start=(fc == 0), stop=(fc == FC - 1))
                    return ins
                self.em("tensor", ["lnS", "inv1024"], [("ps", 4)], stats)

                def evac(s, tc=tc):
                    return s.copy(rowA[:, :], p.ps8[0:1, 4, :])
                self.em("scalar", [("ps", 4)], ["rowA"], evac)

                def squares(s, tc=tc):
                    for fc in range(FC):
                        ins = s.activation(out=lnS[:, fc, :],
                                           in_=src_sb[:, fc, bass.ts(tc, TC)],
                                           func=AF.Square)
                    return ins
                self.em("scalar", [src_res, "lnS"], ["lnS"], squares)

                def stats2(t, tc=tc):
                    for fc in range(FC):
                        ins = t.matmul(p.ps8[0:1, 5, :], p.inv1024[:, :],
                                       lnS[:, fc, :],
                                       start=(fc == 0), stop=(fc == FC - 1))
                    return ins
                self.em("tensor", ["lnS", "inv1024"], [("ps", 5)], stats2)

                def evac2(s, tc=tc):
                    return s.copy(rowB[:, :], p.ps8[0:1, 5, :])
                self.em("scalar", [("ps", 5)], ["rowB"], evac2)

                def rowmath(v, tc=tc):
                    v.tensor_tensor(rowC[:, :], rowA[:, :], rowA[:, :], ALU.mult)
                    v.tensor_tensor(rowC[:, :], rowB[:, :], rowC[:, :], ALU.subtract)
                    v.tensor_scalar(rowB[:, :], rowC[:, :], float(LN_EPS), None, ALU.add)
                    v.drain()
                    v.reciprocal(rowB[:, :], rowB[:, :])
                    return v.drain()
                self.em("vector", ["rowA", "rowB"], ["rowB", "rowC"], rowmath)

                def sq1(s, tc=tc):
                    return s.sqrt(rowB[:, :], rowB[:, :])
                self.em("scalar", ["rowB"], ["rowB"], sq1)

                if rows1_out is not None:
                    spo, b1o = rows1_out

                    def sav1(v, tc=tc):
                        v.tensor_scalar(spo[0:1, bass.ts(tc, TC)], rowB[:, :], 1.0,
                                        None, ALU.add)
                        return v.tensor_tensor(b1o[0:1, bass.ts(tc, TC)], rowA[:, :],
                                               rowB[:, :], ALU.mult)
                    self.em("vector", ["rowA", "rowB"], ["lnrows1"], sav1)

                if double:
                    def dmath(v, tc=tc):
                        v.tensor_tensor(rowC[:, :], rowC[:, :], rowB[:, :], ALU.mult)
                        v.tensor_tensor(rowC[:, :], rowC[:, :], rowB[:, :], ALU.mult)
                        v.tensor_scalar(rowC[:, :], rowC[:, :], float(LN_EPS), None,
                                        ALU.add)
                        v.drain()
                        v.reciprocal(rowC[:, :], rowC[:, :])
                        return v.drain()
                    self.em("vector", ["rowB", "rowC"], ["rowC"], dmath)

                    def sq2(s, tc=tc):
                        return s.sqrt(rowC[:, :], rowC[:, :])
                    self.em("scalar", ["rowC"], ["rowC"], sq2)

                    def comb(v, tc=tc):
                        v.tensor_tensor(rowB[:, :], rowB[:, :], rowC[:, :], ALU.mult)
                        return v.tensor_tensor(rowA[:, :], rowA[:, :], rowB[:, :],
                                               ALU.mult)
                    self.em("vector", ["rowC", "rowA", "rowB"], ["rowA", "rowB"], comb)
                else:
                    def comb(v, tc=tc):
                        return v.tensor_tensor(rowA[:, :], rowA[:, :], rowB[:, :],
                                               ALU.mult)
                    self.em("vector", ["rowA", "rowB"], ["rowA"], comb)

                if rows_out is not None:
                    rA_out, rB_out = rows_out

                    def savrows(v, tc=tc):
                        v.tensor_copy(rA_out[:, bass.ts(tc, TC)], rowB[:, :])
                        return v.tensor_copy(rB_out[:, bass.ts(tc, TC)], rowA[:, :])
                    self.em("vector", ["rowA", "rowB"], ["lnrows"], savrows)

                def bcast(t, tc=tc):
                    t.matmul(p.ps8[:, 4, :], p.ones1x128[:, :], rowB[:, :],
                             start=True, stop=True)
                    return t.matmul(p.ps8[:, 5, :], p.ones1x128[:, :], rowA[:, :],
                                    start=True, stop=True)
                self.em("tensor", ["rowA", "rowB", "ones1x128"],
                        [("ps", 4), ("ps", 5)], bcast)

                def apply(v, tc=tc):
                    for fc in range(FC):
                        sl = p.F1[:, fc % 2, :]
                        v.tensor_tensor(sl, src_sb[:, fc, bass.ts(tc, TC)],
                                        p.ps8[:, 4, :], ALU.mult)
                        if g_b is None:
                            ins = v.tensor_tensor(out_sb[:, fc, bass.ts(tc, TC)], sl,
                                                  p.ps8[:, 5, :], ALU.subtract)
                        else:
                            gvec, bvec = g_b
                            v.tensor_tensor(sl, sl, p.ps8[:, 5, :], ALU.subtract)
                            if accum:
                                v.tensor_scalar(sl, sl, gvec[:, fc:fc + 1],
                                                bvec[:, fc:fc + 1], ALU.mult, ALU.add)
                                ins = v.tensor_tensor(out_sb[:, fc, bass.ts(tc, TC)],
                                                      out_sb[:, fc, bass.ts(tc, TC)],
                                                      sl, ALU.add)
                            else:
                                ins = v.tensor_scalar(out_sb[:, fc, bass.ts(tc, TC)],
                                                      sl, gvec[:, fc:fc + 1],
                                                      bvec[:, fc:fc + 1],
                                                      ALU.mult, ALU.add)
                    return ins
                self.em("vector", [src_res, out_res, ("ps", 4), ("ps", 5), "F1"],
                        [out_res, "F1"], apply)

    def p1_l0_attn(self):
        p = self
        nc = self.nc
        G0, G1, B0, B1c = 0, 1, 2, 3

        with (
            self.scope("n2_sb", [128, FC, T], BF16) as n2_sb,
            self.scope("os_sb", [128, 4 * NQ], F32) as os_sb,
            self.scope("bh_sb", [128, 2 * NQ], F32) as bh_sb,
            self.scope("bo_sb", [128, FC], F32) as bo_sb,

            self.scope("spb1", [1, 2 * T], F32) as spb1,
            self.scope("r_vec", [128, NQ], F32) as r_vec,
            self.scope("c2_fm", [128, NQ], F32) as c2_fm,
            self.scope("g0N_fm", [128, NQ], F32) as g0N_fm,

        ):
            for dst, srcap, resname in [
                (os_sb, p.i_l0_os[:, :], "os"), (bh_sb, p.i_l0_bh[:, :], "bh"),
                (bo_sb, p.i_l0_bo[:, :], "bo"),
            ]:
                self.em("sync", [], [resname],
                        lambda sy, dst=dst, srcap=srcap: sy.dma_start(out=dst[:, :], in_=srcap),
                        dma=True)

            with self.scope("xload", [128, FC, T], F32) as xload:
                self.em("sync", [], ["xload"],
                        lambda sy: sy.dma_start(
                            out=xload[:, :, :],
                            in_=p.i_x.rearrange("(fc k) t -> k fc t", k=128)),
                        dma=True)
                self.emit_ln("xload", xload, "n2", n2_sb,
                             (p.l0_ln[:, 0:FC], p.l0_ln[:, FC:2 * FC]), double=True,
                             rows1_out=(spb1[0:1, 0:T], spb1[0:1, T:2 * T]))

            if self.dev == "l0attn":
                apn = self.dbg("n2", [128, FC * T], F32)
                self.em("gpsimd", ["n2"], ["dbgn2"],
                        lambda g: g.dma_start(out=apn,
                                              in_=n2_sb[:, :, :].rearrange("k a b -> k (a b)")),
                        dma=True)
            with (
                self.scope("qkt", [128, NQ, QK], BF16) as qkt,
                self.scope("bqk_row", [1, QK], BF16) as bqk_row,
                self.scope("g1row", [1, QK], F32) as g1row,
                self.scope("b1row", [1, QK], F32) as b1row,
                self.scope("qs_row", [1, QK], F32) as qs_row,
            ):
                for dst, srcap, resname in [
                    (bqk_row[:, :], p.i_l0_bqkr[:, :], "bqk"),
                    (g1row[:, :], p.i_l0_osr[0:1, :], "osr"),
                    (b1row[:, :], p.i_l0_osr[1:2, :], "osrb"),
                ]:
                    self.em("sync", [], [resname],
                            lambda sy, dst=dst, srcap=srcap: sy.dma_start(out=dst, in_=srcap),
                            dma=True)
                with self.scope("wqk_sb", [128, FC, QK // 2], BF16) as wqk_sb:
                    for half in range(2):
                        self.em("gpsimd", [], ["wqk"],
                                lambda g, half=half: g.dma_start(
                                    out=wqk_sb[:, :, :],
                                    in_=p.i_l0_Wqk.rearrange("(fc k) n -> k fc n", k=128)[:, :, half * 1024:(half + 1) * 1024]),
                                dma=True)
                        for nh in range(2):
                            ncc = half * 2 + nh
                            for tt in range(16):
                                bank = (tt * 2 + nh) % 2

                                def mm(t, tt=tt, ncc=ncc, nh=nh, bank=bank, half=half):
                                    for fc in range(FC):
                                        t.matmul(p.ps8[:, bank, :],
                                                 n2_sb[:, fc, bass.ts(tt, 128)],
                                                 wqk_sb[:, fc, bass.ts(nh, 512)],
                                                 start=(fc == 0), stop=False)
                                    return t.matmul(p.ps8[:, bank, :], p.ones1x128b[:, :],
                                                    bqk_row[:, bass.ts(ncc, 512)],
                                                    start=False, stop=True)
                                self.em("tensor", ["n2", "wqk", "bqk", "ones1x128b"],
                                        [("ps", bank)], mm)

                                def ev(s, tt=tt, ncc=ncc, bank=bank):
                                    return s.activation(out=qkt[:, tt, bass.ts(ncc, 512)],
                                                        in_=p.ps8[:, bank, :], func=AF.Silu)
                                self.em("scalar", [("ps", bank)], [("qkt", tt, ncc)], ev)

                qkt_all = [("qkt", tt, ncc) for tt in range(16) for ncc in range(4)]

                for ncc in range(4):
                    def mm(t, ncc=ncc):
                        for tt in range(16):
                            ins = t.matmul(p.ps8[0:1, 2, :], p.ones128x1b[:, :],
                                           qkt[:, tt, bass.ts(ncc, 512)],
                                           start=(tt == 0), stop=(tt == 15))
                        return ins
                    self.em("tensor", qkt_all + ["ones128x1b"], [("ps", 2)], mm)

                    def ev(s, ncc=ncc):
                        return s.copy(qs_row[:, bass.ts(ncc, 512)], p.ps8[0:1, 2, :])
                    self.em("scalar", [("ps", 2)], [("qs", ncc)], ev)

                for dt in range(16):
                    def tp(t, dt=dt):
                        return t.transpose(p.ps8[:, 3, (dt % 2):(dt % 2) + 1],
                                           qs_row[:, bass.ts(dt, 128)], p.identf[0:1, 0:1])
                    self.em("tensor", [("qs", dt // 4), "identf"], [("ps", 3)], tp)

                    def ev(v, dt=dt):
                        return v.tensor_copy(c2_fm[:, dt:dt + 1], p.ps8[:, 3, (dt % 2):(dt % 2) + 1])
                    self.em("vector", [("ps", 3)], [("c2", dt)], ev)

                def c2math(v):
                    v.tensor_tensor(c2_fm[:, :], c2_fm[:, :],
                                    os_sb[:, G0 * NQ:(G0 + 1) * NQ], ALU.mult)
                    v.tensor_scalar(c2_fm[:, :], c2_fm[:, :], 1.0 / T, None, ALU.mult)
                    return v.tensor_scalar(g0N_fm[:, :], os_sb[:, G0 * NQ:(G0 + 1) * NQ],
                                           1.0 / T, None, ALU.mult)
                self.em("vector", [("c2", dt2) for dt2 in range(16)] + ["os"],
                        ["c2r", "g0N"], c2math)

                if self.dev == "l0attn":
                    apq = self.dbg("qkt", [128, NQ * QK], F32)
                    self.em("gpsimd", qkt_all, ["dbgqkt"],
                            lambda g: g.dma_start(out=apq,
                                                  in_=qkt[:, :, :].rearrange("k a b -> k (a b)")),
                            dma=True)
                self.em("vector", [("qs", i) for i in range(4)], ["qsall"],
                        lambda v: v.engine_nop())

                with (
                    self.scope("Bg1", [128, QK], BF16) as Bg1,
                    self.scope("Bb1", [128, QK], BF16) as Bb1,
                    self.scope("Bup", [128, QK], BF16) as Bup,
                    self.scope("simB", [128, QK], F32) as simB,
                    self.scope("mx_sb", [128, 2], F32) as mx_sb,
                    self.scope("sum_sb", [128, 2], F32) as sum_sb,
                ):
                    for i, (Bt, row, rres, wres) in enumerate([
                            (Bg1, g1row[0:1, :], "osr", "Bg1"),
                            (Bb1, b1row[0:1, :], "osrb", "Bb1"),
                            (Bup, qs_row[0:1, :], "qsall", "Bup")]):
                        for ncc in range(4):
                            def mm(t, row=row, ncc=ncc, rres=rres):
                                return t.matmul(p.ps8[:, 2 + ncc % 2, :], p.ones1x128[:, :],
                                                row[:, bass.ts(ncc, 512)], start=True, stop=True)
                            self.em("tensor", [rres, "ones1x128"], [("ps", 2 + ncc % 2)], mm)

                            def ev(v, Bt=Bt, ncc=ncc):
                                return v.tensor_copy(Bt[:, bass.ts(ncc, 512)],
                                                     p.ps8[:, 2 + ncc % 2, :])
                            self.em("vector", [("ps", 2 + ncc % 2)], [wres], ev)

                    def upmath(v):
                        v.tensor_tensor(Bup[:, :], Bup[:, :], Bg1[:, :], ALU.mult)
                        v.tensor_scalar(Bup[:, :], Bup[:, :], 1.0 / T, None, ALU.mult)
                        return v.tensor_tensor(Bup[:, :], Bup[:, :], Bb1[:, :], ALU.add)
                    self.em("vector", ["Bg1", "Bb1", "Bup"], ["Bup"], upmath)

                    for dt in range(16):
                        grp = 0
                        banks = [("ps", i) for i in range(4)]

                        def mm(t, dt=dt, grp=grp):
                            for tt in range(16):
                                for ncc in range(4):
                                    ins = t.matmul(p.ps8[:, ncc, :],
                                                   qkt[:, tt, bass.ts(dt, 128)],
                                                   qkt[:, tt, bass.ts(ncc, 512)],
                                                   start=(tt == 0), stop=(tt == 15))
                            return ins
                        self.em("tensor", qkt_all, banks, mm)

                        def corr(v, dt=dt, grp=grp):
                            v.tensor_scalar(simB[:, :], Bup[:, :],
                                            os_sb[:, B0 * NQ + dt:B0 * NQ + dt + 1],
                                            None, ALU.mult)
                            v.scalar_tensor_tensor(simB[:, :], Bb1[:, :], c2_fm[:, dt:dt + 1],
                                                   simB[:, :], ALU.mult, ALU.add)
                            for ncc in range(4):
                                sl = p.F1[:, ncc % 2, :]
                                v.tensor_tensor(sl, p.ps8[:, ncc, :],
                                                Bg1[:, bass.ts(ncc, 512)], ALU.mult)
                                v.scalar_tensor_tensor(simB[:, bass.ts(ncc, 512)], sl,
                                                       g0N_fm[:, dt:dt + 1],
                                                       simB[:, bass.ts(ncc, 512)],
                                                       ALU.mult, ALU.add)
                            v.drain()
                            v.tensor_reduce(mx_sb[:, grp:grp + 1], simB[:, :], axis=AXX,
                                            op=ALU.max)
                            v.drain()
                            return v.tensor_scalar(mx_sb[:, grp:grp + 1], mx_sb[:, grp:grp + 1],
                                                   -1.0, None, ALU.mult)
                        self.em("vector",
                                banks + ["Bg1", "Bb1", "Bup", "c2r", "g0N", "os", "F1"],
                                ["simB", ("mx", grp), "F1"], corr)

                        def expsum(s, dt=dt, grp=grp):
                            s.activation(out=simB[:, :], in_=simB[:, :], func=AF.Exp,
                                         bias=mx_sb[:, grp:grp + 1],
                                         accum_out=sum_sb[:, grp:grp + 1])
                            return s.drain()
                        self.em("scalar", ["simB", ("mx", grp)],
                                ["simB", ("sum", grp)], expsum)

                        def rdone(v, dt=dt, grp=grp):
                            v.reciprocal(mx_sb[:, grp:grp + 1], sum_sb[:, grp:grp + 1])
                            v.drain()
                            return v.tensor_tensor(r_vec[:, dt:dt + 1], sum_sb[:, grp:grp + 1],
                                                   mx_sb[:, grp:grp + 1], ALU.mult)
                        self.em("vector", [("sum", grp)], [("mx", grp), ("rv", dt)], rdone)

            p.x_sb = self.sbuf("x_sb", [128, FC, T], F32, side="right")
            self.em("sync", [], ["x"],
                    lambda sy: sy.dma_start(
                        out=p.x_sb[:, :, :],
                        in_=p.i_x.rearrange("(fc k) t -> k fc t", k=128)),
                    dma=True)
            self._p1_ln1_residual(spb1[0:1, 0:T], spb1[0:1, T:2 * T])
            with (
                self.scope("wout_ring", [128, 2, NQ, 128], BF16) as wout_ring,
                self.scope("wh_ring", [128, 8, FC, 128], BF16) as wh_ring,
                self.scope("u_sb", [128, NQ, TC], BF16) as u_sb,
                self.scope("vtmp", [128, 2, TC], F32) as vtmp,
            ):
                whr = p.i_l0_Wh.rearrange("(fc k) n -> k fc n", k=128)
                for tc in range(NT):
                    for ht in range(NQ):
                        for half in range(2):
                            slot = (ht * 2 + half) % 8
                            col = ht * 128 if half == 0 else HID + ht * 128
                            self.em("gpsimd", [], [("whr", slot)],
                                    lambda g, slot=slot, col=col: g.dma_start(
                                        out=wh_ring[:, slot, :, :], in_=whr[:, :, col:col + 128]),
                                    dma=True)

                            bank = 2 + (ht * 2 + half) % 2

                            def mm(t, tc=tc, slot=slot, bank=bank):
                                for fc in range(FC):
                                    ins = t.matmul(p.ps8[:, bank, :], wh_ring[:, slot, fc, :],
                                                   n2_sb[:, fc, bass.ts(tc, TC)],
                                                   start=(fc == 0), stop=(fc == FC - 1))
                                return ins
                            self.em("tensor", [("whr", slot), "n2"], [("ps", bank)], mm)

                            if half == 0:
                                def ev(s, ht=ht, bank=bank):
                                    return s.activation(out=vtmp[:, ht % 2, :],
                                                        in_=p.ps8[:, bank, :], func=AF.Silu,
                                                        bias=bh_sb[:, ht:ht + 1])
                                self.em("scalar", [("ps", bank), "bh"], [("vt", ht % 2)], ev)
                            else:
                                def ev(s, ht=ht, bank=bank):
                                    return s.activation(out=p.F2[:, ht % 2, :],
                                                        in_=p.ps8[:, bank, :], func=AF.Silu,
                                                        bias=bh_sb[:, NQ + ht:NQ + ht + 1])
                                self.em("scalar", [("ps", bank), "bh"], [("gt", ht % 2)], ev)

                                def uop(v, tc=tc, ht=ht):
                                    return v.scalar_tensor_tensor(
                                        u_sb[:, ht, :], vtmp[:, ht % 2, :],
                                        r_vec[:, ht:ht + 1], p.F2[:, ht % 2, :],
                                        ALU.mult, ALU.mult)
                                self.em("vector",
                                        [("vt", ht % 2), ("gt", ht % 2), ("rv", ht)],
                                        [("u", ht)], uop)

                    for ot in range(FC):
                        bank = 4 + ot % 2
                        wsl = ot % 2
                        self.em("gpsimd", [], [("woutr", wsl)],
                                lambda g, ot=ot, wsl=wsl: g.dma_start(
                                    out=wout_ring[:, wsl, :, :],
                                    in_=p.i_l0_Wo.rearrange("(ht k) n -> k ht n", k=128)[:, :, ot * 128:(ot + 1) * 128]),
                                dma=True)

                        def mm(t, tc=tc, ot=ot, bank=bank, wsl=wsl):
                            for ht in range(NQ):
                                ins = t.matmul(p.ps8[:, bank, :],
                                               wout_ring[:, wsl, ht, :],
                                               u_sb[:, ht, :],
                                               start=(ht == 0), stop=(ht == NQ - 1))
                            return ins
                        self.em("tensor",
                                [("u", ht2) for ht2 in range(NQ)] + [("woutr", wsl)],
                                [("ps", bank)], mm)

                        def res(v, tc=tc, ot=ot, bank=bank):
                            return v.scalar_tensor_tensor(
                                p.x_sb[:, ot, bass.ts(tc, TC)], p.ps8[:, bank, :],
                                bo_sb[:, ot:ot + 1], p.x_sb[:, ot, bass.ts(tc, TC)],
                                ALU.add, ALU.add)
                        self.em("vector", [("ps", bank), "bo", "x"], ["x"], res)

        if self.dev == "l0attn":
            apr = self.dbg("rv", [128, NQ], F32)
            self.em("sync", [("rv", i) for i in range(16)], ["dbgrv"],
                    lambda sy: sy.dma_start(out=apr, in_=r_vec[:, :]),
                    dma=True)
            ap = self.dbg("x1", [128, FC * T], F32)
            self.em("sync", ["x"], ["dbg"],
                    lambda sy: sy.dma_start(out=ap,
                                            in_=p.x_sb[:, :, :].rearrange("k a b -> k (a b)")),
                    dma=True)

    def p9_store(self):
        p = self
        self.em("sync", ["x"], ["y"],
                lambda sy: sy.dma_start(out=p.o_y.rearrange("(fc k) t -> k fc t", k=128),
                                        in_=p.x_sb[:, :, :]),
                dma=True)

        def final_wait(sy):
            return sy.nop()
        self.em("sync", ["y", "dbg"], [], final_wait)

    def _p1_ln1_residual(self, sp_sv, b1_sv):
        p = self
        for tc in range(NT):
            def lbc(t, tc=tc):
                t.matmul(p.ps8[:, 4, :], p.ones1x128[:, :],
                         sp_sv[0:1, bass.ts(tc, TC)], start=True, stop=True)
                return t.matmul(p.ps8[:, 5, :], p.ones1x128[:, :],
                                b1_sv[0:1, bass.ts(tc, TC)], start=True, stop=True)
            self.em("tensor", ["lnrows1", "ones1x128"], [("ps", 4), ("ps", 5)], lbc)

            def lapp(v, tc=tc):
                for fc in range(FC):
                    v.tensor_tensor(p.x_sb[:, fc, bass.ts(tc, TC)],
                                    p.x_sb[:, fc, bass.ts(tc, TC)],
                                    p.ps8[:, 4, :], ALU.mult)
                    ins = v.tensor_tensor(p.x_sb[:, fc, bass.ts(tc, TC)],
                                          p.x_sb[:, fc, bass.ts(tc, TC)],
                                          p.ps8[:, 5, :], ALU.subtract)
                return ins
            self.em("vector", [("ps", 4), ("ps", 5), "x"], ["x"], lapp)

    def p_moe(self, layer):
        p = self
        nc = self.nc
        CAP = CAP0 if layer == 0 else CAP1
        J = J0 if layer == 0 else J1
        NS = NS0 if layer == 0 else NS1
        NST = NS // 512
        act = AF.Silu if layer == 0 else AF.Relu
        dmp = (self.dev == "l0ffn" and layer == 0) or (self.dev == "l1ffn" and layer == 1)
        if dmp:
            apxin = self.dbg("xin", [128, FC * T], F32)
            self.em("sync", ["x"], ["dbg"],
                    lambda sy: sy.dma_start(
                        out=apxin, in_=p.x_sb[:, :, :].rearrange("k a b -> k (a b)")),
                    dma=True)
        iWr = p.i_l0_Wr if layer == 0 else p.i_l1_Wr
        iW1 = p.i_l0_W1 if layer == 0 else p.i_l1_W1
        iW2 = p.i_l0_W2 if layer == 0 else p.i_l1_W2
        ib1 = p.i_l0_b1 if layer == 0 else p.i_l1_b1
        ib2 = p.i_l0_b2 if layer == 0 else p.i_l1_b2
        d_ret_in = p.d_ret_in[layer]
        d_ret = p.d_ret[layer]
        L = f"m{layer}_"

        with ExitStack() as _st0:
            probs = _st0.enter_context(self.scope(L + "probs", [128, 16, E], F32))
            e1_sb = _st0.enter_context(self.scope(L + "e1", [128, 16], F32))
            e2_sb = _st0.enter_context(self.scope(L + "e2", [128, 16], F32))
            g1_sb = _st0.enter_context(self.scope(L + "g1", [128, 16], F32))
            g2_sb = _st0.enter_context(self.scope(L + "g2", [128, 16], F32))
            sm = _st0.enter_context(self.scope(L + "sm", [128, 16], F32))
            msk = _st0.enter_context(self.scope(L + "msk", [128, 16], U32))
            scr = _st0.enter_context(self.scope(L + "scr", [128, 1024], F32))
            cst = _st0.enter_context(self.scope(L + "cst", [128, 144], F32))
            rA_cm = self.scope(L + "rAs", [1, 2048], F32)
            rA_sv = rA_cm.__enter__()
            rB_cm = self.scope(L + "rBs", [1, 2048], F32)
            rB_sv = rB_cm.__enter__()
            nF_cm = self.scope(L + "nF", [128, FC, T], BF16)
            nF = nF_cm.__enter__()
            self.emit_ln("x", p.x_sb, L + "nF", nF, None, False,
                         rows_out=(rA_sv, rB_sv))

            with (
                self.scope(L + "wr", [128, FC, E], F32) as wr_sb,
                self.scope(L + "lg", [8, T], F32) as lg_sb,
                self.scope(L + "br", [E, 1], F32) as br_sb,
                self.scope(L + "n32", [128, FC, TC], F32) as n32,
            ):
                self.em("sync", [], [L + "wr"],
                        lambda sy: sy.dma_start(out=wr_sb[:, :, :],
                                                in_=iWr.rearrange("(fc k) e -> k fc e", k=128)),
                        dma=True)
                if layer == 0:
                    self.em("sync", [], [L + "br"],
                            lambda sy: sy.dma_start(out=br_sb[:, :], in_=p.i_l0_br[:, :]),
                            dma=True)
                for tc in range(NT):
                    def rbc(t, tc=tc):
                        t.matmul(p.ps8[:, 4, :], p.ones1x128[:, :],
                                 rA_sv[:, bass.ts(tc, TC)], start=True, stop=True)
                        return t.matmul(p.ps8[:, 5, :], p.ones1x128[:, :],
                                        rB_sv[:, bass.ts(tc, TC)], start=True, stop=True)
                    self.em("tensor", ["lnrows", "ones1x128"], [("ps", 4), ("ps", 5)], rbc)

                    def mkn32(v, tc=tc):
                        for fc in range(FC):
                            v.tensor_tensor(n32[:, fc, :], p.x_sb[:, fc, bass.ts(tc, TC)],
                                            p.ps8[:, 4, :], ALU.mult)
                            ins = v.tensor_tensor(n32[:, fc, :], n32[:, fc, :],
                                                  p.ps8[:, 5, :], ALU.subtract)
                        return ins
                    self.em("vector", ["x", ("ps", 4), ("ps", 5)], [L + "n32"], mkn32)

                    def mm(t, tc=tc):
                        for fc in range(FC):
                            ins = t.matmul(p.ps8[0:E, 0, :], wr_sb[:, fc, :],
                                           n32[:, fc, :],
                                           start=(fc == 0), stop=(fc == FC - 1))
                        return ins
                    self.em("tensor", [L + "n32", L + "wr"], [("ps", 0)], mm)

                    def ev(s, tc=tc):
                        if layer == 0:
                            return s.activation(out=lg_sb[:, bass.ts(tc, TC)],
                                                in_=p.ps8[0:E, 0, :], func=AF.Identity,
                                                bias=br_sb[:, :])
                        return s.copy(lg_sb[:, bass.ts(tc, TC)], p.ps8[0:E, 0, :])
                    self.em("scalar", [("ps", 0), L + "br"], [(L + "lg", tc)], ev)

                if dmp:
                    aplg = self.dbg("lg", [8, T], F32)
                    self.em("sync", [(L + "lg", tc2) for tc2 in range(NT)], ["dbg"],
                            lambda sy: sy.dma_start(out=aplg, in_=lg_sb[:, :]),
                            dma=True)
                for tt in range(16):
                    def tp(t, tt=tt):
                        return t.transpose(p.ps8[:, 1, 0:E],
                                           lg_sb[:, bass.ts(tt, 128)], p.identf[0:E, 0:E])
                    self.em("tensor", [(L + "lg", tt // 4), "identf"], [("ps", 1)], tp)

                    def ev(v, tt=tt):
                        return v.tensor_copy(probs[:, tt, :], p.ps8[:, 1, 0:E])
                    self.em("vector", [("ps", 1)], [(L + "pr", tt)], ev)


            self.em("vector", [], [L + "c999"], lambda v: v.memset(cst[:, 0:8], 999.0))
            self.em("vector", [], [L + "cneg"], lambda v: v.memset(cst[:, 8:16], -1.0))
            for tt in range(16):
                def smax(v, tt=tt):
                    pr = probs[:, tt, :]
                    v.drain()
                    v.tensor_reduce(sm[:, 0:1], pr, axis=AXX, op=ALU.max)
                    v.drain()
                    return v.tensor_scalar(sm[:, 0:1], sm[:, 0:1], -1.0, None, ALU.mult)
                self.em("vector", [(L + "pr", tt)], [L + "sm"], smax)

                def sexp(s, tt=tt):
                    s.activation(out=probs[:, tt, :], in_=probs[:, tt, :], func=AF.Exp,
                                 bias=sm[:, 0:1], accum_out=sm[:, 1:2])
                    return s.drain()
                self.em("scalar", [(L + "pr", tt), L + "sm"],
                        [(L + "pr", tt), L + "sm"], sexp)

                def top2(v, tt=tt):
                    pr = probs[:, tt, :]
                    v.reciprocal(sm[:, 2:3], sm[:, 1:2])
                    v.drain()
                    v.tensor_scalar(pr, pr, sm[:, 2:3], None, ALU.mult)
                    v.drain()
                    v.tensor_reduce(g1_sb[:, tt:tt + 1], pr, axis=AXX, op=ALU.max)
                    v.drain()
                    v.tensor_scalar(msk[:, 0:8], pr, g1_sb[:, tt:tt + 1], None, ALU.is_ge)
                    v.drain()
                    v.select(sm[:, 0:8], msk[:, 0:8], p.iota8[:, :], cst[:, 0:8])
                    v.drain()
                    v.tensor_reduce(e1_sb[:, tt:tt + 1], sm[:, 0:8], axis=AXX, op=ALU.min)
                    v.drain()
                    v.tensor_scalar(msk[:, 0:8], p.iota8[:, :], e1_sb[:, tt:tt + 1],
                                    None, ALU.is_equal)
                    v.drain()
                    v.select(scr[:, 0:8], msk[:, 0:8], cst[:, 8:16], pr)
                    v.drain()
                    v.tensor_reduce(g2_sb[:, tt:tt + 1], scr[:, 0:8], axis=AXX, op=ALU.max)
                    v.drain()
                    v.tensor_scalar(msk[:, 8:16], scr[:, 0:8], g2_sb[:, tt:tt + 1],
                                    None, ALU.is_ge)
                    v.drain()
                    v.select(sm[:, 8:16], msk[:, 8:16], p.iota8[:, :], cst[:, 0:8])
                    v.drain()
                    v.tensor_reduce(e2_sb[:, tt:tt + 1], sm[:, 8:16], axis=AXX, op=ALU.min)
                    v.drain()
                    v.tensor_scalar(sm[:, 0:8], p.iota8[:, :], e1_sb[:, tt:tt + 1],
                                    None, ALU.is_equal)
                    v.tensor_scalar(sm[:, 8:16], p.iota8[:, :], e2_sb[:, tt:tt + 1],
                                    None, ALU.is_equal)
                    v.tensor_tensor(sm[:, 0:8], sm[:, 0:8], sm[:, 8:16], ALU.add)
                    return v.tensor_tensor(probs[:, tt, :], pr, sm[:, 0:8], ALU.mult)
                self.em("vector",
                        [(L + "pr", tt), L + "sm", "iota8", L + "c999", L + "cneg",
                         L + "scr"],
                        [(L + "pr", tt), (L + "e", tt), L + "scr", L + "sm"], top2)

            if dmp:
                for nmx, tsr in [("e1", e1_sb), ("e2", e2_sb), ("g1", g1_sb), ("g2", g2_sb)]:
                    apx = self.dbg(nmx, [128, 16], F32)
                    self.em("sync", [(L + "e", tt2) for tt2 in range(16)], ["dbg"],
                            lambda sy, apx=apx, tsr=tsr: sy.dma_start(out=apx, in_=tsr[:, :]),
                            dma=True)
                appr = self.dbg("probs", [128, 16 * E], F32)
                self.em("sync", [(L + "pr", tt2) for tt2 in range(16)], ["dbg"],
                        lambda sy: sy.dma_start(
                            out=appr, in_=probs[:, :, :].rearrange("k a b -> k (a b)")),
                        dma=True)
            with self.scope(L + "gfm", [8, T], F32) as gfm:
                for tt in range(16):
                    def tp(t, tt=tt):
                        return t.transpose(p.ps8[0:E, 1, 0:128], probs[:, tt, :],
                                           p.identf[:, :])
                    self.em("tensor", [(L + "e", tt), (L + "pr", tt), "identf"],
                            [("ps", 1)], tp)

                    def ev(v, tt=tt):
                        return v.tensor_copy(gfm[:, bass.ts(tt, 128)], p.ps8[0:E, 1, 0:128])
                    self.em("vector", [("ps", 1)], [(L + "gfm", tt)], ev)
                self.em("sync", [(L + "gfm", tt) for tt in range(16)], [L + "gfmd"],
                        lambda sy: sy.dma_start(out=p.d_gates_in[:, :], in_=gfm[:, :]),
                        dma=True)
            self.em("gpsimd", [L + "gfmd"], [L + "agg"],
                    lambda g: g.collective_compute(
                        "AllGather", ALU.bypass,
                        replica_groups=[list(range(NCORES))],
                        ins=[p.d_gates_in.ap().opt()], outs=[p.d_gates.ap().opt()]),
                    coll=True)
            with self.scope(L + "ntok", [128, 16, 1024], BF16) as ntok:
                for tt in range(16):
                    for fc in range(FC):
                        bk = (tt * FC + fc) % 2

                        def tp(t, tt=tt, fc=fc, bk=bk):
                            return t.transpose(p.psB[:, bk, 0:128],
                                               nF[:, fc, bass.ts(tt, 128)], p.identb[:, :])
                        self.em("tensor", [L + "nF", "identb"], [("psB", bk)], tp)

                        def ev(v, tt=tt, fc=fc, bk=bk):
                            return v.tensor_copy(ntok[:, tt, bass.ts(fc, 128)],
                                                 p.psB[:, bk, 0:128])
                        self.em("vector", [("psB", bk)], [(L + "ntok", tt)], ev)
                self.em("sync", [(L + "ntok", tt) for tt in range(16)], [L + "ntokd"],
                        lambda sy: sy.dma_start(
                            out=p.d_normtok.rearrange("(tt k) d -> k tt d", k=128),
                            in_=ntok[:, :, :]),
                        dma=True)
            self.em("gpsimd", [L + "ntokd"], [L + "agn"],
                    lambda g: g.collective_compute(
                        "AllGather", ALU.bypass,
                        replica_groups=[list(range(NCORES))],
                        ins=[p.d_normtok.ap().opt()], outs=[p.d_agnorm.ap().opt()]),
                    coll=True)

            nF_cm.__exit__(None, None, None)
            rB_cm.__exit__(None, None, None)
            rA_cm.__exit__(None, None, None)

            dkeep = nc.dram_tensor(L + "dkeep", [128, 1024], F32)
            drank = nc.dram_tensor(L + "drank", [128, 1024], F32)
            with ExitStack() as _str:
                Gfull = _str.enter_context(self.scope(L + "Gfull", [128, 1024], F32))
                keep_sb = _str.enter_context(self.scope(L + "keep", [128, 1024], F32))
                rank_sb = _str.enter_context(self.scope(L + "rank", [128, 1024], F32))
                scr2 = _str.enter_context(self.scope(L + "scr2", [128, 1024], F32))
                self.em("sync", [L + "agg"], [L + "Gf"],
                        lambda sy: sy.dma_start(
                            out=Gfull[:, :],
                            in_=p.d_gates.rearrange("re (h t) -> (re h) t", h=2)),
                        dma=True)
                if layer == 0:
                    self._moe_bisect(L, Gfull, keep_sb, rank_sb, scr, scr2, CAP)
                else:
                    self._moe_fcfs(L, Gfull, keep_sb, rank_sb, scr, CAP)
                self.em("sync", [L + "keepw"], [L + "dkeep"],
                        lambda sy: sy.dma_start(out=dkeep[:, :], in_=keep_sb[:, :]),
                        dma=True)
                self.em("sync", [L + "rankw"], [L + "drank"],
                        lambda sy: sy.dma_start(out=drank[:, :], in_=rank_sb[:, :]),
                        dma=True)

            Ftok_cm = self.scope(L + "Ftok", [128, 32], U32)
            Ftok = Ftok_cm.__enter__()
            with ExitStack() as _st:
                kb = _st.enter_context(self.scope(L + "kb", [128, 128], F32))
                rk = _st.enter_context(self.scope(L + "rk", [128, 128], F32))
                key = _st.enter_context(self.scope(L + "key", [128, 128], F32))
                keyB = _st.enter_context(self.scope(L + "keyB", [128, 128], F32))
                kmk = _st.enter_context(self.scope(L + "kmk", [128, 128], U32))
                Ttok = _st.enter_context(self.scope(L + "Ttok", [128, 128], U32))
                mwiv = _st.enter_context(self.scope(L + "mwiv", [128, 8], F32))
                mwii = _st.enter_context(self.scope(L + "mwii", [128, 8], U32))
                cB = _st.enter_context(self.scope(L + "cB", [128, 1], F32))
                ONx = _st.enter_context(self.scope(L + "ONx", [128, 1], F32))
                ONj = _st.enter_context(self.scope(L + "ONj", [128, 1], F32))
                wB = _st.enter_context(self.scope(L + "wB", [128, 1], F32))
                idx2 = _st.enter_context(self.scope(L + "idx2", [128, 32], U32))

                def own_slice(g, dsrc, dst):
                    cval = nc.values_load(p.cid_sb[0:1, 0:1],
                                          engines=[mybir.EngineType.Pool])
                    src = dsrc.rearrange("(r eh) (blk t2) -> r eh blk t2",
                                         eh=16, t2=128)
                    src2 = src[:, bass.DynSlice(cval * 2, 2), :, :]
                    return g.dma_start(out=dst[:, :], in_=src2)
                self.em("gpsimd", [L + "dkeep", "cid"], [L + "kb"],
                        lambda g: own_slice(g, dkeep, kb), dma=True)
                self.em("gpsimd", [L + "drank", "cid"], [L + "rk"],
                        lambda g: own_slice(g, drank, rk), dma=True)

                def cnts(v):
                    v.drain()
                    v.tensor_reduce(cB[:, :], kb[:, :], axis=AXX, op=ALU.add)
                    return v.drain()
                self.em("vector", [L + "kb"], [L + "cB"], cnts)

                def offmm(t):
                    return t.matmul(p.ps8[:, 0, 0:1], p.tril[:, :], cB[:, :],
                                    start=True, stop=True)
                self.em("tensor", [L + "cB", "tril"], [("ps", 0)], offmm)

                def offev(v):
                    v.tensor_tensor(ONx[:, :], p.ps8[:, 0, 0:1], cB[:, :], ALU.add)
                    return v.tensor_scalar(wB[:, :], cB[:, :], -1.0, 128.0,
                                           ALU.mult, ALU.add)
                self.em("vector", [("ps", 0), L + "cB"], [L + "ONx", L + "wB"], offev)

                def keyinit(v):
                    v.tensor_scalar(keyB[:, :], p.iotaf[:, :], -1.0, 1000.0,
                                    ALU.mult, ALU.add)
                    v.tensor_tensor(key[:, :], kb[:, :], keyB[:, :], ALU.mult)
                    v.tensor_scalar(key[:, :], key[:, :], -1.0, None, ALU.add)
                    return v.drain()
                self.em("vector", [L + "kb", "iotaf"], [L + "key", L + "keyB"], keyinit)

                self.em("vector", [], [L + "cneg2"],
                        lambda v: v.memset(cst[:, 16:144], -2.0))
                for r in range(16):
                    def rnd(v, r=r):
                        v.max(mwiv[:, :], key[:, :])
                        v.max_index(mwii[:, :], mwiv[:, :], key[:, :])
                        v.drain()
                        v.tensor_scalar(Ttok[:, r * 8:(r + 1) * 8], mwii[:, :],
                                        p.bbase[:, :], None, ALU.add)
                        v.tensor_scalar(kmk[:, :], key[:, :], mwiv[:, 7:8], None, ALU.is_ge)
                        v.drain()
                        v.select(key[:, :], kmk[:, :], cst[:, 16:144], key[:, :])
                        return v.drain()
                    self.em("vector",
                            [L + "key", "bbase", L + "cneg2"],
                            [L + "key", L + "keyB", (L + "Tt", r)], rnd)

                self.em("sync", [(L + "Tt", r) for r in range(16)], [L + "Td"],
                        lambda sy: sy.dma_start(
                            out=p.d_T0.rearrange("(b r) one -> b (r one)", b=128),
                            in_=Ttok[:, :]),
                        dma=True)

                for j in range(J):
                    def onj(v, j=j):
                        return v.tensor_scalar(ONj[:, :], ONx[:, :], float(-128 * j),
                                               None, ALU.add)
                    self.em("vector", [L + "ONx"], [L + "ONj"], onj)

                    def cmpj(v, j=j):
                        return v.tensor_scalar(scr[:, 0:128], p.iotaf[:, :], ONj[:, :],
                                               None, ALU.is_ge)
                    self.em("vector", [L + "ONj", "iotaf"], [L + "scr"], cmpj)

                    def gmm(t, j=j):
                        return t.matmul(p.ps8[:, 1, 0:1], scr[:, 0:128], wB[:, :],
                                        start=True, stop=True)
                    self.em("tensor", [L + "scr", L + "wB"], [("ps", 1)], gmm)

                    def gev(v, j=j):
                        return v.tensor_scalar(idx2[:, j:j + 1], p.ps8[:, 1, 0:1],
                                               p.iotaJ[:, j:j + 1], float(NTOK - 1),
                                               ALU.add, ALU.min)
                    self.em("vector", [("ps", 1), "iotaJ"], [(L + "i2", j)], gev)

                for j in range(J):
                    self.em("gpsimd", [(L + "i2", j), L + "Td"], [(L + "Ft", j)],
                            lambda g, j=j: g.indirect_dma_start(
                                out=Ftok[:, j:j + 1], out_offset=None,
                                in_=p.d_T0.ap(),
                                in_offset=bass.IndirectOffsetOnAxis(
                                    ap=idx2[:, j:j + 1], axis=0)),
                            dma=True)

            if dmp:
                apf = self.dbg("Ftok", [128, J], U32)
                self.em("sync", [(L + "Ft", j2) for j2 in range(J)], ["dbg"],
                        lambda sy, apf=apf, J=J: sy.dma_start(out=apf, in_=Ftok[:, 0:J]),
                        dma=True)
            GH = 12
            with self.scope(L + "Xe", [128, FC, NS], BF16) as Xe:
                with self.scope(L + "gath", [128, GH, 1024], BF16) as gath:
                    for j in range(J):
                        gs = j % GH
                        self.em("gpsimd", [(L + "Ft", j), L + "agn"], [(L + "g", gs)],
                                lambda g, j=j, gs=gs: g.indirect_dma_start(
                                    out=gath[:, gs, :], out_offset=None,
                                    in_=p.d_agnorm.ap(),
                                    in_offset=bass.IndirectOffsetOnAxis(
                                        ap=Ftok[:, j:j + 1], axis=0)),
                                dma=True)
                        for fc in range(FC):
                            bk = (j * FC + fc) % 2

                            def tp(t, j=j, gs=gs, fc=fc, bk=bk):
                                return t.transpose(p.psB[:, bk, 0:128],
                                                   gath[:, gs, bass.ts(fc, 128)],
                                                   p.identb[:, :])
                            self.em("tensor", [(L + "g", gs), "identb"],
                                    [("psB", bk)], tp)

                            def ev(v, j=j, fc=fc, bk=bk):
                                return v.tensor_copy(Xe[:, fc, bass.ts(j, 128)],
                                                     p.psB[:, bk, 0:128])
                            self.em("vector", [("psB", bk)], [(L + "Xe", j)], ev)

                if dmp:
                    apxe = self.dbg("Xe", [128, FC * NS], BF16)
                    self.em("gpsimd", [(L + "Xe", j2) for j2 in range(NS // 128)], ["dbg"],
                            lambda g, apxe=apxe: g.dma_start(
                                out=apxe, in_=Xe[:, :, :].rearrange("k a b -> k (a b)")),
                            dma=True)
                self._moe_ffn(L, layer, Xe, iW1, iW2, ib1, ib2, act,
                              NS, NST, d_ret_in)
            Ftok_cm.__exit__(None, None, None)

            self.em("gpsimd", [L + "reti"], [L + "agr"],
                    lambda g: g.collective_compute(
                        "AllGather", ALU.bypass,
                        replica_groups=[list(range(NCORES))],
                        ins=[d_ret_in.ap().opt()], outs=[d_ret.ap().opt()]),
                    coll=True)

            self._moe_owner(L, layer, d_ret, NS, CAP, dkeep, drank,
                            e1_sb, e2_sb, g1_sb, g2_sb, scr, sm, cst)

        if dmp:
            ap = self.dbg("x2", [128, FC * T], F32)
            self.em("sync", ["x"], ["dbg"],
                    lambda sy: sy.dma_start(out=ap,
                                            in_=p.x_sb[:, :, :].rearrange("k a b -> k (a b)")),
                    dma=True)
            for nm, dsrc, shp, rres in [
                ("gates", p.d_gates, [NCORES * E, T], L + "agg"),
                ("keep", dkeep, [128, 1024], L + "dkeep"),
                ("rank", drank, [128, 1024], L + "drank"),
                ("T0", p.d_T0, [NTOK, 1], L + "Td"),
                ("ret", p.d_ret[layer], [NCORES * NS, D], L + "agr"),
                ("agn", p.d_agnorm, [NTOK, D], L + "agn"),
            ]:
                dty = BF16 if nm in ("ret", "agn") else (U32 if nm == "T0" else F32)
                apd = self.dbg(nm, shp, dty)
                self.em("sync", [rres], ["dbg"],
                        lambda sy, apd=apd, dsrc=dsrc: sy.dma_start(
                            out=apd, in_=dsrc.ap()),
                        dma=True)

    def _moe_bisect(self, L, Gfull, keep_sb, rank_sb, scr, scr2, CAP):
        p = self
        nc = self.nc
        with (
            self.scope(L + "bst", [E, 4], F32) as bst,
            self.scope(L + "bmk", [E, 1], U32) as bmk,
            self.scope(L + "cnt", [128, 1], F32) as cnt,
            self.scope(L + "thv", [128, 1], F32) as thv,
            self.scope(L + "rv8", [128, 1], F32) as rv8,
            self.scope(L + "rc1", [128, 1], F32) as rc1,
        ):
            self.em("vector", [], [L + "blo"], lambda v: v.memset(bst[:, 0:1], 0.0))
            self.em("vector", [], [L + "bhi"], lambda v: v.memset(bst[:, 1:2], 1.0))
            for it in range(BISECT_ITERS):
                def mid(v, it=it):
                    v.tensor_tensor(bst[:, 2:3], bst[:, 0:1], bst[:, 1:2], ALU.add)
                    return v.tensor_scalar(bst[:, 2:3], bst[:, 2:3], 0.5, None, ALU.mult)
                self.em("vector", [L + "blo", L + "bhi"], [L + "bmid"], mid)

                def bc(t, it=it):
                    return t.matmul(p.ps8[:, 2, 0:1], p.selET[:, :], bst[:, 2:3],
                                    start=True, stop=True)
                self.em("tensor", [L + "bmid", "selET"], [("ps", 2)], bc)

                def cmp(v, it=it):
                    v.tensor_scalar(scr[:, :], Gfull[:, :], p.ps8[:, 2, 0:1],
                                    None, ALU.is_gt)
                    v.drain()
                    v.tensor_reduce(cnt[:, :], scr[:, :], axis=AXX, op=ALU.add)
                    return v.drain()
                self.em("vector", [L + "Gf", ("ps", 2), L + "scr"],
                        [L + "scr", L + "cntw"], cmp)

                def cmm(t, it=it):
                    return t.matmul(p.ps8[0:E, 3, 0:1], p.selE[:, :], cnt[:, :],
                                    start=True, stop=True)
                self.em("tensor", [L + "cntw", "selE"], [("ps", 3)], cmm)

                def upd(v, it=it):
                    v.tensor_scalar(bmk[:, :], p.ps8[0:E, 3, 0:1], float(CAP),
                                    None, ALU.is_ge)
                    v.drain()
                    v.select(bst[:, 0:1], bmk[:, :], bst[:, 2:3], bst[:, 0:1])
                    v.select(bst[:, 1:2], bmk[:, :], bst[:, 1:2], bst[:, 2:3])
                    return v.drain()
                self.em("vector", [("ps", 3), L + "bmid"], [L + "blo", L + "bhi"], upd)

            def fbc(t):
                return t.matmul(p.ps8[:, 2, 0:1], p.selET[:, :], bst[:, 1:2],
                                start=True, stop=True)
            self.em("tensor", [L + "bhi", "selET"], [("ps", 2)], fbc)
            self.em("vector", [("ps", 2)], [L + "thv"],
                    lambda v: v.tensor_copy(thv[:, :], p.ps8[:, 2, 0:1]))

            def cmp2(v):
                v.tensor_scalar(scr[:, :], Gfull[:, :], thv[:, :], None, ALU.is_gt)
                v.drain()
                v.tensor_reduce(cnt[:, :], scr[:, :], axis=AXX, op=ALU.add)
                return v.drain()
            self.em("vector", [L + "Gf", L + "thv", L + "scr"],
                    [L + "scr", L + "cntw"], cmp2)

            def cmm2(t):
                return t.matmul(p.ps8[0:E, 3, 0:1], p.selE[:, :], cnt[:, :],
                                start=True, stop=True)
            self.em("tensor", [L + "cntw", "selE"], [("ps", 3)], cmm2)

            def re8(v):
                return v.tensor_scalar(bst[:, 3:4], p.ps8[0:E, 3, 0:1], -1.0,
                                       float(CAP), ALU.mult, ALU.add)
            self.em("vector", [("ps", 3)], [L + "re8"], re8)

            def rbc(t):
                return t.matmul(p.ps8[:, 2, 0:1], p.selET[:, :], bst[:, 3:4],
                                start=True, stop=True)
            self.em("tensor", [L + "re8", "selET"], [("ps", 2)], rbc)
            self.em("vector", [("ps", 2)], [L + "rv8"],
                    lambda v: v.tensor_copy(rv8[:, :], p.ps8[:, 2, 0:1]))

            def eq(v):
                return v.tensor_scalar(scr2[:, :], Gfull[:, :], thv[:, :],
                                       None, ALU.is_equal)
            self.em("vector", [L + "Gf", L + "thv"], [L + "scr2"], eq)

            def eqscan(v):
                v.drain()
                v.tensor_tensor_scan(keep_sb[:, :], scr2[:, :], scr2[:, :], 0.0,
                                     ALU.add, ALU.bypass)
                v.drain()
                return v.tensor_copy(rc1[:, :], keep_sb[:, 1023:1024])
            self.em("vector", [L + "scr2"], [L + "keepw", L + "rc1"], eqscan)

            def eqoff(t):
                return t.matmul(p.ps8[:, 2, 0:1], p.triE[:, :], rc1[:, :],
                                start=True, stop=True)
            self.em("tensor", [L + "rc1", "triE"], [("ps", 2)], eqoff)

            def keepfin(v):
                v.tensor_tensor(keep_sb[:, :], keep_sb[:, :], scr2[:, :], ALU.subtract)
                v.tensor_scalar(keep_sb[:, :], keep_sb[:, :], p.ps8[:, 2, 0:1],
                                None, ALU.add)
                v.tensor_scalar(keep_sb[:, :], keep_sb[:, :], rv8[:, :], None, ALU.is_lt)
                v.tensor_tensor(keep_sb[:, :], keep_sb[:, :], scr2[:, :], ALU.mult)
                return v.tensor_tensor(keep_sb[:, :], keep_sb[:, :], scr[:, :], ALU.add)
            self.em("vector", [L + "keepw", L + "scr2", L + "scr", L + "rv8", ("ps", 2)],
                    [L + "keepw"], keepfin)

            self._rank_from_keep(L, keep_sb, rank_sb, scr2, rc1)

    def _moe_fcfs(self, L, Gfull, keep_sb, rank_sb, scr, CAP):
        p = self
        with (
            self.scope(L + "rc1", [128, 1], F32) as rc1,
        ):
            def assigned(v):
                return v.tensor_scalar(scr[:, :], Gfull[:, :], 0.0, None, ALU.is_gt)
            self.em("vector", [L + "Gf"], [L + "scr"], assigned)

            def pscan(v):
                v.drain()
                v.tensor_tensor_scan(keep_sb[:, :], scr[:, :], scr[:, :], 0.0,
                                     ALU.add, ALU.bypass)
                v.drain()
                return v.tensor_copy(rc1[:, :], keep_sb[:, 1023:1024])
            self.em("vector", [L + "scr"], [L + "keepw", L + "rc1"], pscan)

            def poff(t):
                return t.matmul(p.ps8[:, 2, 0:1], p.triE[:, :], rc1[:, :],
                                start=True, stop=True)
            self.em("tensor", [L + "rc1", "triE"], [("ps", 2)], poff)

            def keepfin(v):
                v.tensor_tensor(keep_sb[:, :], keep_sb[:, :], scr[:, :], ALU.subtract)
                v.tensor_scalar(keep_sb[:, :], keep_sb[:, :], p.ps8[:, 2, 0:1],
                                None, ALU.add)
                v.tensor_scalar(keep_sb[:, :], keep_sb[:, :], float(CAP), None, ALU.is_lt)
                return v.tensor_tensor(keep_sb[:, :], keep_sb[:, :], scr[:, :], ALU.mult)
            self.em("vector", [L + "keepw", L + "scr", ("ps", 2)], [L + "keepw"], keepfin)

            self._rank_from_keep(L, keep_sb, rank_sb, scr, rc1)

    def _rank_from_keep(self, L, keep_sb, rank_sb, tmp, rc1):
        p = self

        def rscan(v):
            v.drain()
            v.tensor_tensor_scan(rank_sb[:, :], keep_sb[:, :], keep_sb[:, :], 0.0,
                                 ALU.add, ALU.bypass)
            v.drain()
            return v.tensor_copy(rc1[:, :], rank_sb[:, 1023:1024])
        self.em("vector", [L + "keepw"], [L + "rankw", L + "rc1k"], rscan)

        def roff(t):
            return t.matmul(p.ps8[:, 2, 0:1], p.triE[:, :], rc1[:, :],
                            start=True, stop=True)
        self.em("tensor", [L + "rc1k", "triE"], [("ps", 2)], roff)

        def rfin(v):
            v.tensor_tensor(rank_sb[:, :], rank_sb[:, :], keep_sb[:, :], ALU.subtract)
            return v.tensor_scalar(rank_sb[:, :], rank_sb[:, :], p.ps8[:, 2, 0:1],
                                   None, ALU.add)
        self.em("vector", [L + "rankw", L + "keepw", ("ps", 2)], [L + "rankw"], rfin)

    def _moe_ffn(self, L, layer, Xe, iW1, iW2, ib1, ib2, act, NS, NST, d_ret_in):
        p = self
        nc = self.nc
        with (
            self.scope(L + "b1", [128, 32], F32) as b1_sb,
            self.scope(L + "b2", [128, 8], F32) as b2_sb,
            self.scope(L + "w1r", [128, 2, FC, 128], BF16) as w1r,
            self.scope(L + "w2r", [128, 2, 32, 128], BF16) as w2r,
            self.scope(L + "h", [128, 32, 512], BF16) as h_sb,
            self.scope(L + "of", [128, 1, 512], BF16) as of_fm,
            self.scope(L + "ot", [128, 4, 1024], BF16) as ot,
        ):
            self.em("sync", [], [L + "b1"],
                    lambda sy: sy.dma_start(out=b1_sb[:, :], in_=ib1[:, :]), dma=True)
            self.em("sync", [], [L + "b2"],
                    lambda sy: sy.dma_start(out=b2_sb[:, :], in_=ib2[:, :]), dma=True)
            w1v = iW1.rearrange("(fc k) hh -> k fc hh", k=128)
            w2v = iW2.rearrange("(hh k) d2 -> k hh d2", k=128)
            for st in range(NST):
                for ht in range(32):
                    slot = ht % 2
                    self.em("gpsimd", [], [(L + "w1r", slot)],
                            lambda g, ht=ht, slot=slot: g.dma_start(
                                out=w1r[:, slot, :, :],
                                in_=w1v[:, :, ht * 128:(ht + 1) * 128]),
                            dma=True)
                    bank = ht % 2

                    def mm(t, st=st, ht=ht, slot=slot, bank=bank):
                        for fc in range(FC):
                            ins = t.matmul(p.ps8[:, bank, :], w1r[:, slot, fc, :],
                                           Xe[:, fc, bass.ts(st, 512)],
                                           start=(fc == 0), stop=(fc == FC - 1))
                        return ins
                    self.em("tensor",
                            [(L + "w1r", slot)] + [(L + "Xe", j) for j in range(NS // 128)],
                            [("ps", bank)], mm)

                    def ev(s, ht=ht, bank=bank):
                        return s.activation(out=h_sb[:, ht, :], in_=p.ps8[:, bank, :],
                                            func=act, bias=b1_sb[:, ht:ht + 1])
                    self.em("scalar", [("ps", bank), L + "b1"], [(L + "h", ht)], ev)

                for dt in range(FC):
                    slot = dt % 2
                    self.em("gpsimd", [], [(L + "w2r", slot)],
                            lambda g, dt=dt, slot=slot: g.dma_start(
                                out=w2r[:, slot, :, :],
                                in_=w2v[:, :, dt * 128:(dt + 1) * 128]),
                            dma=True)
                    bank = 2 + dt % 2

                    def mm2(t, st=st, dt=dt, slot=slot, bank=bank):
                        for ht in range(32):
                            ins = t.matmul(p.ps8[:, bank, :], w2r[:, slot, ht, :],
                                           h_sb[:, ht, :],
                                           start=(ht == 0), stop=(ht == 31))
                        return ins
                    self.em("tensor",
                            [(L + "w2r", slot)] + [(L + "h", ht2) for ht2 in range(32)],
                            [("ps", bank)], mm2)

                    def ev2(s, dt=dt, bank=bank):
                        return s.activation(out=of_fm[:, 0, :], in_=p.ps8[:, bank, :],
                                            func=AF.Identity, bias=b2_sb[:, dt:dt + 1])
                    self.em("scalar", [("ps", bank), L + "b2"], [L + "of"], ev2)

                    for sub in range(4):
                        bk = (dt * 4 + sub) % 2

                        def tp(t, dt=dt, sub=sub, bk=bk):
                            return t.transpose(p.psB[:, bk, 0:128],
                                               of_fm[:, 0, bass.ts(sub, 128)],
                                               p.identb[:, :])
                        self.em("tensor", [L + "of", "identb"], [("psB", bk)], tp)

                        def ev3(v, dt=dt, sub=sub, bk=bk):
                            return v.tensor_copy(ot[:, sub, bass.ts(dt, 128)],
                                                 p.psB[:, bk, 0:128])
                        self.em("vector", [("psB", bk)], [(L + "ot", sub)], ev3)

                for sub in range(4):
                    self.em("sync", [(L + "ot", sub)], [L + "reti"],
                            lambda sy, st=st, sub=sub: sy.dma_start(
                                out=d_ret_in[st * 512 + sub * 128:
                                             st * 512 + (sub + 1) * 128, :],
                                in_=ot[:, sub, :]),
                            dma=True)

    def _moe_owner(self, L, layer, d_ret, NS, CAP, dkeep, drank,
                   e1_sb, e2_sb, g1_sb, g2_sb, scr, sm, cst):
        p = self
        nc = self.nc

        with (
            self.scope(L + "koA", [8, 1024], F32) as koA,
            self.scope(L + "koB", [8, 1024], F32) as koB,
            self.scope(L + "roA", [8, 1024], F32) as roA,
            self.scope(L + "roB", [8, 1024], F32) as roB,
            self.scope(L + "ktok", [128, 16, E], F32) as ktok,
            self.scope(L + "rtok", [128, 16, E], F32) as rtok,
            self.scope(L + "oidx", [128, 32], U32) as oidx,
            self.scope(L + "gkv", [128, 32], F32) as gkv,
        ):
            def own16(g, dsrc, dst, h2):
                cval = nc.values_load(p.cid_sb[0:1, 0:1], engines=[mybir.EngineType.Pool])
                src = dsrc.rearrange("(r e h) t -> r h e t", e=8, h=2)
                return g.dma_start(
                    out=dst[:, :],
                    in_=src[bass.DynSlice(cval, 1), h2, :, :])
            for h2, (kd, rd) in enumerate([(koA, roA), (koB, roB)]):
                self.em("gpsimd", [L + "dkeep", "cid"], [(L + "ko", h2)],
                        lambda g, h2=h2, kd=kd: own16(g, dkeep, kd, h2), dma=True)
                self.em("gpsimd", [L + "drank", "cid"], [(L + "ro", h2)],
                        lambda g, h2=h2, rd=rd: own16(g, drank, rd, h2), dma=True)

            for srcs, dst_sb, rkeyb, wkey in [((koA, koB), ktok, L + "ko", L + "ktok"),
                                              ((roA, roB), rtok, L + "ro", L + "rtok")]:
                for h2 in range(2):
                    for tsub in range(8):
                        tt = h2 * 8 + tsub
                        src_sb = srcs[h2]

                        def tp(t, src_sb=src_sb, tsub=tsub):
                            return t.transpose(p.ps8[:, 1, 0:E],
                                               src_sb[:, bass.ts(tsub, 128)],
                                               p.identf[0:E, 0:E])
                        self.em("tensor", [(rkeyb, h2), "identf"], [("ps", 1)], tp)

                        def ev(v, dst_sb=dst_sb, tt=tt):
                            return v.tensor_copy(dst_sb[:, tt, :], p.ps8[:, 1, 0:E])
                        self.em("vector", [("ps", 1)], [(wkey, tt)], ev)

            for k in range(2):
                ek = e1_sb if k == 0 else e2_sb
                gk = g1_sb if k == 0 else g2_sb
                for tt in range(16):
                    col = k * 16 + tt

                    def mkidx(v, k=k, tt=tt, col=col, ek=ek, gk=gk):
                        v.tensor_scalar(sm[:, 0:8], p.iota8[:, :], ek[:, tt:tt + 1],
                                        None, ALU.is_equal)
                        v.tensor_tensor(sm[:, 0:8], sm[:, 0:8], rtok[:, tt, :],
                                        ALU.mult)
                        v.drain()
                        v.tensor_reduce(gkv[:, col:col + 1], sm[:, 0:8], axis=AXX,
                                        op=ALU.add)
                        v.drain()
                        v.tensor_scalar(oidx[:, col:col + 1], ek[:, tt:tt + 1],
                                        float(NS), None, ALU.mult)
                        v.tensor_tensor(oidx[:, col:col + 1], oidx[:, col:col + 1],
                                        gkv[:, col:col + 1], ALU.add)
                        v.tensor_scalar(oidx[:, col:col + 1], oidx[:, col:col + 1],
                                        float(NCORES * NS - 1), None, ALU.min)
                        v.tensor_scalar(sm[:, 0:8], p.iota8[:, :], ek[:, tt:tt + 1],
                                        None, ALU.is_equal)
                        v.tensor_tensor(sm[:, 0:8], sm[:, 0:8], ktok[:, tt, :], ALU.mult)
                        v.drain()
                        v.tensor_reduce(gkv[:, col:col + 1], sm[:, 0:8], axis=AXX,
                                        op=ALU.add)
                        v.drain()
                        if layer == 0:
                            ins = v.tensor_tensor(gkv[:, col:col + 1], gkv[:, col:col + 1],
                                                  gk[:, tt:tt + 1], ALU.mult)
                        else:
                            v.tensor_tensor(sm[:, 8:9], g1_sb[:, tt:tt + 1],
                                            g2_sb[:, tt:tt + 1], ALU.add)
                            v.drain()
                            v.reciprocal(sm[:, 9:10], sm[:, 8:9])
                            v.drain()
                            v.tensor_tensor(gkv[:, col:col + 1], gkv[:, col:col + 1],
                                            gk[:, tt:tt + 1], ALU.mult)
                            ins = v.tensor_tensor(gkv[:, col:col + 1], gkv[:, col:col + 1],
                                                  sm[:, 9:10], ALU.mult)
                        return ins
                    self.em("vector",
                            [(L + "rtok", tt), (L + "ktok", tt), (L + "e", tt),
                             "iota8", L + "sm"],
                            [(L + "oi", col), L + "sm"], mkidx)

            if (self.dev == "l1ffn" and layer == 1) or (self.dev == "l0ffn" and layer == 0):
                apg = self.dbg("gkv", [128, 32], F32)
                apo = self.dbg("oidx", [128, 32], U32)
                self.em("sync", [(L + "oi", c2) for c2 in range(32)], ["dbg"],
                        lambda sy: sy.dma_start(out=apg, in_=gkv[:, :]), dma=True)
                self.em("sync", [(L + "oi", c2) for c2 in range(32)], ["dbg"],
                        lambda sy: sy.dma_start(out=apo, in_=oidx[:, :]), dma=True)
            with self.scope(L + "ct", [128, 16, 1024], BF16) as ct:
                for k in range(2):
                    for tt in range(16):
                        col = k * 16 + tt
                        self.em("gpsimd", [(L + "oi", col), L + "agr"], [(L + "ct", tt)],
                                lambda g, col=col, tt=tt: g.indirect_dma_start(
                                    out=ct[:, tt, :], out_offset=None,
                                    in_=d_ret.ap(),
                                    in_offset=bass.IndirectOffsetOnAxis(
                                        ap=oidx[:, col:col + 1], axis=0)),
                                dma=True)

                        def scale(v, col=col, tt=tt):
                            return v.tensor_scalar(ct[:, tt, :], ct[:, tt, :],
                                                   gkv[:, col:col + 1], None, ALU.mult)
                        self.em("vector", [(L + "ct", tt), (L + "oi", col)],
                                [(L + "ct", tt)], scale)

                        for fc in range(FC):
                            bk = fc % 2

                            def tp(t, tt=tt, fc=fc, bk=bk):
                                return t.transpose(p.psB[:, bk, 0:128],
                                                   ct[:, tt, bass.ts(fc, 128)],
                                                   p.identb[:, :])
                            self.em("tensor", [(L + "ct", tt), "identb"],
                                    [("psB", bk)], tp)

                            def add(v, tt=tt, fc=fc, bk=bk):
                                return v.tensor_tensor(
                                    p.x_sb[:, fc, bass.ts(tt, 128)],
                                    p.x_sb[:, fc, bass.ts(tt, 128)],
                                    p.psB[:, bk, 0:128], ALU.add)
                            self.em("vector", [("psB", bk), "x"], ["x"], add)

    def p3_l1_attn(self):
        p = self
        nc = self.nc
        T2 = 1024
        PI = 3.141592653589793
        TWO_PI = 6.283185307179586
        INV2PI = 1.0 / TWO_PI
        MAGIC = 12582912.0
        C1 = 6.28125
        C2 = 0.001935073079168797
        C3 = 3.3173541006524535e-09
        L = "a1_"

        with ExitStack() as st0:
            xc = st0.enter_context(self.scope(L + "xc", [128, FC, T], BF16))
            convw = st0.enter_context(self.scope(L + "cw", [128, FC * 4], F32))
            convb = st0.enter_context(self.scope(L + "cb", [128, FC], F32))
            for dst, srcap, rn in [
                (convw[:, :], p.i_l1_convw[:, :], L + "cw"),
                (convb[:, :], p.i_l1_convb[:, :], L + "cb"),
            ]:
                self.em("sync", [], [rn],
                        lambda sy, dst=dst, srcap=srcap: sy.dma_start(out=dst, in_=srcap),
                        dma=True)

            with self.scope(L + "n3p", [128, FC, T + 3], BF16) as n3p, \
                    self.scope(L + "Fw", [128, T], F32) as Fw:
                self.em("vector", [], [L + "zpad"],
                        lambda v: v.memset(n3p[:, :, 0:3], 0.0))
                self.emit_ln("x", p.x_sb, L + "n3", n3p[:, :, 3:3 + T], None, False)
                for fc in range(FC):
                    def conv(v, fc=fc):
                        v.tensor_scalar(Fw[:, :], n3p[:, fc, 0:T],
                                        convw[:, 0 * FC + fc:0 * FC + fc + 1], None,
                                        ALU.mult)
                        for k in range(1, 4):
                            v.scalar_tensor_tensor(Fw[:, :], n3p[:, fc, k:k + T],
                                                   convw[:, k * FC + fc:k * FC + fc + 1],
                                                   Fw[:, :], ALU.mult, ALU.add)
                        return v.tensor_scalar(xc[:, fc, :], Fw[:, :],
                                               convb[:, fc:fc + 1], None, ALU.add)
                    self.em("vector",
                            [L + "n3", L + "zpad", L + "cw", L + "cb", L + "Fw"],
                            [(L + "xc", fc), L + "Fw"], conv)
            decay = st0.enter_context(self.scope(L + "dec", [16, T], BF16))
            decr = st0.enter_context(self.scope(L + "der", [16, T], BF16))
            lng = st0.enter_context(self.scope(L + "lng", [16, 1], F32))
            gamv = st0.enter_context(self.scope(L + "gam", [16, 1], F32))
            f1 = st0.enter_context(self.scope(L + "f1", [128, 16], F32))
            s128 = st0.enter_context(self.scope(L + "s128", [128, 16], F32))
            cosA = st0.enter_context(self.scope(L + "cosA", [128, 256], F32))
            sinA = st0.enter_context(self.scope(L + "sinA", [128, 256], F32))
            aacc = st0.enter_context(self.scope(L + "aacc", [128, 16], F32))
            kcw = st0.enter_context(self.scope(L + "kcw", [128, 16], F32))
            gbv = st0.enter_context(self.scope(L + "gb", [128, 16], F32))
            lnp = st0.enter_context(self.scope(L + "lnp", [128, 2 * FC], F32))
            freq = st0.enter_context(self.scope(L + "fq", [128, 16], F32))
            carry = st0.enter_context(self.scope(L + "cry", [128, 16], F32))
            kbnd = st0.enter_context(self.scope(L + "kbd", [128, 16], F32))
            wtmp = st0.enter_context(self.scope(L + "wtmp", [128, FC, 128], BF16))
            Fk2 = st0.enter_context(self.scope(L + "Fk2", [128, 16], F32))
            honehb = st0.enter_context(self.scope(L + "hohb", [16, 2048], BF16))
            for dst, srcap, rn in [
                (honehb[:, :], p.i_honehb[:, :], L + "hohb"),
                (gbv[:, :], p.i_l1_gb[:, :], L + "gb"),
                (lnp[:, :], p.i_l1_lnpost[:, :], L + "lnp"),
                (freq[:, :], p.i_l1_freqs[:, :], L + "fq"),
                (gamv[:, :], p.i_l1_gamma[:, :], L + "gam"),
            ]:
                self.em("sync", [], [rn],
                        lambda sy, dst=dst, srcap=srcap: sy.dma_start(out=dst, in_=srcap),
                        dma=True)

            d_decb = nc.dram_tensor(L + "d_decb", [16 * 128, T], BF16)
            d_derb = nc.dram_tensor(L + "d_derb", [16 * 128, T], BF16)
            with self.scope(L + "trow", [1, T], F32) as trow, \
                    self.scope(L + "dsc", [16, T // 4], F32) as dscr:
                self.em("sync", [], [L + "trow"],
                        lambda sy: sy.dma_start(out=trow[:, :], in_=p.i_trow[:, :]),
                        dma=True)

                def lngf(s):
                    return s.activation(out=lng[:, :], in_=gamv[:, :], func=AF.Ln)
                self.em("scalar", [L + "gam"], [L + "lng"], lngf)
                for tc in range(NT):
                    def tb(t, tc=tc):
                        return t.matmul(p.ps8[0:16, 2, :], p.ones1x128[0:1, 0:16],
                                        trow[:, bass.ts(tc, TC)], start=True, stop=True)
                    self.em("tensor", [L + "trow", "ones1x128"], [("ps", 2)], tb)

                    def dc(s, tc=tc):
                        return s.activation(out=decay[:, bass.ts(tc, TC)],
                                            in_=p.ps8[0:16, 2, :], func=AF.Exp,
                                            scale=lng[:, :])
                    self.em("scalar", [("ps", 2), L + "lng"], [(L + "dct", tc)], dc)

                    def dr(v, tc=tc):
                        v.tensor_scalar(dscr[:, :], decay[:, bass.ts(tc, TC)],
                                        1e-6, None, ALU.add)
                        v.drain()
                        v.reciprocal(dscr[:, :], dscr[:, :])
                        v.drain()
                        return v.tensor_copy(decr[:, bass.ts(tc, TC)], dscr[:, :])
                    self.em("vector", [(L + "dct", tc), L + "dsc"],
                            [(L + "der", tc), L + "dsc"], dr)

                with self.scope(L + "dtmp", [128, T], BF16) as dtmp:
                    for h in range(16):
                        for src_row, dnm, dd in ((decay, "dct", d_decb),
                                                 (decr, "der", d_derb)):
                            for tc in range(NT):
                                bk = tc % 2

                                def bmm(t, h=h, src_row=src_row, tc=tc, bk=bk):
                                    return t.matmul(
                                        p.ps8[:, 2 + bk, :],
                                        honehb[0:16, h * 128:(h + 1) * 128],
                                        src_row[0:16, bass.ts(tc, TC)],
                                        start=True, stop=True)
                                self.em("tensor",
                                        [(L + dnm, tc2) for tc2 in range(NT)]
                                        + [L + "hohb"],
                                        [("ps", 2 + bk)], bmm)

                                def bev(v, tc=tc, bk=bk):
                                    return v.tensor_copy(dtmp[:, bass.ts(tc, TC)],
                                                         p.ps8[:, 2 + bk, :])
                                self.em("vector", [("ps", 2 + bk)],
                                        [(L + "dtmp", tc)], bev)
                            self.em("sync", [(L + "dtmp", tc2) for tc2 in range(NT)],
                                    [L + "dtmp", (L + "ddrm", h, dnm)],
                                    lambda sy, h=h, dd=dd: sy.dma_start(
                                        out=dd.ap()[h * 128:(h + 1) * 128, :],
                                        in_=dtmp[:, :]),
                                    dma=True)
            qk = st0.enter_context(self.scope(L + "qk", [128, 16, T2], BF16))

            def _cody(v, out, in_, n):
                v.scalar_tensor_tensor(out, n, -C1, in_, ALU.mult, ALU.add)
                v.scalar_tensor_tensor(out, n, -C2, out, ALU.mult, ALU.add)
                return v.scalar_tensor_tensor(out, n, -C3, out, ALU.mult, ALU.add)

            def _wrap(v, out, in_, addv, tmp):
                v.tensor_scalar(out, in_, addv, None, ALU.add)
                v.tensor_scalar(tmp, out, INV2PI, MAGIC, ALU.mult, ALU.add)
                v.tensor_scalar(tmp, tmp, MAGIC, None, ALU.subtract)
                return v.scalar_tensor_tensor(out, tmp, -TWO_PI, out, ALU.mult, ALU.add)

            def fred(v):
                v.tensor_scalar(kcw[:, :], freq[:, :], INV2PI, MAGIC, ALU.mult, ALU.add)
                v.tensor_scalar(kcw[:, :], kcw[:, :], MAGIC, None, ALU.subtract)
                _cody(v, f1[:, :], freq[:, :], kcw[:, :])
                v.tensor_scalar(s128[:, :], f1[:, :], 128.0, None, ALU.mult)
                v.tensor_scalar(kcw[:, :], s128[:, :], INV2PI, MAGIC, ALU.mult, ALU.add)
                v.tensor_scalar(kcw[:, :], kcw[:, :], MAGIC, None, ALU.subtract)
                _cody(v, s128[:, :], s128[:, :], kcw[:, :])
                return v.memset(aacc[:, :], 0.0)
            self.em("vector", [L + "fq"], [L + "f1", L + "s128", L + "aacc"], fred)

            for a in range(16):
                def awrap(v, a=a):
                    if a > 0:
                        v.tensor_tensor(aacc[:, :], aacc[:, :], s128[:, :], ALU.add)
                        _wrap(v, aacc[:, :], aacc[:, :], 0.0, kcw[:, :])
                    _wrap(v, kbnd[:, :], aacc[:, :], PI / 2, kcw[:, :])
                    return v.drain()
                self.em("vector", [L + "s128", L + "aacc", L + "kcww"],
                        [L + "aacc", L + "kbd", L + "kcww"], awrap)

                def asin(s, a=a):
                    s.activation(out=sinA[:, a * 16:(a + 1) * 16], in_=aacc[:, :],
                                 func=AF.Sin, scale=-1.0)
                    return s.activation(out=cosA[:, a * 16:(a + 1) * 16], in_=kbnd[:, :],
                                        func=AF.Sin)
                self.em("scalar", [L + "aacc", L + "kbd"],
                        [(L + "cA", a), (L + "sA", a)], asin)

            def wtrans(isrc, h, wi, wdst, wkey):
                self.em("gpsimd", [], [L + "wtmp"],
                        lambda g: g.dma_start(
                            out=wtmp[:, :, :],
                            in_=isrc.rearrange("(hh k) c -> k hh c", k=128)[:, h, :]
                                .rearrange("k (fc c2) -> k fc c2", c2=128)),
                        dma=True)
                for fc in range(FC):
                    bk = fc % 2

                    def tpw(t, fc=fc, bk=bk):
                        return t.transpose(p.psB[:, bk, 0:128], wtmp[:, fc, :],
                                           p.identb[:, :])
                    self.em("tensor", [L + "wtmp", "identb"], [("psB", bk)], tpw)

                    def evw(s, fc=fc, bk=bk, wi=wi, wdst=wdst):
                        return s.copy(wdst[:, wi, fc, :], p.psB[:, bk, 0:128])
                    self.em("scalar", [("psB", bk)], [(wkey, fc)], evw)

            for half in range(2):
                HB = half * T2
                sth = ExitStack()
                sth.__enter__()
                hs = f"{half}_"
                wT = sth.enter_context(self.scope(L + hs + "wT", [128, 3, FC, 128], BF16))
                kcw2 = sth.enter_context(self.scope(L + hs + "kcw2", [128, 128], F32))
                Fv = sth.enter_context(self.scope(L + hs + "Fv", [128, T2], F32))
                Fg = sth.enter_context(self.scope(L + hs + "Fg", [128, T2], F32))
                Fc = sth.enter_context(self.scope(L + hs + "Fc", [128, T2], F32))
                Fm = sth.enter_context(self.scope(L + hs + "Fm", [128, T2], F32))
                Fk = sth.enter_context(self.scope(L + hs + "Fk", [128, T2], F32))
                FmB = sth.enter_context(self.scope(L + hs + "FmB", [128, T2], BF16))
                Fb = sth.enter_context(self.scope(L + hs + "Fb", [128, 2, 128], F32))
                Fb2 = sth.enter_context(self.scope(L + hs + "Fb2", [128, 2, 128], F32))
                rr = sth.enter_context(self.scope(L + hs + "rr", [1, T2], F32))
                decR = sth.enter_context(self.scope(L + hs + "decR", [128, 2, T2], BF16))
                derR = sth.enter_context(self.scope(L + hs + "derR", [128, 2, T2], BF16))

                for h in range(16):
                    wtrans(p.i_l1_qkw, h, 0, wT, (L + "wTq"))
                    for tc in range(2):
                        bank = tc % 2

                        def mmq(t, h=h, tc=tc, bank=bank, HB=HB):
                            for fc in range(FC):
                                ins = t.matmul(p.ps8[:, bank, :], wT[:, 0, fc, :],
                                               xc[:, fc, HB + tc * 512:HB + (tc + 1) * 512],
                                               start=(fc == 0), stop=(fc == FC - 1))
                            return ins
                        self.em("tensor",
                                [((L + "wTq"), fc2) for fc2 in range(FC)]
                                + [(L + "xc", fc2) for fc2 in range(FC)],
                                [("ps", bank)], mmq)

                        def evq(s, h=h, tc=tc, bank=bank):
                            return s.activation(out=qk[:, h, bass.ts(tc, 512)],
                                                in_=p.ps8[:, bank, :], func=AF.Identity)
                        self.em("scalar", [("ps", bank)],
                                [(L + "qk", h), (L + "op", h)], evq)

                    def sqh(s, h=h):
                        return s.activation(out=FmB[:, :], in_=qk[:, h, :], func=AF.Square)
                    self.em("scalar", [(L + "qk", h), L + "FmB"], [L + "FmB"], sqh)

                    def ssm(t, h=h):
                        t.matmul(p.ps8[0:1, 2, :], p.ones128x1b[:, :], FmB[:, 0:512],
                                 start=True, stop=True)
                        return t.matmul(p.ps8[0:1, 3, :], p.ones128x1b[:, :],
                                        FmB[:, 512:1024], start=True, stop=True)
                    self.em("tensor", [L + "FmB", "ones128x1b"], [("ps", 2), ("ps", 3)], ssm)

                    def rrev(s, h=h):
                        s.copy(rr[0:1, 0:512], p.ps8[0:1, 2, :])
                        s.copy(rr[0:1, 512:1024], p.ps8[0:1, 3, :])
                        return s.sqrt(rr[0:1, :], rr[0:1, :])
                    self.em("scalar", [("ps", 2), ("ps", 3), L + "rr"], [L + "rr"], rrev)

                    def rmath(v, h=h):
                        v.tensor_scalar(rr[0:1, :], rr[0:1, :], 1e-12, None, ALU.max)
                        v.drain()
                        v.reciprocal(rr[0:1, :], rr[0:1, :])
                        return v.drain()
                    self.em("vector", [L + "rr"], [L + "rr"], rmath)

                    def bcq(t, h=h):
                        t.matmul(p.ps8[:, 2, :], p.ones1x128[:, :], rr[0:1, 0:512],
                                 start=True, stop=True)
                        return t.matmul(p.ps8[:, 3, :], p.ones1x128[:, :],
                                        rr[0:1, 512:1024], start=True, stop=True)
                    self.em("tensor", [L + "rr", "ones1x128"], [("ps", 2), ("ps", 3)], bcq)

                    def qmul(v, h=h):
                        v.tensor_tensor(qk[:, h, 0:512], qk[:, h, 0:512], p.ps8[:, 2, :],
                                        ALU.mult)
                        return v.tensor_tensor(qk[:, h, 512:1024], qk[:, h, 512:1024],
                                               p.ps8[:, 3, :], ALU.mult)
                    self.em("vector", [("ps", 2), ("ps", 3), (L + "qk", h)],
                            [(L + "qk", h)], qmul)

                for h in range(16):
                    wtrans(p.i_l1_vw, h, 1, wT, (L + "wTv"))
                    wtrans(p.i_l1_gw, h, 2, wT, (L + "wTg"))
                    for tc in range(2):
                        def mmv(t, h=h, tc=tc, HB=HB):
                            for fc in range(FC):
                                ins = t.matmul(p.ps8[:, tc, :], wT[:, 1, fc, :],
                                               xc[:, fc, HB + tc * 512:HB + (tc + 1) * 512],
                                               start=(fc == 0), stop=(fc == FC - 1))
                            return ins
                        self.em("tensor",
                                [((L + "wTv"), fc2) for fc2 in range(FC)]
                                + [(L + "xc", fc2) for fc2 in range(FC)],
                                [("ps", tc)], mmv)

                        def mmg(t, h=h, tc=tc, HB=HB):
                            for fc in range(FC):
                                ins = t.matmul(p.ps8[:, 2 + tc, :], wT[:, 2, fc, :],
                                               xc[:, fc, HB + tc * 512:HB + (tc + 1) * 512],
                                               start=(fc == 0), stop=(fc == FC - 1))
                            return ins
                        self.em("tensor",
                                [((L + "wTg"), fc2) for fc2 in range(FC)]
                                + [(L + "xc", fc2) for fc2 in range(FC)],
                                [("ps", 2 + tc)], mmg)

                    def evv(s, h=h):
                        s.copy(Fv[:, 0:512], p.ps8[:, 0, :])
                        return s.copy(Fv[:, 512:1024], p.ps8[:, 1, :])
                    self.em("scalar", [("ps", 0), ("ps", 1)], [L + "Fv"], evv)

                    def evg(s, h=h):
                        s.activation(out=Fg[:, 0:512], in_=p.ps8[:, 2, :],
                                     func=AF.Sigmoid, bias=gbv[:, h:h + 1])
                        return s.activation(out=Fg[:, 512:1024], in_=p.ps8[:, 3, :],
                                            func=AF.Sigmoid, bias=gbv[:, h:h + 1])
                    self.em("scalar", [("ps", 2), ("ps", 3), L + "gb"], [L + "Fg"], evg)

                    def cosb(v, h=h):
                        v.tensor_scalar(Fb[:, 0, :], p.iotaf[:, :], f1[:, h:h + 1],
                                        None, ALU.mult)
                        v.tensor_scalar(kcw2[:, :], Fb[:, 0, :], INV2PI, MAGIC,
                                        ALU.mult, ALU.add)
                        v.tensor_scalar(kcw2[:, :], kcw2[:, :], MAGIC, None, ALU.subtract)
                        v.scalar_tensor_tensor(Fb[:, 0, :], kcw2[:, :], -C1,
                                               Fb[:, 0, :], ALU.mult, ALU.add)
                        v.scalar_tensor_tensor(Fb[:, 0, :], kcw2[:, :], -C2,
                                               Fb[:, 0, :], ALU.mult, ALU.add)
                        v.scalar_tensor_tensor(Fb[:, 0, :], kcw2[:, :], -C3,
                                               Fb[:, 0, :], ALU.mult, ALU.add)
                        v.tensor_scalar(Fb[:, 1, :], Fb[:, 0, :], PI / 2, None, ALU.add)
                        v.tensor_scalar(kcw2[:, :], Fb[:, 1, :], INV2PI, MAGIC,
                                        ALU.mult, ALU.add)
                        v.tensor_scalar(kcw2[:, :], kcw2[:, :], MAGIC, None, ALU.subtract)
                        v.scalar_tensor_tensor(Fb[:, 1, :], kcw2[:, :], -TWO_PI,
                                               Fb[:, 1, :], ALU.mult, ALU.add)
                        return v.drain()
                    self.em("vector", ["iotaf", L + "f1", L + "kcw2", L + "Fb"],
                            [L + "Fb", L + "kcw2"], cosb)

                    def cosb2(s, h=h):
                        s.activation(out=Fb2[:, 0, :], in_=Fb[:, 0, :], func=AF.Sin)
                        return s.activation(out=Fb2[:, 1, :], in_=Fb[:, 1, :], func=AF.Sin)
                    self.em("scalar", [L + "Fb"], [L + "Fb2"], cosb2)

                    def costile(v, h=h, half=half):
                        v.nodrain_on()
                        for a in range(8):
                            aa = half * 8 + a
                            v.tensor_scalar(Fc[:, bass.ts(a, 128)], Fb2[:, 1, :],
                                            cosA[:, aa * 16 + h:aa * 16 + h + 1],
                                            None, ALU.mult)
                        v.drain()
                        for a in range(8):
                            aa = half * 8 + a
                            sl = Fc[:, bass.ts(a, 128)]
                            v.scalar_tensor_tensor(sl, Fb2[:, 0, :],
                                                   sinA[:, aa * 16 + h:aa * 16 + h + 1],
                                                   sl, ALU.mult, ALU.add)
                        v.nodrain_off()
                        return v.drain()
                    self.em("vector",
                            [L + "Fb2"]
                            + [(L + "cA", half * 8 + a2) for a2 in range(8)]
                            + [(L + "sA", half * 8 + a2) for a2 in range(8)],
                            [L + "Fc"], costile)

                    def ksh(v, h=h, half=half):
                        if half == 0:
                            v.memset(Fk[:, 0:1], 0.0)
                        else:
                            v.tensor_copy(Fk[:, 0:1], carry[:, h:h + 1])
                        ins = v.tensor_copy(carry[:, h:h + 1], qk[:, h, T2 - 1:T2])
                        return ins
                    self.em("vector", [(L + "qk", h), (L + "cryr", h, half)],
                            [L + "Fk", (L + "cryr", h, half + 1)], ksh)

                    slot = h % 2
                    self.em("sync", [(L + "ddrm", h, "dct"), (L + "decR", slot)],
                            [(L + "decL", h)],
                            lambda sy, h=h, slot=slot, HB=HB: sy.dma_start(
                                out=decR[:, slot, :],
                                in_=d_decb.ap()[h * 128:(h + 1) * 128, HB:HB + T2]),
                            dma=True)
                    self.em("sync", [(L + "ddrm", h, "der"), (L + "derR", slot)],
                            [(L + "derL", h)],
                            lambda sy, h=h, slot=slot, HB=HB: sy.dma_start(
                                out=derR[:, slot, :],
                                in_=d_derb.ap()[h * 128:(h + 1) * 128, HB:HB + T2]),
                            dma=True)

                    def state(v, h=h, half=half, slot=slot):
                        v.tensor_tensor(Fc[:, 0:1], Fc[:, 0:1], Fk[:, 0:1], ALU.add)
                        v.tensor_tensor(Fc[:, 1:T2], Fc[:, 1:T2], qk[:, h, 0:T2 - 1],
                                        ALU.add)
                        v.tensor_tensor(Fc[:, :], Fc[:, :], Fv[:, :], ALU.mult)
                        v.tensor_tensor(Fc[:, :], Fc[:, :], Fg[:, :], ALU.mult)
                        v.tensor_tensor(Fc[:, :], Fc[:, :], decR[:, slot, :], ALU.mult)
                        v.drain()
                        if half == 0:
                            v.memset(Fk2[:, h:h + 1], 0.0)
                            v.drain()
                        v.tensor_tensor_scan(Fm[:, :], Fc[:, :], Fc[:, :],
                                             Fk2[:, h:h + 1], ALU.add, ALU.bypass)
                        v.drain()
                        v.tensor_copy(Fk2[:, h:h + 1], Fm[:, T2 - 1:T2])
                        v.tensor_tensor(Fm[:, :], Fm[:, :], derR[:, slot, :], ALU.mult)
                        return v.tensor_tensor(qk[:, h, :], Fm[:, :], qk[:, h, :],
                                               ALU.mult)
                    self.em("vector",
                            [L + "Fc", L + "Fk", L + "Fv", L + "Fg", L + "Fm",
                             (L + "decL", h), (L + "derL", h),
                             (L + "qk", h), (L + "sc2", h, half)],
                            [L + "Fc", L + "Fm", L + "Fk2", (L + "qk", h),
                             (L + "op", h), (L + "sc2", h, half + 1),
                             (L + "decR", slot), (L + "derR", slot)],
                            state)

                sth.__exit__(None, None, None)
                self._l1_ow(L, half, HB, qk, wtmp, lnp)

        if self.dev == "l1attn":
            ap = self.dbg("x3", [128, FC * T], F32)
            self.em("sync", ["x"], ["dbg"],
                    lambda sy: sy.dma_start(out=ap,
                                            in_=p.x_sb[:, :, :].rearrange("k a b -> k (a b)")),
                    dma=True)

    def _l1_ow(self, L, half, HB, op_sb, wtmp, lnp):
        p = self
        nc = self.nc
        with self.scope(L + f"o32_{half}", [128, FC, 1024], F32) as o32:
            stow = ExitStack()
            stow.__enter__()
            owTg = stow.enter_context(self.scope(L + f"owTg{half}", [128, 4, FC, 128], BF16))
            for hg in range(4):
                for hh in range(4):
                    h = hg * 4 + hh
                    self.em("gpsimd", [], [L + "wtmp"],
                            lambda g, h=h: g.dma_start(
                                out=wtmp[:, :, :],
                                in_=p.i_l1_ow.rearrange("(fc k) e2 -> k fc e2", k=128)
                                    [:, :, h * 128:(h + 1) * 128]),
                            dma=True)
                    for fc in range(FC):
                        bk = fc % 2

                        def tpo(t, fc=fc, bk=bk):
                            return t.transpose(p.psB[:, bk, 0:128], wtmp[:, fc, :],
                                               p.identb[:, :])
                        self.em("tensor", [L + "wtmp", "identb"], [("psB", bk)], tpo)

                        def evo(s, hh=hh, fc=fc, bk=bk):
                            return s.copy(owTg[:, hh, fc, :], p.psB[:, bk, 0:128])
                        self.em("scalar", [("psB", bk)], [(L + "owTg", hh, fc)], evo)

                for fc in range(FC):
                    for tc in range(2):
                        bank = (fc * 2 + tc) % 2

                        def mmo(t, hg=hg, fc=fc, tc=tc, bank=bank):
                            for hh in range(4):
                                h = hg * 4 + hh
                                ins = t.matmul(p.ps8[:, bank, :], owTg[:, hh, fc, :],
                                               op_sb[:, h, bass.ts(tc, 512)],
                                               start=(hh == 0), stop=(hh == 3))
                            return ins
                        self.em("tensor",
                                [(L + "owTg", hh2, fc) for hh2 in range(4)]
                                + [(L + "op", hg * 4 + hh2) for hh2 in range(4)],
                                [("ps", bank)], mmo)

                        def evo2(v, hg=hg, fc=fc, tc=tc, bank=bank):
                            if hg == 0:
                                return v.tensor_copy(o32[:, fc, bass.ts(tc, 512)],
                                                     p.ps8[:, bank, :])
                            return v.tensor_tensor(o32[:, fc, bass.ts(tc, 512)],
                                                   o32[:, fc, bass.ts(tc, 512)],
                                                   p.ps8[:, bank, :], ALU.add)
                        self.em("vector", [("ps", bank), (L + "o32f", fc, tc)],
                                [(L + "o32f", fc, tc)], evo2)

            stow.__exit__(None, None, None)
            self.em("vector", [(L + "o32f", fc2, tc2) for fc2 in range(FC)
                               for tc2 in range(2)], [L + "o32r"],
                    lambda v: v.engine_nop())
            self.emit_ln(L + "o32r", o32, "x", p.x_sb[:, :, HB:HB + 1024],
                         (lnp[:, 0:FC], lnp[:, FC:2 * FC]), False, W=1024,
                         accum=True)

def _host_inputs(inputs, core):
    ii = {}
    ii["x"] = np.ascontiguousarray(inputs["L0_attn_x"][core].T.astype(np.float32))
    ii["l0_ln"] = np.concatenate([_featmaj(inputs["L0_attn_ln_g"]),
                                  _featmaj(inputs["L0_attn_ln_b"])], axis=1)
    ii["l0_Wh"] = _f32(inputs["L0_attn_W_hidden"])
    ii["l0_bh"] = _featmaj(inputs["L0_attn_b_hidden"])
    ii["l0_Wqk"] = _f32(inputs["L0_attn_W_qk"])
    osg = inputs["L0_attn_os_gamma"]
    osb = inputs["L0_attn_os_beta"]
    ii["l0_os"] = np.concatenate([_featmaj(osg[0]), _featmaj(osg[1]),
                                  _featmaj(osb[0]), _featmaj(osb[1])], axis=1)
    ii["l0_osr"] = np.stack([_f32(osg[1]), _f32(osb[1])], axis=0)
    ii["l0_bqkr"] = _bfh(inputs["L0_attn_b_qk"]).reshape(1, -1)
    ii["l0_Wo"] = _f32(inputs["L0_attn_W_out"])
    ii["l0_bo"] = _featmaj(inputs["L0_attn_b_out"])
    ii["l0_Wr"] = _f32(inputs["L0_ffn_Wr"])
    ii["l0_br"] = _f32(inputs["L0_ffn_br"]).reshape(E, 1)
    ii["l0_W1"] = _f32(inputs["L0_ffn_W1"][core])
    ii["l0_b1"] = _featmaj(inputs["L0_ffn_b1"][core])
    ii["l0_W2"] = _f32(inputs["L0_ffn_W2"][core])
    ii["l0_b2"] = _featmaj(inputs["L0_ffn_b2"][core])
    ii["l1_convw"] = np.concatenate(
        [_featmaj(inputs["L1_attn_conv_w"][:, 0, k]) for k in range(4)], axis=1)
    ii["l1_convb"] = _featmaj(inputs["L1_attn_conv_b"])
    ii["l1_qkw"] = _f32(inputs["L1_attn_qk_w"])
    ii["l1_vw"] = _f32(inputs["L1_attn_v_w"])
    ii["l1_gw"] = _f32(inputs["L1_attn_g_w"])
    ii["l1_gb"] = _featmaj(inputs["L1_attn_g_b"])
    ii["l1_ow"] = _f32(inputs["L1_attn_o_w"])
    ii["l1_ln"] = np.concatenate([_featmaj(np.ones(D, np.float32)),
                                  _featmaj(np.zeros(D, np.float32))], axis=1)
    ii["l1_lnpost"] = np.concatenate([_featmaj(inputs["L1_attn_ln_g"]),
                                      _featmaj(inputs["L1_attn_ln_b"])], axis=1)
    ii["l1_freqs"] = np.ascontiguousarray(_f32(inputs["L1_attn_freqs"]).T)
    ii["l1_gamma"] = _f32(inputs["L1_attn_gamma"]).reshape(16, 1)
    ii["trow"] = _f32(np.arange(T).reshape(1, T))
    ii["honehb"] = _bfh(np.repeat(np.eye(16), 128, axis=1))
    ii["l1_Wr"] = _f32(inputs["L1_ffn_w_router"])
    ii["l1_W1"] = _f32(inputs["L1_ffn_w1"][core])
    ii["l1_b1"] = _featmaj(inputs["L1_ffn_b1"][core])
    ii["l1_W2"] = _f32(inputs["L1_ffn_w2"][core])
    ii["l1_b2"] = _featmaj(inputs["L1_ffn_b2"][core])
    ii["identb"] = _bfh(np.eye(128))
    ii["identf"] = _f32(np.eye(128))
    ii["inv1024"] = _bfh(np.full((128, 1), 1.0 / 1024))
    ii["ones1x128"] = _f32(np.ones((1, 128)))
    ii["ones128x1b"] = _bfh(np.ones((128, 1)))
    ii["ones1x128b"] = _bfh(np.ones((1, 128)))
    pe = (np.arange(128) // 2) % 8
    ii["selE"] = _f32(pe[:, None] == np.arange(E)[None, :])
    ii["selET"] = _f32(pe[None, :] == np.arange(E)[:, None])
    rowrank = (np.arange(128) // 16) * 2 + (np.arange(128) % 2)
    ii["triE"] = _f32((pe[:, None] == pe[None, :]) &
                      (rowrank[:, None] < rowrank[None, :]))
    ii["tril"] = _f32(np.arange(128)[:, None] < np.arange(128)[None, :])
    ii["iota8"] = _f32(np.tile(np.arange(E)[None, :], (128, 1)))
    ii["iotaJ"] = _f32(np.arange(128)[:, None] + 128 * np.arange(32)[None, :])
    ii["iotat"] = _f32(np.arange(128)[:, None] + 128 * np.arange(16)[None, :])
    ii["iotaf"] = _f32(np.tile(np.arange(128)[None, :], (128, 1)))
    ii["bbase"] = _f32(128 * np.arange(128)[:, None])
    ii["cid"] = np.array([[core, core]], dtype=np.uint32)
    return ii


_LN_DEFAULT_LN = None


def kernel(**inputs):
    dev_stage = inputs.pop("_dev_stage", None)
    prog = Prog(dev_stage)
    nc = prog.build()
    in_maps = [_host_inputs(inputs, c) for c in range(NCORES)]
    res = run_bass_kernel_spmd(nc, in_maps, core_ids=list(range(NCORES)))
    if dev_stage:
        return res
    out = np.stack([r["y"].T for r in res.results], axis=0)
    return np.ascontiguousarray(out.astype(np.float32))
